# version 14
# baseline (speedup 1.0000x reference)
import numpy as np
from contextlib import ExitStack

import concourse.bass as bass
import concourse.mybir as mybir
from concourse import dve_ops
from concourse.bass_utils import run_bass_kernel_spmd

F32 = mybir.dt.float32
BF16 = mybir.dt.bfloat16
AF = mybir.ActivationFunctionType
ALU = mybir.AluOpType
AX = mybir.AxisListType

NCORES = 8
D = 1024
CH = 512
NH = 16
HD = 64
KT = D // 128


_DT_SIZE = {"float32": 4, "int32": 4, "uint32": 4, "bfloat16": 2,
            "float16": 2, "int16": 2, "uint16": 2, "int8": 1, "uint8": 1,
            "float8_e4m3": 1, "float8_e5m2": 1, "float64": 8}


def _pap_span(pap):
    try:
        name = pap.memref
        esz = _DT_SIZE.get(str(pap.dtype).split(".")[-1], 8)
        off = int(pap.offset)
        pairs = [(int(p[0]), int(p[1])) for p in pap.ap]
        if not pairs:
            return (name, 0, 1 << 24, 0, 1 << 50, 0, 0, 0)
        ps, pn = pairs[0]
        rest = pairs[1:]
        neg = sum(min(0, s * (n - 1)) for s, n in rest)
        pos = sum(max(0, s * (n - 1)) for s, n in rest)
        if ps > 0:
            f0 = off % ps
            if f0 + neg >= 0 and f0 + pos < ps:
                p0 = off // ps
                st = re_ = w = 0
                if len(rest) == 1 and rest[0][0] > 1:
                    st = rest[0][0] * esz
                    re_ = (f0 * esz) % st
                    w = esz
                return (name, p0, p0 + pn - 1,
                        (f0 + neg) * esz, (f0 + pos + 1) * esz, st, re_, w)
        return (name, 0, 1 << 24, 0, 1 << 50, 0, 0, 0)
    except Exception:
        return None


def _op_spans(ins):
    try:
        bir = ins.ins
        opname = type(bir).__name__.lower()
        if "accum" in opname:
            return None
        rd, wr = [], []
        for a in bir.ins:
            if getattr(a, "memref", None) is None:
                continue
            s = _pap_span(a)
            if s is None:
                return None
            rd.append(s)
        for a in bir.outs:
            if getattr(a, "memref", None) is None:
                continue
            s = _pap_span(a)
            if s is None:
                return None
            wr.append(s)
        return (rd, wr, False)
    except Exception:
        return None


def _spans_overlap(spans, pending):
    for q in pending:
        if q is None:
            return bool(spans)
        for s in spans:
            if not (s[0] == q[0] and s[1] <= q[2] and q[1] <= s[2]
                    and s[3] < q[4] and q[3] < s[4]):
                continue
            if s[5] and s[5] == q[5]:
                st = s[5]
                d = (s[6] - q[6]) % st
                if d >= q[7] and st - d >= s[7]:
                    continue
            return True
    return False


class Prog:

    ENGINES = ("sync", "vector", "scalar", "tensor", "gpsimd")

    def __init__(self, nc, chain="ve"):
        self.nc = nc
        self.chain = chain
        self.ops = {e: [] for e in self.ENGINES}
        self.count = {}
        self.record = None
        self.plan = None
        self.plan_stats = None

    def sem(self, name):
        self.count.setdefault(name, 0)
        return name

    def op(self, engine, fn, wait=(), inc=None, inc_amt=1, is_dma=False,
           drain_inc=False, nodrain=False, bar=False):
        if inc is not None:
            self.sem(inc)
            self.count[inc] += inc_amt
        self.ops[engine].append(
            (fn, [(s, v) for (s, v) in wait if v > 0], inc, inc_amt,
             is_dma, drain_inc, nodrain,
             self.count[inc] if inc is not None else None, bar))
        return self.count[inc] if inc is not None else None

    def recip(self, out, in_, scratch, wait=(), inc=None):
        return self.op("vector", lambda e: e.reciprocal(out=out, in_=in_),
                       wait=wait, inc=inc)

    def barrier(self):
        snap = [(s, v) for s, v in self.count.items() if v > 0]
        for e in self.ENGINES:
            self.op(e, lambda en: en.nop(), wait=snap, bar=True)

    def dma(self, engine, out, in_, wait=(), inc=None):
        return self.op(engine, lambda e: e.dma_start(out=out, in_=in_),
                       wait=wait, inc=inc, inc_amt=16, is_dma=True)

    def build(self, es):
        drain_engines = ()
        if self.chain == "all":
            run = {n: 0 for n in self.count}
            for ename in self.ENGINES:
                self.sem("SELF_" + ename)
                run.setdefault("SELF_" + ename, 0)
                prev = None
                out = []
                for fn, waits, inc, amt, is_dma, di, nd, cum, bar in self.ops[ename]:
                    if inc is None:
                        inc = "SELF_" + ename
                        self.count[inc] = self.count.get(inc, 0) + 1
                        amt = 1
                    run[inc] = run.get(inc, 0) + amt
                    if prev is not None:
                        waits = waits + [prev]
                    out.append((fn, waits, inc, amt, is_dma, di, nd, cum, bar))
                    prev = (inc, run[inc])
                self.ops[ename] = out
        elif self.chain == "ve":
            drain_engines = ("vector", "scalar")
        analyze = self.chain == "analyze"
        sel = self.chain == "sel"
        if analyze:
            self.record = {e: [] for e in self.ENGINES}
        sems = {n: es.enter_context(self.nc.semaphore(n)) for n in self.count}
        block = es.enter_context(self.nc.Block())
        for ename in self.ENGINES:
            stream = self.ops[ename]
            use_drain = ename in drain_engines
            drain_always = use_drain and ename in ("vector", "scalar")

            chained = bool(self.chain) and self.chain != "all"

            def make(fstream, use_drain, drain_always, ename=ename):
                sel_ve = sel and ename in ("vector", "scalar")
                if sel_ve:
                    drain_before, drain_inc_set = self.plan[ename]
                rec = self.record[ename] if analyze else None

                def body(eng):
                    for i, (fn, waits, inc, amt, is_dma, di, nd, cum, bar) \
                            in enumerate(fstream):
                        if sel_ve and i in drain_before:
                            eng.drain(fusable=True)
                        for s, v in waits:
                            eng.wait_ge(sems[s], v)
                        ins = fn(eng)
                        if rec is not None:
                            rec.append(_op_spans(ins))
                        if is_dma:
                            if inc is not None:
                                ins.then_inc(sems[inc], amt)
                        elif sel_ve:
                            if inc is not None:
                                if i in drain_inc_set:
                                    eng.drain(fusable=True).then_inc(
                                        sems[inc], amt)
                                else:
                                    ins.then_inc(sems[inc], amt)
                        elif analyze:
                            if inc is not None:
                                ins.then_inc(sems[inc], amt)
                        elif nd and inc is None:
                            pass
                        elif use_drain and (drain_always or inc is not None):
                            d = eng.drain(fusable=True)
                            if inc is not None:
                                d.then_inc(sems[inc], amt)
                        elif chained and di and inc is not None:
                            eng.drain(fusable=False).then_inc(sems[inc], amt)
                        elif inc is not None:
                            ins.then_inc(sems[inc], amt)
                return body

            getattr(block, ename)(make(stream, use_drain, drain_always))

    def compute_plan(self):
        waited = {}
        inc_engines = {}
        for ename in self.ENGINES:
            for (fn, waits, inc, amt, is_dma, di, nd, cum, bar) in self.ops[ename]:
                for s, v in waits:
                    waited.setdefault(s, set()).add(v)
                if inc is not None:
                    inc_engines.setdefault(inc, set()).add(ename)
        mixed = {s for s, es_ in inc_engines.items() if len(es_) > 1}
        plan = {}
        stats = {}
        for ename in ("vector", "scalar"):
            rec = self.record[ename]
            ops = self.ops[ename]
            drain_before, drain_inc_set = set(), set()
            pending = []
            for i, (op_t, info) in enumerate(zip(ops, rec)):
                fn, waits, inc, amt, is_dma, di, nd, cum, bar = op_t
                if is_dma:
                    continue
                if bar:
                    if pending:
                        drain_before.add(i)
                        pending = []
                    continue
                if info is None:
                    if pending and not nd:
                        drain_before.add(i)
                    pending = [None]
                else:
                    rd, wr, uni = info
                    if uni:
                        if pending and not nd:
                            drain_before.add(i)
                        pending = [None]
                    else:
                        if not nd and _spans_overlap(rd + wr, pending):
                            drain_before.add(i)
                            pending = []
                        pending.extend(wr)
                if inc is not None:
                    drain_inc_set.add(i)
                    pending = []
            plan[ename] = (drain_before, drain_inc_set)
            stats[ename] = (len(ops), len(drain_before), len(drain_inc_set))
        self.plan_stats = stats
        return plan


def rearr_tok(dram_ap, t0, ntok):
    return dram_ap[t0:t0 + ntok, :].rearrange("(t p) d -> p t d", p=128)


def build_phase_a(n_pre=7, n_own=9, chain="ve", stages=6, ctx=None):
    standalone = ctx is None
    if standalone:
        nc = bass.Bass("TRN2", target_bir_lowering=False, debug=False)
        P = None
    else:
        nc, P = ctx
    T = (n_pre + n_own) * CH
    TO = n_own * CH

    xw = nc.dram_tensor("xw", [T, D], F32, kind="ExternalInput").ap()
    wqkvT = nc.dram_tensor("wqkvT", [D, 3 * D], F32, kind="ExternalInput").ap()
    bq_col = nc.dram_tensor("bq_col", [128, 24], F32, kind="ExternalInput").ap()
    bv_row = nc.dram_tensor("bv_row", [1, D], F32, kind="ExternalInput").ap()
    omega_bd = nc.dram_tensor("omega_bd", [8, 128, 128], F32, kind="ExternalInput").ap()
    quadv = nc.dram_tensor("quadv", [2, 1], F32, kind="ExternalInput").ap()
    quadw = nc.dram_tensor("quadw", [2, 1], F32, kind="ExternalInput").ap()
    selq = nc.dram_tensor("selq", [2, 128], F32, kind="ExternalInput").ap()
    sel2 = nc.dram_tensor("sel2", [2, 128], F32, kind="ExternalInput").ap()
    bd2 = nc.dram_tensor("bd2", [128, 2], F32, kind="ExternalInput").ap()
    flag = nc.dram_tensor("flag", [128, 1], F32, kind="ExternalInput").ap()
    flag_bf = nc.dram_tensor("flag_bf", [128, 1], BF16, kind="ExternalInput").ap()
    tri = nc.dram_tensor("tri", [128, 4, CH], F32, kind="ExternalInput").ap()
    ident_b = nc.dram_tensor("ident_b", [128, 128], BF16, kind="ExternalInput").ap()
    woutT = nc.dram_tensor("woutT", [D, D], F32, kind="ExternalInput").ap()
    bo_row = nc.dram_tensor("bo_row", [1, D], F32, kind="ExternalInput").ap()

    kind = "ExternalOutput" if standalone else "Internal"
    xn_sc = nc.dram_tensor("xn_sc", [TO, D], F32, kind=kind).ap()
    ao_sc = nc.dram_tensor("ao_sc", [TO, D], F32, kind=kind).ap()

    if standalone:
        P = Prog(nc, chain=chain)
    es = ExitStack()
    with es:
        sb = lambda n, s, d=F32: es.enter_context(nc.sbuf_tensor('pa_' + n, s, d))
        ps = lambda n, s, d=F32: es.enter_context(nc.psum_tensor('pa_' + n, s, d))

        s_wqkv = sb("s_wqkv", [128, KT, 3 * D], BF16)
        s_bq = sb("s_bq", [128, 24])
        s_bv = sb("s_bv", [1, D], BF16)
        s_om = sb("s_om", [128, 8, 128], BF16)
        s_qv = sb("s_qv", [2, 1])
        s_qw = sb("s_qw", [2, 1])
        s_selq = sb("s_selq", [2, 128])
        s_sel2 = sb("s_sel2", [2, 128])
        s_bd2 = sb("s_bd2", [128, 2])
        s_flag = sb("s_flag", [128, 1])
        s_flagb = sb("s_flagb", [128, 1], BF16)
        s_tri = sb("s_tri", [128, 4, CH])
        s_idb = sb("s_idb", [128, 128], BF16)
        s_wout = sb("s_wout", [128, KT, D], BF16)
        s_bo = sb("s_bo", [1, D], BF16)
        s_ones_b = sb("s_ones_b", [1, CH], BF16)
        s_ones64 = sb("s_ones64", [1, 64])
        s_cols = sb("s_cols", [128, 3])
        s_c40 = sb("s_c40", [128, 1])
        s_cm20 = sb("s_cm20", [128, 1])
        s_cscr = sb("s_cscr", [2, 1])
        s_cscr2 = sb("s_cscr2", [2, 1])

        s_x = sb("s_x", [128, 4, D])
        s_xn = sb("s_xn", [128, 4, D])
        s_xnb = sb("s_xnb", [128, 4, D], BF16)
        s_lncol = sb("s_lncol", [128, 8])
        s_xnT = sb("s_xnT", [128, KT, CH], BF16)
        s_zf = sb("s_zf", [128, 2, CH])
        s_zb = sb("s_zb", [128, 2, CH], BF16)
        s_z2 = sb("s_z2", [128, 2, CH])
        s_rs = sb("s_rs", [2, CH])
        s_rsb = sb("s_rsb", [128, 2, CH])
        s_pz = sb("s_pz", [128, 2, CH])
        s_fb = sb("s_fb", [128, 2, CH])
        s_kf = sb("s_kf", [128, 8, CH], BF16)
        s_qf = sb("s_qf", [128, 8, CH], BF16)
        s_vaug = sb("s_vaug", [128, 4, NH, HD + 1], BF16)
        s_kft = sb("s_kft", [128, 4, 8, 128], BF16)
        s_carry = sb("s_carry", [64, NH * (HD + 1)])
        s_carryb = sb("s_carryb", [128, HD + 1], BF16)
        s_sS = sb("s_sS", [128, 2, CH], BF16)
        s_dent = sb("s_dent", [1, CH])
        s_dscr = sb("s_dscr", [1, CH])
        s_drec = sb("s_drec", [1, CH])
        s_denb = sb("s_denb", [64, CH])
        s_attn = sb("s_attn", [128, KT, CH], BF16)
        s_ao = s_x

        _pt = ps("p_trb0", [128, 128], BF16)
        p_trb = [_pt, _pt]
        p_bc2 = ps("p_bc2", [128, CH])
        p_mm = [ps("p_mm0", [128, CH]), ps("p_mm1", [128, CH])]
        p_S = ps("p_S", [128, CH])
        p_ctx = ps("p_ctx", [HD + 1, CH])
        p_upd = ps("p_upd", [64, HD + 1])
        p_bc = ps("p_bc", [128, CH])

        LD = P.sem("LD")
        LDC = P.sem("LDC")
        nld = [0, 0]

        def load(out, in_, cast=False):
            if cast:
                P.dma("gpsimd", out, in_, inc=LDC)
                nld[1] += 1
            else:
                P.dma("sync", out, in_, inc=LD)
                nld[0] += 1

        load(s_wqkv[:], wqkvT.rearrange("(k p) n -> p k n", p=128), cast=True)
        load(s_bq[:], bq_col)
        load(s_bv[:], bv_row, cast=True)
        load(s_om[:], omega_bd.rearrange("a p n -> p a n"), cast=True)
        load(s_qv[:], quadv)
        load(s_qw[:], quadw)
        load(s_selq[:], selq)
        load(s_sel2[:], sel2)
        load(s_bd2[:], bd2)
        load(s_flag[:], flag)
        load(s_flagb[:], flag_bf)
        load(s_tri[:], tri)
        load(s_idb[:], ident_b)
        load(s_wout[:], woutT.rearrange("(k p) n -> p k n", p=128), cast=True)
        load(s_bo[:], bo_row, cast=True)
        LDA = [(LD, 16 * nld[0]), (LDC, 16 * nld[1])]

        V, S, TE, DO = P.sem("V"), P.sem("S"), P.sem("TE"), P.sem("DO")
        TRC, SAV = P.sem("TRC"), P.sem("SAV")
        CHV, CHT, CHU = P.sem("CHV"), P.sem("CHT"), P.sem("CHU")
        DX = [P.sem("DX0")]
        dx_cnt = [0]

        P.op("vector", lambda e: e.memset(s_ones_b[:], 1.0), wait=LDA)
        P.op("vector", lambda e: e.memset(s_ones64[:], 1.0))
        P.op("vector", lambda e: e.memset(s_c40[:], 40.0))
        P.op("vector", lambda e: e.memset(s_cm20[:], -20.0))
        P.op("vector", lambda e: e.memset(s_carry[:], 0.0))
        P.op("vector", lambda e: e.memset(s_vaug[:, :, :, HD:HD + 1], 1.0))
        P.op("vector", lambda e: e.tensor_scalar(out=s_cscr[:], in0=s_qv[:],
             scalar1=0.0, scalar2=2.0, op0=ALU.max, op1=ALU.mult))
        v1 = P.op("vector", lambda e: e.tensor_scalar(out=s_cscr[:], in0=s_cscr[:],
                  scalar1=1e-30, scalar2=None, op0=ALU.max), inc=V)
        s0 = P.op("scalar", lambda e: e.activation(out=s_cscr2[:], in_=s_cscr[:],
                  func=AF.Ln), wait=[(V, v1)], inc=S)
        s1 = P.op("scalar", lambda e: e.activation(out=s_cscr2[:], in_=s_cscr2[:],
                  func=AF.Exp, scale=0.5), wait=[(S, s0)], inc=S)
        v1 = P.op("vector", lambda e: e.tensor_scalar(out=s_cscr2[:], in0=s_cscr2[:],
                  scalar1=-1.0, scalar2=None, op0=ALU.mult), wait=[(S, s1)], inc=V)
        t1 = P.op("tensor", lambda e: e.matmul(out=p_bc[:, 0:1], lhsT=s_selq[:],
                  rhs=s_cscr2[:], start=True, stop=True), wait=[(V, v1)],
                  inc=TE, drain_inc=True)
        v1 = P.op("vector", lambda e: e.tensor_copy(out=s_cols[:, 0:1],
                  in_=p_bc[:, 0:1]), wait=[(TE, t1)], inc=V)
        v2 = P.op("vector", lambda e: e.tensor_scalar(out=s_cscr[:], in0=s_qv[:],
                  scalar1=20.0, scalar2=None, op0=ALU.add), inc=V)
        t1 = P.op("tensor", lambda e: e.matmul(out=p_bc[:, 1:2], lhsT=s_selq[:],
                  rhs=s_cscr[:], start=True, stop=True), wait=[(V, v2)],
                  inc=TE, drain_inc=True)
        v1 = P.op("vector", lambda e: e.tensor_copy(out=s_cols[:, 1:2],
                  in_=p_bc[:, 1:2]), wait=[(TE, t1)], inc=V)
        v2 = P.op("vector", lambda e: e.tensor_scalar(out=s_cscr[:], in0=s_qw[:],
                  scalar1=0.0, scalar2=1e-60, op0=ALU.max, op1=ALU.max),
                  wait=[(TE, t1)], inc=V)
        s2 = P.op("scalar", lambda e: e.activation(out=s_cscr[:], in_=s_cscr[:],
                  func=AF.Ln), wait=[(V, v2)], inc=S)
        s2 = P.op("scalar", lambda e: e.activation(out=s_cscr[:], in_=s_cscr[:],
                  func=AF.Exp, scale=0.5), wait=[(S, s2)], inc=S)
        v2 = P.op("vector", lambda e: e.tensor_scalar(out=s_cscr[:], in0=s_cscr[:],
                  scalar1=1.0 / 32.0, scalar2=None, op0=ALU.mult),
                  wait=[(S, s2)], inc=V)
        t1 = P.op("tensor", lambda e: e.matmul(out=p_bc[:, 2:3], lhsT=s_selq[:],
                  rhs=s_cscr[:], start=True, stop=True), wait=[(V, v2)],
                  inc=TE, drain_inc=True)
        v1 = P.op("vector", lambda e: e.tensor_copy(out=s_cols[:, 2:3],
                  in_=p_bc[:, 2:3]), wait=[(TE, t1)], inc=V)
        setup_v = (V, P.count["V"])

        mm_free = [(V, 0), (V, 0)]
        S_free = (V, 0)
        sS_loc = [0, 0]
        ctx_free = (V, 0)
        upd_free = (V, 0)
        bc_free = (V, P.count["V"])
        bc2_free = (V, 0)
        nsq_free = (V, 0)
        last_xn_dma = (DO, 0)
        last_nsq = (TE, 0)
        last_rs_mm = (TE, 0)
        last_feat = (V, 0)
        last_ao_dma = (DO, 0)
        last_ao_read = (TE, 0)
        ln_scalar_last = (S, 0)
        mm_turn = [0]
        pz_par = [0]

        def mm_bank():
            b = mm_turn[0] % 2
            mm_turn[0] += 1
            return b

        nch = n_pre + n_own
        for c in range(nch):
            own = c >= n_pre
            o = c - n_pre
            buf = c % 2
            wl1 = [(CHV, c)] if c >= 1 else []
            if c >= 1:
                wl1.append(last_ao_dma)
            dx_cnt[0] += 16
            P.dma("sync", s_x[:], rearr_tok(xw, c * CH, CH),
                  wait=wl1, inc=DX[0])
            ldx = (DX[0], dx_cnt[0])

            if stages == 0:
                vv = None
                for i in range(4):
                    vv = P.op("vector", lambda e, i=i: e.tensor_copy(
                        out=s_xn[:, i, :], in_=s_x[:, i, :]),
                        wait=[ldx] if i == 0 else [], inc=V)
                if own:
                    dd = P.dma("sync", rearr_tok(xn_sc, o * CH, CH), s_xn[:],
                               wait=[(V, vv)], inc=DO)
                    last_xn_dma = (DO, dd)
                P.op("tensor", lambda e: e.nop(), inc=CHT)
                P.op("vector", lambda e: e.nop(), inc=CHV)
                continue
            if stages == 10:
                vv = None
                for i in range(4):
                    vr0 = P.op("vector", lambda e, i=i: e.tensor_reduce(
                        out=s_lncol[:, 2 * i:2 * i + 1], in_=s_x[:, i, :],
                        axis=AX.X, op=ALU.add), wait=[ldx] if i == 0 else [],
                        inc=V)
                    vm = P.op("vector", lambda e, i=i, vr0=vr0: e.tensor_scalar(
                        out=s_lncol[:, 2 * i:2 * i + 1],
                        in0=s_lncol[:, 2 * i:2 * i + 1],
                        scalar1=1.0 / D, scalar2=None, op0=ALU.mult),
                        wait=[(V, vr0)], inc=V)
                    vv = P.op("vector", lambda e, i=i: e.tensor_scalar(
                        out=s_xn[:, i, :], in0=s_x[:, i, :],
                        scalar1=s_lncol[:, 2 * i:2 * i + 1], scalar2=None,
                        op0=ALU.subtract), wait=[(V, vm)], inc=V)
                if own:
                    dd = P.dma("sync", rearr_tok(xn_sc, o * CH, CH), s_xn[:],
                               wait=[(V, vv)], inc=DO)
                    last_xn_dma = (DO, dd)
                P.op("tensor", lambda e: e.nop(), inc=CHT)
                P.op("vector", lambda e: e.nop(), inc=CHV)
                continue
            if stages == 11:
                vv = None
                for i in range(4):
                    P.op("vector", lambda e, i=i: e.tensor_reduce(
                        out=s_lncol[:, 2 * i:2 * i + 1], in_=s_x[:, i, :],
                        axis=AX.X, op=ALU.add), wait=[ldx] if i == 0 else [])
                    P.op("vector", lambda e, i=i: e.tensor_scalar(
                        out=s_lncol[:, 2 * i:2 * i + 1],
                        in0=s_lncol[:, 2 * i:2 * i + 1],
                        scalar1=1.0 / D, scalar2=None, op0=ALU.mult))
                    va = P.op("vector", lambda e, i=i: e.tensor_scalar(
                        out=s_xn[:, i, :], in0=s_x[:, i, :],
                        scalar1=s_lncol[:, 2 * i:2 * i + 1], scalar2=None,
                        op0=ALU.subtract), inc=V)
                    ssq = P.op("scalar", lambda e, i=i: e.activation(
                        out=s_z2[:].rearrange("p a b -> p (a b)"),
                        in_=s_xn[:, i, :], func=AF.Square,
                        accum_out=s_lncol[:, 2 * i + 1:2 * i + 2]),
                        wait=[(V, va)], inc=S)
                    vv = P.op("vector", lambda e, i=i: e.tensor_scalar(
                        out=s_xn[:, i, :], in0=s_xn[:, i, :],
                        scalar1=s_lncol[:, 2 * i + 1:2 * i + 2], scalar2=None,
                        op0=ALU.bypass if False else ALU.mult),
                        wait=[(S, ssq)], inc=V)
                if own:
                    dd = P.dma("sync", rearr_tok(xn_sc, o * CH, CH), s_xn[:],
                               wait=[(V, vv)], inc=DO)
                    last_xn_dma = (DO, dd)
                P.op("tensor", lambda e: e.nop(), inc=CHT)
                P.op("vector", lambda e: e.nop(), inc=CHV)
                continue
            prev_S = ln_scalar_last
            for i in range(4):
                w0 = [ldx] if i == 0 else []
                if i == 0 and c == 0:
                    w0.append(setup_v)
                P.op("vector", lambda e, i=i: e.tensor_reduce(
                    out=s_lncol[:, 2 * i:2 * i + 1], in_=s_x[:, i, :],
                    axis=AX.X, op=ALU.add), wait=w0)
                P.op("vector", lambda e, i=i: e.tensor_scalar(
                    out=s_lncol[:, 2 * i:2 * i + 1],
                    in0=s_lncol[:, 2 * i:2 * i + 1],
                    scalar1=1.0 / D, scalar2=None, op0=ALU.mult))
                wxc = [prev_S, last_nsq] if i == 0 else []
                if i == 0 and own and o >= 1:
                    wxc.append(last_xn_dma)
                vv = P.op("vector", lambda e, i=i: e.tensor_scalar(
                    out=s_xn[:, i, :], in0=s_x[:, i, :],
                    scalar1=s_lncol[:, 2 * i:2 * i + 1], scalar2=None,
                    op0=ALU.subtract),
                    wait=wxc, inc=V)
                ssq = P.op("scalar", lambda e, i=i: e.activation(
                    out=s_z2[:].rearrange("p a b -> p (a b)"),
                    in_=s_xn[:, i, :], func=AF.Square,
                    accum_out=s_lncol[:, 2 * i + 1:2 * i + 2]),
                    wait=[(V, vv)], inc=S)
                vv = P.op("vector", lambda e, i=i: e.tensor_scalar(
                    out=s_lncol[:, 2 * i + 1:2 * i + 2],
                    in0=s_lncol[:, 2 * i + 1:2 * i + 2],
                    scalar1=1.0 / D, scalar2=1e-5, op0=ALU.mult, op1=ALU.add),
                    wait=[(S, ssq)], inc=V)
                ss = P.op("scalar", lambda e, i=i: e.activation(
                    out=s_lncol[:, 2 * i + 1:2 * i + 2],
                    in_=s_lncol[:, 2 * i + 1:2 * i + 2], func=AF.Ln),
                    wait=[(V, vv)], inc=S)
                ss = P.op("scalar", lambda e, i=i: e.activation(
                    out=s_lncol[:, 2 * i + 1:2 * i + 2],
                    in_=s_lncol[:, 2 * i + 1:2 * i + 2], func=AF.Exp,
                    scale=-0.5), wait=[(S, ss)], inc=S)
                vv = P.op("vector", lambda e, i=i: e.tensor_scalar(
                    out=s_xn[:, i, :], in0=s_xn[:, i, :],
                    scalar1=s_lncol[:, 2 * i + 1:2 * i + 2], scalar2=None,
                    op0=ALU.mult), wait=[(S, ss)], inc=V)
                sc = P.op("scalar", lambda e, i=i: e.activation(
                    out=s_xnb[:, i, :], in_=s_xn[:, i, :], func=AF.Copy),
                    wait=[(V, vv)], inc=S)
            ln_scalar_last = (S, P.count["S"])
            ln_done_v = (V, P.count["V"])
            ln_done_s = (S, P.count["S"])

            if own:
                dd = P.dma("sync", rearr_tok(xn_sc, o * CH, CH), s_xn[:],
                           wait=[ln_done_v], inc=DO)
                last_xn_dma = (DO, dd)

            tloc = []
            for i in range(4) if stages >= 2 else []:
                for j in range(KT):
                    idx = i * KT + j
                    w = []
                    if idx == 0:
                        w.append(ln_done_s)
                        if c >= 1:
                            w.append((CHT, c))
                    if idx >= 1:
                        w.append((TRC, tloc[idx - 1]))
                    tt = P.op("tensor", lambda e, i=i, j=j, idx=idx: e.transpose(
                        out=p_trb[0][:],
                        in_=s_xnb[:, i, j * 128:(j + 1) * 128],
                        identity=s_idb[:]), wait=w, inc=TE, drain_inc=True)
                    cc = P.op("vector", lambda e, i=i, j=j, idx=idx: e.tensor_copy(
                        out=s_xnT[:, j, i * 128:(i + 1) * 128],
                        in_=p_trb[0][:]), wait=[(TE, tt)], inc=TRC)
                    tloc.append(cc)
            xnT_done = (TRC, P.count["TRC"])

            sides = [("k", s_kf, D)] if not own else [("q", s_qf, 0), ("k", s_kf, D)]
            if stages < 3:
                sides = []
            for sname, sdst, coff in sides:
                for hp in range(8):
                    par = pz_par[0] % 2
                    pz_par[0] += 1
                    b = mm_bank()
                    tt = None
                    for k in range(KT):
                        w = []
                        if k == 0:
                            w = [xnT_done, mm_free[b]]
                        r = P.op("tensor", lambda e, k=k, hp=hp, b=b, coff=coff:
                                 e.matmul(out=p_mm[b][:],
                                          lhsT=s_wqkv[:, k, coff + hp * 128:coff + (hp + 1) * 128],
                                          rhs=s_xnT[:, k, :],
                                          start=(k == 0), stop=(k == KT - 1)),
                                 wait=w, inc=TE if k == KT - 1 else None)
                        if r is not None:
                            tt = r
                    mcol = (coff + hp * 128) // 128
                    ev = P.op("vector", lambda e, b=b, par=par, mcol=mcol:
                              e.tensor_scalar(out=s_zf[:, par, :], in0=p_mm[b][:],
                                              scalar1=s_bq[:, mcol:mcol + 1],
                                              scalar2=None, op0=ALU.add),
                              wait=[(TE, tt)], inc=V)
                    sc1 = P.op("scalar", lambda e, par=par: e.activation(
                        out=s_zb[:, par, :], in_=s_zf[:, par, :], func=AF.Copy),
                        wait=[(V, ev)], inc=S)
                    sc2 = P.op("scalar", lambda e, par=par: e.activation(
                        out=s_z2[:, par, :], in_=s_zf[:, par, :], func=AF.Square),
                        inc=S)
                    tn = P.op("tensor", lambda e, par=par: e.matmul(
                        out=p_bc[0:2, :], lhsT=s_bd2[:], rhs=s_z2[:, par, :],
                        start=True, stop=True),
                        wait=[(S, sc2), nsq_free, bc_free], inc=TE,
                        drain_inc=True)
                    vr = P.op("vector", lambda e: e.tensor_scalar(
                        out=s_rs[:], in0=p_bc[0:2, :], scalar1=1e-24,
                        scalar2=None, op0=ALU.max),
                        wait=[(TE, tn), last_rs_mm], inc=V)
                    nsq_free = (V, vr)
                    sr = P.op("scalar", lambda e: e.activation(
                        out=s_rs[:], in_=s_rs[:], func=AF.Ln),
                        wait=[(V, vr)], inc=S)
                    sr = P.op("scalar", lambda e: e.activation(
                        out=s_rs[:], in_=s_rs[:], func=AF.Exp,
                        scale=-0.5), wait=[(S, sr)], inc=S)
                    tb = P.op("tensor", lambda e: e.matmul(
                        out=p_bc[:], lhsT=s_sel2[:], rhs=s_rs[:],
                        start=True, stop=True), wait=[(S, sr), bc_free],
                        inc=TE, drain_inc=True)
                    last_rs_mm = (TE, tb)
                    vb = P.op("vector", lambda e, par=par: e.tensor_copy(
                        out=s_rsb[:, par, :], in_=p_bc[:]), wait=[(TE, tb)],
                        inc=V)
                    bc_free = (V, vb)
                    tp = P.op("tensor", lambda e, hp=hp, b=b, par=par: e.matmul(
                        out=p_mm[b][:], lhsT=s_om[:, hp, :], rhs=s_zb[:, par, :],
                        start=True, stop=True), wait=[(S, sc1)], inc=TE,
                        drain_inc=True)
                    vp = P.op("vector", lambda e, b=b, par=par: e.tensor_tensor(
                        out=s_pz[:, par, :], in0=p_mm[b][:],
                        in1=s_rsb[:, par, :], op=ALU.mult),
                        wait=[(TE, tp), (V, vb)], inc=V)
                    mm_free[b] = (V, vp)
                    sf = P.op("scalar", lambda e, par=par: e.activation(
                        out=s_fb[:, 0, :], in_=s_pz[:, par, :], func=AF.Relu,
                        scale=s_cols[:, 0:1], bias=s_cols[:, 1:2]),
                        wait=[(V, vp), last_feat] if True else [], inc=S)
                    sf = P.op("scalar", lambda e: e.activation(
                        out=s_fb[:, 0, :], in_=s_fb[:, 0, :], func=AF.Relu,
                        scale=-1.0, bias=s_c40[:, 0:1]), wait=[(S, sf)], inc=S)
                    sf = P.op("scalar", lambda e: e.activation(
                        out=s_fb[:, 0, :], in_=s_fb[:, 0, :], func=AF.Exp,
                        bias=s_cm20[:, 0:1]), wait=[(S, sf)], inc=S)
                    sf = P.op("scalar", lambda e, par=par: e.activation(
                        out=s_fb[:, 1, :], in_=s_pz[:, par, :], func=AF.Square),
                        inc=S)
                    wf = [(S, sf)]
                    if hp == 0 and c >= 1:
                        wf.append((CHT, c))
                    lf = P.op("vector", lambda e, sdst=sdst, hp=hp: e.scalar_tensor_tensor(
                        out=sdst[:, hp, :], in0=s_fb[:, 1, :],
                        scalar=s_cols[:, 2:3], in1=s_fb[:, 0, :],
                        op0=ALU.mult, op1=ALU.mult), wait=wf, inc=V)
                    last_feat = (V, lf)
            feat_done = (V, P.count["V"])
            last_nsq = (TE, P.count["TE"])

            for i in range(4) if stages >= 4 else []:
                for n in range(2):
                    b = mm_bank()
                    for k in range(KT):
                        w = [xnT_done, mm_free[b]] if k == 0 else []
                        P.op("tensor", lambda e, i=i, n=n, k=k, b=b: e.matmul(
                            out=p_mm[b][:],
                            lhsT=s_xnT[:, k, i * 128:(i + 1) * 128],
                            rhs=s_wqkv[:, k, 2 * D + n * CH:2 * D + (n + 1) * CH],
                            start=(k == 0), stop=False), wait=w)
                    tt = P.op("tensor", lambda e, n=n, b=b: e.matmul(
                        out=p_mm[b][:], lhsT=s_ones_b[0:1, 0:128],
                        rhs=s_bv[0:1, n * CH:(n + 1) * CH],
                        start=False, stop=True), inc=TE)
                    w = [(TE, tt)]
                    if i == 0 and n == 0 and c >= 1:
                        w.append((CHU, c))
                    vv = P.op("vector", lambda e, i=i, n=n, b=b: e.tensor_copy(
                        out=s_vaug[:, i, 8 * n:8 * (n + 1), 0:HD],
                        in_=p_mm[b][:].rearrange("p (h d) -> p h d", d=HD)),
                        wait=w, inc=V)
                    mm_free[b] = (V, vv)
            v_done = (V, P.count["V"])

            kloc = []
            for hp in range(8) if stages >= 4 else []:
                for j in range(4):
                    idx = hp * 4 + j
                    w = []
                    if idx == 0:
                        w.append(feat_done)
                        w.append((TRC, P.count["TRC"]))
                    if idx >= 1:
                        w.append((TRC, kloc[idx - 1]))
                    tt = P.op("tensor", lambda e, hp=hp, j=j, idx=idx: e.transpose(
                        out=p_trb[0][:],
                        in_=s_kf[:, hp, j * 128:(j + 1) * 128],
                        identity=s_idb[:]), wait=w, inc=TE, drain_inc=True)
                    cc = P.op("vector", lambda e, hp=hp, j=j, idx=idx: e.tensor_copy(
                        out=s_kft[:, j, hp, :], in_=p_trb[0][:]),
                        wait=[(TE, tt)], inc=TRC)
                    kloc.append(cc)
            kft_done = (TRC, P.count["TRC"])

            if c == n_pre and stages >= 4:
                P.op("vector", lambda e: e.tensor_scalar(
                    out=s_kft[:], in0=s_kft[:], scalar1=s_flag[:, 0:1],
                    scalar2=None, op0=ALU.mult), wait=[kft_done], inc=TRC)
                kft_done = (TRC, P.count["TRC"])

            for h in range(NH) if stages >= 5 else []:
                hp, half = h // 2, h % 2
                r0 = 64 * half
                if own:
                    vcb = P.op("vector", lambda e, h=h, r0=r0: e.tensor_copy(
                        out=s_carryb[r0:r0 + 64, :],
                        in_=s_carry[:, h * 65:(h + 1) * 65]),
                        wait=[upd_free, ctx_free], inc=V)
                    P.op("tensor", lambda e, hp=hp, r0=r0: e.matmul(
                        out=p_ctx[:], lhsT=s_carryb[r0:r0 + 64, :],
                        rhs=s_qf[r0:r0 + 64, hp, :], start=True, stop=False),
                        wait=[(V, vcb), feat_done])
                    for j in range(4):
                        tt = P.op("tensor", lambda e, hp=hp, r0=r0, j=j: e.matmul(
                            out=p_S[:],
                            lhsT=s_kf[r0:r0 + 64, hp, j * 128:(j + 1) * 128],
                            rhs=s_qf[r0:r0 + 64, hp, :], start=True, stop=True),
                            wait=[S_free], inc=TE)
                        wv = [(TE, tt)]
                        if sS_loc[j % 2] > 0:
                            wv.append((SAV, sS_loc[j % 2]))
                        vv = P.op("vector", lambda e, j=j: e.tensor_tensor(
                            out=s_sS[:, j % 2, :], in0=p_S[:],
                            in1=s_tri[:, j, :], op=ALU.mult), wait=wv, inc=V)
                        S_free = (V, vv)
                        av = P.op("tensor", lambda e, h=h, j=j: e.matmul(
                            out=p_ctx[:], lhsT=s_vaug[:, j, h, :],
                            rhs=s_sS[:, j % 2, :], start=False, stop=(j == 3)),
                            wait=[(V, vv), v_done], inc=SAV)
                        sS_loc[j % 2] = av
                    ctx_t = (SAV, P.count["SAV"])
                ut = None
                for j in range(4):
                    w = []
                    if j == 0:
                        w = [kft_done, v_done, upd_free]
                    r = P.op("tensor", lambda e, hp=hp, r0=r0, j=j, h=h: e.matmul(
                        out=p_upd[:], lhsT=s_kft[:, j, hp, r0:r0 + 64],
                        rhs=s_vaug[:, j, h, :], start=(j == 0), stop=(j == 3)),
                        wait=w, inc=TE if j == 3 else None, drain_inc=True)
                    if r is not None:
                        ut = r
                if own:
                    vv = P.op("vector", lambda e: e.tensor_copy(
                        out=s_dent[:], in_=p_ctx[HD:HD + 1, :]),
                        wait=[ctx_t], inc=V)
                    vv = P.op("vector", lambda e: e.tensor_scalar(
                        out=s_dent[:], in0=s_dent[:], scalar1=1e-6,
                        scalar2=None, op0=ALU.max), inc=V)
                    vv = P.op("vector", lambda e: e.reciprocal(
                        out=s_dent[:], in_=s_dent[:]), inc=V)
                    tb = P.op("tensor", lambda e: e.matmul(
                        out=p_bc[0:64, :], lhsT=s_ones64[:], rhs=s_dent[:],
                        start=True, stop=True), wait=[(V, vv), bc_free],
                        inc=TE, drain_inc=True)
                    vv = P.op("vector", lambda e: e.tensor_copy(
                        out=s_denb[:], in_=p_bc[0:64, :]), wait=[(TE, tb)], inc=V)
                    bc_free = (V, vv)
                    wat = [(V, vv)]
                    if h == 0:
                        wat.append(last_ao_read)
                    av = P.op("vector", lambda e, hp=hp, r0=r0: e.tensor_tensor(
                        out=s_attn[r0:r0 + 64, hp, :], in0=p_ctx[0:HD, :],
                        in1=s_denb[:], op=ALU.mult), wait=wat, inc=V)
                    ctx_free = (V, av)
                vv = P.op("vector", lambda e, h=h: e.tensor_tensor(
                    out=s_carry[:, h * 65:(h + 1) * 65],
                    in0=s_carry[:, h * 65:(h + 1) * 65], in1=p_upd[:],
                    op=ALU.add), wait=[(TE, ut)], inc=V)
                upd_free = (V, vv)
            attn_done = (V, P.count["V"])
            P.op("vector", lambda e: e.nop(), inc=CHU)

            if c == n_pre - 1 and stages >= 5:
                vv = P.op("vector", lambda e: e.tensor_scalar(
                    out=s_carry[:], in0=s_carry[:], scalar1=s_flag[0:64, 0:1],
                    scalar2=None, op0=ALU.mult), wait=[upd_free], inc=V)
                upd_free = (V, vv)

            if own and stages >= 6:
                last_ev = None
                for i in range(4):
                    for n in range(2):
                        b = mm_bank()
                        for k in range(KT):
                            w = [attn_done, mm_free[b]] if k == 0 else []
                            P.op("tensor", lambda e, i=i, n=n, k=k, b=b: e.matmul(
                                out=p_mm[b][:],
                                lhsT=s_attn[:, k, i * 128:(i + 1) * 128],
                                rhs=s_wout[:, k, n * CH:(n + 1) * CH],
                                start=(k == 0), stop=False), wait=w)
                        tt = P.op("tensor", lambda e, n=n, b=b: e.matmul(
                            out=p_mm[b][:], lhsT=s_ones_b[0:1, 0:128],
                            rhs=s_bo[0:1, n * CH:(n + 1) * CH],
                            start=False, stop=True), inc=TE)
                        w = [(TE, tt)]
                        last_ev = P.op("vector", lambda e, i=i, n=n, b=b:
                                       e.tensor_copy(out=s_ao[:, i, n * CH:(n + 1) * CH],
                                                     in_=p_mm[b][:]),
                                       wait=w, inc=V)
                        mm_free[b] = (V, last_ev)
                last_ao_read = (TE, P.count["TE"])
                dd = P.dma("sync", rearr_tok(ao_sc, o * CH, CH), s_ao[:],
                           wait=[(V, last_ev)], inc=DO)
                last_ao_dma = (DO, dd)
            P.op("tensor", lambda e: e.nop(), inc=CHT)
            P.op("vector", lambda e: e.nop(), inc=CHV)

        if standalone:
            P.op("sync", lambda e: e.nop(), wait=[(DO, P.count["DO"])])
            P.build(es)
        scr = dict(xn_sc=xn_sc, ao_sc=ao_sc, ident_b=ident_b, flag=flag,
                   tri=tri)
    return nc if standalone else scr


def stage_a_inputs(inputs, core, n_pre=7, n_own=9):
    import ml_dtypes
    x = np.asarray(inputs["L0_attn_x"])
    b, half = core // 2, core % 2
    seq = x[b]
    TO = n_own * CH
    T = (n_pre + n_own) * CH
    if half == 1:
        xw = seq[8192 - T:] if T < 8192 else seq
    else:
        xw = np.concatenate(
            [np.zeros((T - TO + CH, D), np.float32), seq[:TO - CH]], 0)
    wqkvT = np.ascontiguousarray(np.asarray(inputs["L0_attn_w_qkv"]).T)
    bq = np.ascontiguousarray(np.asarray(inputs["L0_attn_b_qkv"]).reshape(24, 128).T)
    bv = np.asarray(inputs["L0_attn_b_qkv"])[2 * D:].reshape(1, D)
    om = np.asarray(inputs["L0_attn_omega"])
    om_bd = np.zeros((8, 128, 128), np.float32)
    for hp in range(8):
        for t in range(2):
            h = 2 * hp + t
            blk = np.concatenate([om[0, h], om[1, h]], axis=1)
            om_bd[hp, 64 * t:64 * t + 64, 64 * t:64 * t + 64] = blk
    qv = np.asarray(inputs["L0_attn_quad_nodes"]).reshape(2, 1).astype(np.float32)
    qw = np.asarray(inputs["L0_attn_quad_weights"]).reshape(2, 1).astype(np.float32)
    selq = np.zeros((2, 128), np.float32)
    for p in range(128):
        selq[(p // 32) % 2, p] = 1.0
    sel2 = np.zeros((2, 128), np.float32)
    sel2[0, 0:64] = 1.0
    sel2[1, 64:128] = 1.0
    bd2 = np.zeros((128, 2), np.float32)
    bd2[0:64, 0] = 1.0
    bd2[64:128, 1] = 1.0
    flagv = np.full((128, 1), float(half), np.float32)
    tri = np.zeros((128, 4, CH), np.float32)
    ar = np.arange(CH)
    for j in range(4):
        for p in range(128):
            tri[p, j, :] = (j * 128 + p <= ar)
    woutT = np.ascontiguousarray(np.asarray(inputs["L0_attn_w_out"]).T)
    bo = np.asarray(inputs["L0_attn_b_out"]).reshape(1, D)
    return dict(
        xw=np.ascontiguousarray(xw, dtype=np.float32),
        wqkvT=wqkvT.astype(np.float32), bq_col=bq.astype(np.float32),
        bv_row=bv.astype(np.float32), omega_bd=om_bd, quadv=qv, quadw=qw,
        selq=selq, sel2=sel2, bd2=bd2, flag=flagv,
        flag_bf=flagv.astype(ml_dtypes.bfloat16), tri=tri,
        ident_b=np.eye(128, dtype=np.float32).astype(ml_dtypes.bfloat16),
        woutT=woutT.astype(np.float32), bo_row=bo.astype(np.float32))




def build_phase_b(n_own=9, chain="ve", ctx=None, scr=None):
    standalone = ctx is None
    if standalone:
        nc = bass.Bass("TRN2", target_bir_lowering=False, debug=False)
    else:
        nc, P = ctx
    TO = n_own * CH
    I = 2816
    IT = I // 128
    HW = 1152
    if standalone:
        xn_in = nc.dram_tensor("xn_in", [TO, D], F32, kind="ExternalInput").ap()
        ao_in = nc.dram_tensor("ao_in", [TO, D], F32, kind="ExternalInput").ap()
    else:
        xn_in, ao_in = scr["xn_sc"], scr["ao_sc"]
    x_in = nc.dram_tensor("x_in", [TO, D], F32, kind="ExternalInput").ap()
    wg_in = nc.dram_tensor("wg_in", [I, D], F32, kind="ExternalInput").ap()
    wu_in = nc.dram_tensor("wu_in", [I, D], F32, kind="ExternalInput").ap()
    wd_in = nc.dram_tensor("wd_in", [D, I], F32, kind="ExternalInput").ap()
    if standalone:
        ident_b = nc.dram_tensor("ident_b", [128, 128], BF16, kind="ExternalInput").ap()
    else:
        ident_b = scr["ident_b"]
    x1_sc = nc.dram_tensor(
        "x1_sc", [TO, D], F32,
        kind="ExternalOutput" if standalone else "Internal").ap()

    if standalone:
        P = Prog(nc, chain=chain)
    es = ExitStack()
    with es:
        sb = lambda n, s, d=F32: es.enter_context(nc.sbuf_tensor('pb_' + n, s, d))
        ps = lambda n, s, d=F32: es.enter_context(nc.psum_tensor('pb_' + n, s, d))

        s_wgq = sb("s_wgq", [128, KT, I], BF16)
        s_wuq = sb("s_wuq", [128, KT, I], BF16)
        s_wdq = sb("s_wdq", [128, IT, D], BF16)
        s_idb = sb("s_idb", [128, 128], BF16)
        s_h = sb("s_h", [128, 2, I])
        s_u = sb("s_u", [128, HW])
        s_au = sb("s_au", [128, HW])
        s_c = sb("s_c", [128, HW])
        s_d = sb("s_d", [128, HW])
        s_m = sb("s_m", [128, 72])
        s_s = sb("s_s", [128, 72])
        s_rcp = sb("s_rcp", [128, 72])
        s_sh = sb("s_sh", [128, 72])
        s_qh = sb("s_qh", [128, 2, I], BF16)
        s_xqT = sb("s_xqT", [128, KT, 128], BF16)
        s_hqT = sb("s_hqT", [128, IT, 128], BF16)
        s_xni = sb("s_xni", [128, D])
        s_aoi = sb("s_aoi", [128, D])
        s_xi = sb("s_xi", [128, D])
        s_sg = sb("s_sg", [128, CH])

        p_g = ps("p_g", [128, CH])
        p_u = ps("p_u", [128, CH])
        p_t = [ps("p_t0", [128, 128], BF16), ps("p_t1", [128, 128], BF16)]
        p_fo = [ps("p_fo0", [128, CH]), ps("p_fo1", [128, CH])]

        LD = P.sem("LD_B")
        V, S, TE, DO = P.sem("V"), P.sem("S"), P.sem("TE"), P.sem("DO")
        TRC = P.sem("TRC")
        DW = P.sem("DW_B")
        DOB = P.sem("DO_B")
        DXN = P.sem("DXN_B")
        DAO = P.sem("DAO_B")
        DXI = P.sem("DXI_B")

        P.dma("sync", s_idb[:], ident_b, inc=LD)
        ld0 = P.count["LD_B"]

        st = {"v": P.count["V"], "te": P.count["TE"]}
        tr_guard = {}

        def quant(src_ap, dst_ap, W, wait_src=(), gwait=(), dst_guard=()):
            nh = (W + HW - 1) // HW
            last = None
            for hh in range(nh):
                w0, w1 = hh * HW, min((hh + 1) * HW, W)
                ww = w1 - w0
                G = ww // 16
                src3 = src_ap[:, w0:w1].rearrange("p (g k) -> p g k", k=16)
                dst3 = dst_ap[:, w0:w1].rearrange("p (g k) -> p g k", k=16)
                wfirst = (list(wait_src) + list(gwait)) if hh == 0 else []
                P.op("vector", lambda e, src3=src3, G=G: e.tensor_reduce(
                    out=s_m[:, 0:G], in_=src3, axis=AX.X, op=ALU.max,
                    apply_absolute_value=True), wait=wfirst)
                P.op("vector", lambda e, G=G: e.tensor_scalar(
                    out=s_s[:, 0:G], in0=s_m[:, 0:G], scalar1=1.0 / 6.0,
                    scalar2=1e-12, op0=ALU.mult, op1=ALU.max))
                P.recip(s_rcp[:, 0:G], s_s[:, 0:G], None)
                P.op("vector", lambda e, G=G: e.tensor_scalar(
                    out=s_sh[:, 0:G], in0=s_s[:, 0:G], scalar1=0.5,
                    scalar2=None, op0=ALU.mult))
                vv = P.op("vector", lambda e, src3=src3, G=G, ww=ww:
                          e.tensor_tensor(
                    out=s_u[:, 0:ww].rearrange("p (g k) -> p g k", k=16),
                    in0=src3,
                    in1=s_rcp[:, 0:G].unsqueeze(-1).broadcast_to((128, G, 16)),
                    op=ALU.mult), inc=V)
                sa = P.op("scalar", lambda e, ww=ww: e.activation(
                    out=s_au[:, 0:ww], in_=s_u[:, 0:ww], func=AF.Abs),
                    wait=[(V, vv)], inc=S)
                P.op("vector", lambda e, ww=ww: e.tensor_scalar(
                    out=s_c[:, 0:ww], in0=s_au[:, 0:ww], scalar1=0.25,
                    scalar2=None, op0=ALU.is_gt), wait=[(S, sa)])
                for t in (0.75, 1.25, 1.75):
                    P.op("vector", lambda e, t=t, ww=ww: e.scalar_tensor_tensor(
                        out=s_c[:, 0:ww], in0=s_au[:, 0:ww], scalar=t,
                        in1=s_c[:, 0:ww], op0=ALU.is_gt, op1=ALU.add))
                P.op("vector", lambda e, ww=ww: e.tensor_scalar(
                    out=s_d[:, 0:ww], in0=s_au[:, 0:ww], scalar1=2.5,
                    scalar2=None, op0=ALU.is_gt))
                P.op("vector", lambda e, ww=ww: e.scalar_tensor_tensor(
                    out=s_d[:, 0:ww], in0=s_au[:, 0:ww], scalar=3.5,
                    in1=s_d[:, 0:ww], op0=ALU.is_gt, op1=ALU.add))
                P.op("vector", lambda e, ww=ww: e.scalar_tensor_tensor(
                    out=s_c[:, 0:ww], in0=s_d[:, 0:ww], scalar=2.0,
                    in1=s_c[:, 0:ww], op0=ALU.mult, op1=ALU.add))
                P.op("vector", lambda e, ww=ww: e.tensor_scalar(
                    out=s_d[:, 0:ww], in0=s_au[:, 0:ww], scalar1=5.0,
                    scalar2=4.0, op0=ALU.is_gt, op1=ALU.mult))
                vv = P.op("vector", lambda e, ww=ww: e.tensor_add(
                    out=s_c[:, 0:ww], in0=s_c[:, 0:ww], in1=s_d[:, 0:ww]),
                    inc=V)
                sg_ = P.op("scalar", lambda e, ww=ww: e.sign(
                    out=s_au[:, 0:ww], in_=s_u[:, 0:ww]), wait=[(V, vv)], inc=S)
                P.op("vector", lambda e, ww=ww: e.tensor_mul(
                    out=s_c[:, 0:ww], in0=s_c[:, 0:ww], in1=s_au[:, 0:ww]),
                    wait=[(S, sg_)] + (list(dst_guard) if hh == 0 else []))
                last = P.op("vector", lambda e, dst3=dst3, G=G, ww=ww:
                            e.tensor_tensor(
                    out=dst3,
                    in0=s_c[:, 0:ww].rearrange("p (g k) -> p g k", k=16),
                    in1=s_sh[:, 0:G].unsqueeze(-1).broadcast_to((128, G, 16)),
                    op=ALU.mult), inc=V)
            return last

        dwc = [0]
        wq_v = {}
        slabs = ([(wg_in, s_wgq, sl, D, True) for sl in range(IT)]
                 + [(wu_in, s_wuq, sl, D, True) for sl in range(IT)]
                 + [(wd_in, s_wdq, sl, I, False) for sl in range(KT)])
        for gi, (wsrc, wdst, sl, wwidth, d_is_g) in enumerate(slabs):
            slot = gi % 2
            wld = [(V, wq_v.get(gi - 2, 0))]
            if gi == 0:
                wld.append((LD, ld0))
            dwc[0] += 16
            P.dma("sync", s_h[:, slot, 0:wwidth],
                  wsrc[sl * 128:(sl + 1) * 128, :], wait=wld, inc=DW)
            lastq = quant(s_h[:, slot, :], s_qh[:, slot, :], wwidth,
                          wait_src=[(DW, dwc[0])],
                          dst_guard=[(TE, st["te"])])
            wq_v[gi] = lastq
            ntr = wwidth // 128
            for j in range(ntr):
                idx = j % 2
                w = [(V, lastq)] if j == 0 else []
                g = tr_guard.get(idx)
                if g is not None:
                    w.append((TRC, g))
                tt = P.op("tensor", lambda e, j=j, idx=idx, slot=slot: e.transpose(
                    out=p_t[idx][:], in_=s_qh[:, slot, j * 128:(j + 1) * 128],
                    identity=s_idb[:]), wait=w, inc=TE)
                cc = P.op("scalar", lambda e, j=j, idx=idx, sl=sl, wdst=wdst:
                          e.activation(
                    out=wdst[:, j, sl * 128:(sl + 1) * 128],
                    in_=p_t[idx][:], func=AF.Identity),
                    wait=[(TE, tt)], inc=TRC)
                tr_guard[idx] = cc
            st["te"] = P.count["TE"]
        wq_done = (TRC, P.count["TRC"])

        tiles = n_own * 4
        mm_free_g = (S, 0)
        mm_free_u = (V, 0)
        fo_free = [(V, 0), (V, 0)]
        last_x1_dma = {}
        dxn_c, dao_c, dxi_c = [0], [0], [0]
        xnq_done = {}
        xqT_done = {}
        xq_tr_te = {}
        xq_used = {}
        te_after_def = {}
        hq_done = {}
        ev_done = {}
        dao_t = {}
        dxi_t = {}

        def stage_a(t):
            w = [(V, xnq_done.get(t - 1, 0))]
            if t == 0:
                w.append(wq_done)
            dxn_c[0] += 16
            P.dma("sync", s_xni[:], xn_in[t * 128:(t + 1) * 128, :],
                  wait=w, inc=DXN)
            lastq = quant(s_xni, s_qh[:, t % 2, :], D,
                          wait_src=[(DXN, dxn_c[0])],
                          dst_guard=[(TE, te_after_def.get(t - 2, 0))])
            xnq_done[t] = lastq


        def stage_b(t):
            for j in range(KT):
                idx = j % 2
                w = [(V, xnq_done[t])] if j == 0 else []
                if j == 0:
                    w.append((TE, xq_used.get(t - 2, 0)))
                g = tr_guard.get(idx)
                if g is not None:
                    w.append((TRC, g))
                tt = P.op("tensor", lambda e, j=j, idx=idx, t=t: e.transpose(
                    out=p_t[idx][:], in_=s_qh[:, t % 2, j * 128:(j + 1) * 128],
                    identity=s_idb[:]), wait=w, inc=TE)
                cc = P.op("scalar", lambda e, j=j, idx=idx, t=t: e.activation(
                    out=s_xqT[:, j, :], in_=p_t[idx][:], func=AF.Identity),
                    wait=[(TE, tt)], inc=TRC)
                tr_guard[idx] = cc
            xqT_done[t] = P.count["TRC"]
            xq_tr_te[t] = P.count["TE"]

        def stage_c(t):
            for nt in range(6):
                n0 = nt * CH
                nn_ = min(n0 + CH, I) - n0
                tg = tu = None
                for k in range(KT):
                    w = [(TRC, xqT_done[t]), mm_free_g2[0]] if k == 0 else []
                    r = P.op("tensor", lambda e, k=k, n0=n0, nn_=nn_, t=t: e.matmul(
                        out=p_g[:, 0:nn_], lhsT=s_xqT[:, k, :],
                        rhs=s_wgq[:, k, n0:n0 + nn_],
                        start=(k == 0), stop=(k == KT - 1)),
                        wait=w, inc=TE if k == KT - 1 else None)
                    if r is not None:
                        tg = r
                for k in range(KT):
                    w = [mm_free_u2[0]] if k == 0 else []
                    r = P.op("tensor", lambda e, k=k, n0=n0, nn_=nn_, t=t: e.matmul(
                        out=p_u[:, 0:nn_], lhsT=s_xqT[:, k, :],
                        rhs=s_wuq[:, k, n0:n0 + nn_],
                        start=(k == 0), stop=(k == KT - 1)),
                        wait=w, inc=TE if k == KT - 1 else None)
                    if r is not None:
                        tu = r
                ssg = P.op("scalar", lambda e, nn_=nn_: e.activation(
                    out=s_sg[:, 0:nn_], in_=p_g[:, 0:nn_], func=AF.Silu),
                    wait=[(TE, tg), mm_free_u2[0]], inc=S)
                mm_free_g2[0] = (S, ssg)
                vh = P.op("vector", lambda e, n0=n0, nn_=nn_, t=t: e.tensor_tensor(
                    out=s_h[:, t % 2, n0:n0 + nn_], in0=p_u[:, 0:nn_],
                    in1=s_sg[:, 0:nn_], op=ALU.mult),
                    wait=[(S, ssg), (TE, tu)], inc=V)
                mm_free_u2[0] = (V, vh)
            ev_done[t] = P.count["V"]
            xq_used[t] = P.count["TE"]

        def stage_d(t):
            dxi_c[0] += 16
            P.dma("gpsimd", s_xi[:], x_in[t * 128:(t + 1) * 128, :],
                  wait=[(V, last_lastv[0])], inc=DXI)
            dxi_t[t] = dxi_c[0]
            dao_c[0] += 16
            P.dma("gpsimd", s_aoi[:], ao_in[t * 128:(t + 1) * 128, :],
                  wait=[(DOB, last_x1_dma.get(t - 1, 0))], inc=DAO)
            dao_t[t] = dao_c[0]
            lastq = quant(s_h[:, t % 2, :], s_qh[:, t % 2, :], I,
                          wait_src=[(V, ev_done[t])],
                          dst_guard=[(TE, xq_tr_te[t])])
            hq_done[t] = lastq

        def stage_ef(t):
            lastq = hq_done[t]
            for j in range(IT):
                idx = j % 2
                w = [(V, lastq)] if j == 0 else []
                g = tr_guard.get(idx)
                if g is not None:
                    w.append((TRC, g))
                tt = P.op("tensor", lambda e, j=j, idx=idx, t=t: e.transpose(
                    out=p_t[idx][:], in_=s_qh[:, t % 2, j * 128:(j + 1) * 128],
                    identity=s_idb[:]), wait=w, inc=TE)
                cc = P.op("scalar", lambda e, j=j, idx=idx: e.activation(
                    out=s_hqT[:, j, :], in_=p_t[idx][:], func=AF.Identity),
                    wait=[(TE, tt)], inc=TRC)
                tr_guard[idx] = cc
            hqT_t = P.count["TRC"]
            lastv = None
            for n2 in range(2):
                tf = None
                for k in range(IT):
                    w = [(TRC, hqT_t), fo_free[n2]] if k == 0 else []
                    r = P.op("tensor", lambda e, k=k, n2=n2: e.matmul(
                        out=p_fo[n2][:], lhsT=s_hqT[:, k, :],
                        rhs=s_wdq[:, k, n2 * CH:(n2 + 1) * CH],
                        start=(k == 0), stop=(k == IT - 1)),
                        wait=w, inc=TE if k == IT - 1 else None)
                    if r is not None:
                        tf = r
                vv = P.op("vector", lambda e, n2=n2, t=t: e.tensor_tensor(
                    out=s_aoi[:, n2 * CH:(n2 + 1) * CH],
                    in0=p_fo[n2][:],
                    in1=s_aoi[:, n2 * CH:(n2 + 1) * CH],
                    op=ALU.add), wait=[(TE, tf), (DAO, dao_t[t])], inc=V)
                fo_free[n2] = (V, vv)
                lastv = P.op("vector", lambda e, n2=n2, t=t: e.tensor_add(
                    out=s_aoi[:, n2 * CH:(n2 + 1) * CH],
                    in0=s_aoi[:, n2 * CH:(n2 + 1) * CH],
                    in1=s_xi[:, n2 * CH:(n2 + 1) * CH]),
                    wait=[(DXI, dxi_t[t])] if n2 == 0 else [], inc=V)
                last_lastv[0] = lastv
            dd = P.dma("gpsimd", x1_sc[t * 128:(t + 1) * 128, :],
                       s_aoi[:], wait=[(V, lastv)], inc=DOB)
            last_x1_dma[t] = dd
            te_after_def[t] = P.count["TE"]

        mm_free_g2 = [mm_free_g]
        mm_free_u2 = [mm_free_u]
        last_lastv = [0]
        for t in range(tiles + 1):
            if t < tiles:
                stage_a(t)
                stage_b(t)
            if t >= 1:
                stage_d(t - 1)
            if t < tiles:
                stage_c(t)
            if t >= 1:
                stage_ef(t - 1)
        if standalone:
            P.op("sync", lambda e: e.nop(), wait=[(DOB, P.count["DO_B"])])
            P.build(es)
        scr2 = dict(scr or {})
        scr2["x1_sc"] = x1_sc
    return nc if standalone else scr2


def stage_b_inputs(inputs, core, n_own=9):
    import ml_dtypes
    x = np.asarray(inputs["L0_attn_x"])
    b, half = core // 2, core % 2
    seq = x[b]
    TO = n_own * CH
    if half == 1:
        x_own = seq[8192 - TO:]
    else:
        x_own = np.concatenate([np.zeros((CH, D), np.float32),
                                seq[:TO - CH]], 0)
    return dict(
        x_in=np.ascontiguousarray(x_own, dtype=np.float32),
        wg_in=np.asarray(inputs["L0_ffn_w_gate"], np.float32),
        wu_in=np.asarray(inputs["L0_ffn_w_up"], np.float32),
        wd_in=np.asarray(inputs["L0_ffn_w_down"], np.float32),
        ident_b=np.eye(128, dtype=np.float32).astype(ml_dtypes.bfloat16))




def build_phase_c(n_own=9, chain="ve", ctx=None, scr=None):
    standalone = ctx is None
    if standalone:
        nc = bass.Bass("TRN2", target_bir_lowering=False, debug=False)
    else:
        nc, P = ctx
    TO = n_own * CH
    NQ = n_own - 1
    if standalone:
        x1_in = nc.dram_tensor("x1_in", [TO, D], F32, kind="ExternalInput").ap()
    else:
        x1_in = scr["x1_sc"]
    w1_in = nc.dram_tensor("w1_in", [D, 3 * D], F32, kind="ExternalInput").ap()
    wo1_in = nc.dram_tensor("wo1_in", [D, D], F32, kind="ExternalInput").ap()
    gwT_in = nc.dram_tensor("gwT_in", [D, 16], F32, kind="ExternalInput").ap()
    sgwT_in = nc.dram_tensor("sgwT_in", [128, KT], F32, kind="ExternalInput").ap()
    if standalone:
        flag = nc.dram_tensor("flag", [128, 1], F32, kind="ExternalInput").ap()
        tri = nc.dram_tensor("tri", [128, 4, CH], F32, kind="ExternalInput").ap()
        ident_b = nc.dram_tensor("ident_b", [128, 128], BF16, kind="ExternalInput").ap()
    else:
        flag, tri, ident_b = scr["flag"], scr["tri"], scr["ident_b"]
    ident_f = nc.dram_tensor("ident_f", [128, 128], F32, kind="ExternalInput").ap()

    kind = "ExternalOutput" if standalone else "Internal"
    ao1_sc = nc.dram_tensor("ao1_sc", [NQ * CH, D], F32, kind=kind).ap()
    xn1T_sc = nc.dram_tensor("xn1T_sc", [NQ, 128, KT, CH], BF16, kind=kind).ap()
    cmbT_sc = nc.dram_tensor("cmbT_sc", [NQ, 4, 16, 128], F32, kind=kind).ap()
    gate_sc = nc.dram_tensor("gate_sc", [NQ, 128, 4], F32, kind=kind).ap()

    if standalone:
        P = Prog(nc, chain=chain)
    es = ExitStack()
    with es:
        sb = lambda n, s, d=F32: es.enter_context(nc.sbuf_tensor('pc_' + n, s, d))
        ps = lambda n, s, d=F32: es.enter_context(nc.psum_tensor('pc_' + n, s, d))

        s_w1 = sb("s_w1", [128, KT, 3 * D], BF16)
        s_wo1 = sb("s_wo1", [128, KT, D], BF16)
        s_gwT = sb("s_gwT", [128, KT, 16])
        s_sgwT = sb("s_sgwT", [128, KT])
        s_flag = sb("s_flag", [128, 1])
        s_tri = sb("s_tri", [128, 4, CH])
        s_idb = sb("s_idb", [128, 128], BF16)
        s_idf = sb("s_idf", [128, 128])
        s_x1 = sb("s_x1", [128, 4, D])
        s_xn1 = sb("s_xn1", [128, 4, D])
        s_xnb = sb("s_xnb", [128, 4, D], BF16)
        s_lncol = sb("s_lncol", [128, 8])
        s_sq2 = sb("s_sq2", [128, D])
        s_xn1T = sb("s_xn1T", [128, KT, CH], BF16)
        s_xTf = sb("s_xTf", [128, KT, 128])
        s_qf = sb("s_qf", [128, KT, CH], BF16)
        s_kr = [sb("s_kr0", [128, KT, CH], BF16), sb("s_kr1", [128, KT, CH], BF16)]
        s_vr = [sb("s_vr0", [128, 4, NH, HD + 1], BF16),
                sb("s_vr1", [128, 4, NH, HD + 1], BF16)]
        s_eSf = sb("s_eSf", [128, CH])
        s_eSb = sb("s_eSb", [128, 2, CH], BF16)
        s_dent = sb("s_dent", [1, CH])
        s_dscr = sb("s_dscr", [1, CH])
        s_drec = sb("s_drec", [1, CH])
        s_denb = sb("s_denb", [64, CH])
        s_attn = sb("s_attn", [128, KT, CH], BF16)
        s_pr = sb("s_pr", [128, 16])
        s_pc = sb("s_pc", [128, 10])
        s_cmb = sb("s_cmb", [128, 16])
        s_cmbT = sb("s_cmbT", [16, 128])
        s_gq = sb("s_gq", [128, 4])
        s_ones64 = sb("s_ones64c", [1, 64])

        _pt0 = ps("p_t0", [128, 128], BF16)
        p_t = [_pt0, _pt0]
        p_tf = ps("p_tf", [128, 128])
        p_S = [ps("p_S0", [128, CH]), ps("p_S1", [128, CH])]
        p_ctx = ps("p_ctx", [HD + 1, CH])
        p_mm = [ps("p_mm0", [128, CH]), ps("p_mm1", [128, CH])]
        p_bc = ps("p_bc", [128, CH])

        LD = P.sem("LD_C")
        LDC = P.sem("LDC_C")
        V, S, TE, DO = P.sem("V"), P.sem("S"), P.sem("TE"), P.sem("DO")
        TRC, SAV = P.sem("TRC"), P.sem("SAV")
        DX1 = P.sem("DX_C")
        nld = [0, 0]

        def load(out, in_, cast=False):
            if cast:
                P.dma("gpsimd", out, in_, inc=LDC)
                nld[1] += 1
            else:
                P.dma("sync", out, in_, inc=LD)
                nld[0] += 1

        load(s_w1[:], w1_in.rearrange("(k p) n -> p k n", p=128), cast=True)
        load(s_wo1[:], wo1_in.rearrange("(k p) n -> p k n", p=128), cast=True)
        load(s_gwT[:], gwT_in.rearrange("(k p) n -> p k n", p=128))
        load(s_sgwT[:], sgwT_in)
        load(s_flag[:], flag)
        load(s_tri[:], tri)
        load(s_idb[:], ident_b)
        load(s_idf[:], ident_f)
        LDA = [(LD, 16 * nld[0]), (LDC, 16 * nld[1])]

        P.op("vector", lambda e: e.memset(s_vr[0][:, :, :, HD:HD + 1], 1.0),
             wait=LDA)
        P.op("vector", lambda e: e.memset(s_vr[1][:, :, :, HD:HD + 1], 1.0))
        P.op("vector", lambda e: e.memset(s_ones64[:], 1.0))
        setup_v = (V, P.op("vector", lambda e: e.nop(), inc=V))

        tr_guard = {}
        dx_cnt = [0]
        state = {"te": 0, "v": 0}
        mm_free = [(V, 0), (V, 0)]
        S_free = [(S, 0), (S, 0)]
        eSb_guard = [0, 0]
        ctx_free = (V, 0)
        bc_free = (V, 0)
        tf_free = (V, 0)
        ring_guard = [(TE, 0), (TE, 0)]
        qf_guard = (TE, 0)
        attn_guard = (TE, 0)
        last_ao_dma = (DO, 0)
        last_xn1T_dma = (DO, 0)
        last_cmb_dma = (DO, 0)
        prev_S_ln = (S, 0)
        mm_turn = [0]

        def mm_bank():
            b = mm_turn[0] % 2
            mm_turn[0] += 1
            return b

        for c in range(n_own):
            slot = c % 2
            cq = c - 1
            wl = [(V, state["v"])] if c >= 1 else []
            if c >= 1:
                wl.append(last_ao_dma)
            dx_cnt[0] += 16
            P.dma("sync", s_x1[:], rearr_tok(x1_in, c * CH, CH),
                  wait=wl, inc=DX1)
            ldx = (DX1, dx_cnt[0])

            for i in range(4):
                w0 = [ldx] if i == 0 else []
                if i == 0 and c == 0:
                    w0.append(setup_v)
                vr0 = P.op("vector", lambda e, i=i: e.tensor_reduce(
                    out=s_lncol[:, 2 * i:2 * i + 1], in_=s_x1[:, i, :],
                    axis=AX.X, op=ALU.add), wait=w0, inc=V)
                vm = P.op("vector", lambda e, i=i: e.tensor_scalar(
                    out=s_lncol[:, 2 * i:2 * i + 1],
                    in0=s_lncol[:, 2 * i:2 * i + 1],
                    scalar1=1.0 / D, scalar2=None, op0=ALU.mult),
                    wait=[(V, vr0)], inc=V)
                wxc = [(V, vm)]
                if i == 0:
                    wxc.append(prev_S_ln)
                vv = P.op("vector", lambda e, i=i: e.tensor_scalar(
                    out=s_xn1[:, i, :], in0=s_x1[:, i, :],
                    scalar1=s_lncol[:, 2 * i:2 * i + 1], scalar2=None,
                    op0=ALU.subtract), wait=wxc, inc=V)
                ssq = P.op("scalar", lambda e, i=i: e.activation(
                    out=s_sq2[:], in_=s_xn1[:, i, :], func=AF.Square,
                    accum_out=s_lncol[:, 2 * i + 1:2 * i + 2]),
                    wait=[(V, vv)], inc=S)
                vv = P.op("vector", lambda e, i=i: e.tensor_scalar(
                    out=s_lncol[:, 2 * i + 1:2 * i + 2],
                    in0=s_lncol[:, 2 * i + 1:2 * i + 2],
                    scalar1=1.0 / D, scalar2=1e-5, op0=ALU.mult, op1=ALU.add),
                    wait=[(S, ssq)], inc=S if False else V)
                ss = P.op("scalar", lambda e, i=i: e.activation(
                    out=s_lncol[:, 2 * i + 1:2 * i + 2],
                    in_=s_lncol[:, 2 * i + 1:2 * i + 2], func=AF.Ln),
                    wait=[(V, vv)], inc=S)
                ss = P.op("scalar", lambda e, i=i: e.activation(
                    out=s_lncol[:, 2 * i + 1:2 * i + 2],
                    in_=s_lncol[:, 2 * i + 1:2 * i + 2], func=AF.Exp,
                    scale=-0.5), wait=[(S, ss)], inc=S)
                vv = P.op("vector", lambda e, i=i: e.tensor_scalar(
                    out=s_xn1[:, i, :], in0=s_xn1[:, i, :],
                    scalar1=s_lncol[:, 2 * i + 1:2 * i + 2], scalar2=None,
                    op0=ALU.mult), wait=[(S, ss)], inc=V)
                P.op("scalar", lambda e, i=i: e.activation(
                    out=s_xnb[:, i, :], in_=s_xn1[:, i, :], func=AF.Copy),
                    wait=[(V, vv)], inc=S)
            prev_S_ln = (S, P.count["S"])
            ln_s = (S, P.count["S"])

            tloc = []
            for i in range(4):
                for j in range(KT):
                    idx = i * KT + j
                    w = []
                    if idx == 0:
                        w.append(ln_s)
                        w.append(qf_guard)
                        if c >= 2:
                            w.append(last_xn1T_dma)
                    g = tr_guard.get(0)
                    if g is not None:
                        w.append((TRC, g))
                    tt = P.op("tensor", lambda e, i=i, j=j, idx=idx: e.transpose(
                        out=p_t[0][:],
                        in_=s_xnb[:, i, j * 128:(j + 1) * 128],
                        identity=s_idb[:]), wait=w, inc=TE)
                    cc = P.op("vector", lambda e, i=i, j=j, idx=idx: e.tensor_copy(
                        out=s_xn1T[:, j, i * 128:(i + 1) * 128],
                        in_=p_t[0][:]), wait=[(TE, tt)], inc=TRC)
                    tr_guard[0] = cc
                    tloc.append(cc)
            xn1T_done = (TRC, P.count["TRC"])
            if c >= 1:
                dd = P.dma("sync", xn1T_sc[cq], s_xn1T[:],
                           wait=[xn1T_done], inc=DO)
                last_xn1T_dma = (DO, dd)

            for part, coff in (("q", 0), ("k", D)):
                for hp in range(8):
                    b = mm_bank()
                    tt = None
                    for k in range(KT):
                        w = [xn1T_done, mm_free[b]] if k == 0 else []
                        if k == 0 and part == "k":
                            w.append(ring_guard[slot])
                        r = P.op("tensor", lambda e, k=k, hp=hp, b=b, coff=coff:
                                 e.matmul(out=p_mm[b][:],
                                          lhsT=s_w1[:, k, coff + hp * 128:coff + (hp + 1) * 128],
                                          rhs=s_xn1T[:, k, :],
                                          start=(k == 0), stop=(k == KT - 1)),
                                 wait=w, inc=TE if k == KT - 1 else None)
                        if r is not None:
                            tt = r
                    dst = s_qf if part == "q" else s_kr[slot]
                    vv = P.op("vector", lambda e, b=b, hp=hp, dst=dst: e.tensor_copy(
                        out=dst[:, hp, :], in_=p_mm[b][:]),
                        wait=[(TE, tt)] + ([qf_guard] if (part == "q" and hp == 0) else []),
                        inc=V)
                    mm_free[b] = (V, vv)
            for i in range(4):
                for n in range(2):
                    b = mm_bank()
                    tt = None
                    for k in range(KT):
                        w = [xn1T_done, mm_free[b]] if k == 0 else []
                        r = P.op("tensor", lambda e, i=i, n=n, k=k, b=b: e.matmul(
                            out=p_mm[b][:],
                            lhsT=s_xn1T[:, k, i * 128:(i + 1) * 128],
                            rhs=s_w1[:, k, 2 * D + n * CH:2 * D + (n + 1) * CH],
                            start=(k == 0), stop=(k == KT - 1)),
                            wait=w, inc=TE if k == KT - 1 else None)
                        if r is not None:
                            tt = r
                    w = [(TE, tt)]
                    if i == 0 and n == 0:
                        w.append(ring_guard[slot])
                    vv = P.op("vector", lambda e, i=i, n=n, b=b, slot=slot:
                              e.tensor_copy(
                        out=s_vr[slot][:, i, 8 * n:8 * (n + 1), 0:HD],
                        in_=p_mm[b][:].rearrange("p (h d) -> p h d", d=HD)),
                        wait=w, inc=V)
                    mm_free[b] = (V, vv)
            kv_done = (V, P.count["V"])

            if c >= 1:
                for h in range(NH):
                    hp, half = h // 2, h % 2
                    r0 = 64 * half
                    ctx_started = False
                    for js in range(8):
                        ring = (c - 1 + js // 4) % 2
                        j = js % 4
                        eslot = js % 2
                        tt = P.op("tensor", lambda e, hp=hp, r0=r0, j=j,
                                  ring=ring, eslot=eslot: e.matmul(
                            out=p_S[eslot][:],
                            lhsT=s_kr[ring][r0:r0 + 64, hp, j * 128:(j + 1) * 128],
                            rhs=s_qf[r0:r0 + 64, hp, :],
                            start=True, stop=True),
                            wait=[S_free[eslot], kv_done] if js == 0 else [S_free[eslot]],
                            inc=TE)
                        wv = [(TE, tt)]
                        if eSb_guard[eslot] > 0:
                            wv.append((SAV, eSb_guard[eslot]))
                        masked = (js < 4 and c == 1) or js >= 4
                        if js < 4 and c == 1:
                            se = P.op("scalar", lambda e, eslot=eslot: e.activation(
                                out=s_eSf[:], in_=p_S[eslot][:], func=AF.Exp,
                                scale=0.125), wait=wv, inc=S)
                            S_free[eslot] = (S, se)
                            vv = P.op("vector", lambda e, eslot=eslot: e.tensor_scalar(
                                out=s_eSb[:, eslot, :], in0=s_eSf[:],
                                scalar1=s_flag[:, 0:1], scalar2=None,
                                op0=ALU.mult), wait=[(S, se)], inc=V)
                        elif js >= 4:
                            se = P.op("scalar", lambda e, eslot=eslot: e.activation(
                                out=s_eSf[:], in_=p_S[eslot][:], func=AF.Exp,
                                scale=0.125), wait=wv, inc=S)
                            S_free[eslot] = (S, se)
                            vv = P.op("vector", lambda e, eslot=eslot, j=j:
                                      e.tensor_tensor(
                                out=s_eSb[:, eslot, :], in0=s_eSf[:],
                                in1=s_tri[:, j, :], op=ALU.mult),
                                wait=[(S, se)], inc=V)
                        else:
                            se = P.op("scalar", lambda e, eslot=eslot: e.activation(
                                out=s_eSb[:, eslot, :], in_=p_S[eslot][:], func=AF.Exp,
                                scale=0.125), wait=wv, inc=S)
                            S_free[eslot] = (S, se)
                            vv = P.op("vector", lambda e: e.nop(),
                                      wait=[(S, se)], inc=V)
                        av = P.op("tensor", lambda e, h=h, j=j, ring=ring,
                                  eslot=eslot, st=(not ctx_started),
                                  sp=(js == 7): e.matmul(
                            out=p_ctx[:], lhsT=s_vr[ring][:, j, h, :],
                            rhs=s_eSb[:, eslot, :], start=st, stop=sp),
                            wait=[(V, vv)] + ([ctx_free] if not ctx_started else []),
                            inc=SAV)
                        eSb_guard[eslot] = av
                        ctx_started = True
                    ctx_t = (SAV, P.count["SAV"])
                    vv = P.op("vector", lambda e: e.tensor_copy(
                        out=s_dent[:], in_=p_ctx[HD:HD + 1, :]),
                        wait=[ctx_t], inc=V)
                    sd_ = P.op("scalar", lambda e: e.activation(
                        out=s_dscr[:], in_=s_dent[:], func=AF.Ln),
                        wait=[(V, vv)], inc=S)
                    sd_ = P.op("scalar", lambda e: e.activation(
                        out=s_drec[:], in_=s_dscr[:], func=AF.Exp, scale=-1.0),
                        wait=[(S, sd_)], inc=S)
                    tb = P.op("tensor", lambda e: e.matmul(
                        out=p_bc[0:64, :], lhsT=s_ones64[:], rhs=s_drec[:],
                        start=True, stop=True), wait=[(S, sd_), bc_free],
                        inc=TE, drain_inc=True)
                    vv = P.op("vector", lambda e: e.tensor_copy(
                        out=s_denb[:], in_=p_bc[0:64, :]), wait=[(TE, tb)], inc=V)
                    bc_free = (V, vv)
                    wat = [(V, vv)]
                    if h == 0:
                        wat.append(attn_guard)
                    av2 = P.op("vector", lambda e, hp=hp, r0=r0: e.tensor_tensor(
                        out=s_attn[r0:r0 + 64, hp, :], in0=p_ctx[0:HD, :],
                        in1=s_denb[:], op=ALU.mult), wait=wat, inc=V)
                    ctx_free = (V, av2)
                attn_done = (V, P.count["V"])
                last_ev = None
                for i in range(4):
                    for n in range(2):
                        b = mm_bank()
                        tt = None
                        for k in range(KT):
                            w = [attn_done, mm_free[b]] if k == 0 else []
                            r = P.op("tensor", lambda e, i=i, n=n, k=k, b=b: e.matmul(
                                out=p_mm[b][:],
                                lhsT=s_attn[:, k, i * 128:(i + 1) * 128],
                                rhs=s_wo1[:, k, n * CH:(n + 1) * CH],
                                start=(k == 0), stop=(k == KT - 1)),
                                wait=w, inc=TE if k == KT - 1 else None)
                            if r is not None:
                                tt = r
                        last_ev = P.op("vector", lambda e, i=i, n=n, b=b:
                                       e.tensor_copy(
                            out=s_x1[:, i, n * CH:(n + 1) * CH],
                            in_=p_mm[b][:]), wait=[(TE, tt)], inc=V)
                        mm_free[b] = (V, last_ev)
                attn_guard = (TE, P.count["TE"])
                dd = P.dma("sync", rearr_tok(ao1_sc, cq * CH, CH), s_x1[:],
                           wait=[(V, last_ev)], inc=DO)
                last_ao_dma = (DO, dd)

                for i in range(4):
                    lt = None
                    for j in range(KT):
                        w = [ln_s] if j == 0 else []
                        if j == 0:
                            w.append(tf_free)
                        if j >= 1:
                            w.append((TRC, lt))
                        ttf = P.op("tensor", lambda e, i=i, j=j: e.transpose(
                            out=p_tf[:], in_=s_xn1[:, i, j * 128:(j + 1) * 128],
                            identity=s_idf[:]), wait=w, inc=TE, drain_inc=True)
                        lt = P.op("vector", lambda e, j=j: e.tensor_copy(
                            out=s_xTf[:, j, :], in_=p_tf[:]),
                            wait=[(TE, ttf)], inc=TRC)
                    tf_free = (TRC, lt)
                    tl = None
                    for k in range(KT):
                        w = [(TRC, lt), bc_free] if k == 0 else []
                        r = P.op("tensor", lambda e, k=k: e.matmul(
                            out=p_bc[:, 0:16], lhsT=s_xTf[:, k, :],
                            rhs=s_gwT[:, k, :], start=(k == 0),
                            stop=(k == KT - 1)),
                            wait=w, inc=TE if k == KT - 1 else None)
                        if r is not None:
                            tl = r
                    tg2 = None
                    for k in range(KT):
                        r = P.op("tensor", lambda e, k=k: e.matmul(
                            out=p_bc[:, 16:17], lhsT=s_xTf[:, k, :],
                            rhs=s_sgwT[:, k:k + 1], start=(k == 0),
                            stop=(k == KT - 1)),
                            inc=TE if k == KT - 1 else None, drain_inc=True)
                        if r is not None:
                            tg2 = r
                    vv = P.op("vector", lambda e: e.tensor_copy(
                        out=s_pr[:], in_=p_bc[:, 0:16]), wait=[(TE, tl)], inc=V)
                    vv = P.op("vector", lambda e: e.tensor_reduce(
                        out=s_pc[:, 0:1], in_=s_pr[:], axis=AX.X, op=ALU.max),
                        wait=[(V, vv)], inc=V)
                    vv = P.op("vector", lambda e: e.tensor_scalar(
                        out=s_pc[:, 8:9], in0=s_pc[:, 0:1], scalar1=-1.0,
                        scalar2=None, op0=ALU.mult), wait=[(V, vv)], inc=V)
                    sv = P.op("scalar", lambda e: e.activation(
                        out=s_pr[:], in_=s_pr[:], func=AF.Exp,
                        bias=s_pc[:, 8:9]), wait=[(V, vv)], inc=S)
                    vv = P.op("vector", lambda e: e.tensor_reduce(
                        out=s_pc[:, 1:2], in_=s_pr[:], axis=AX.X, op=ALU.add),
                        wait=[(S, sv)], inc=V)
                    vv = P.recip(s_pc[:, 2:3], s_pc[:, 1:2], s_pc[:, 9:10],
                                 wait=[(V, vv)], inc=V)
                    vv = P.op("vector", lambda e: e.tensor_scalar(
                        out=s_pr[:], in0=s_pr[:], scalar1=s_pc[:, 2:3],
                        scalar2=None, op0=ALU.mult), wait=[(V, vv)], inc=V)
                    vv = P.op("vector", lambda e: e.tensor_reduce(
                        out=s_pc[:, 3:4], in_=s_pr[:], axis=AX.X, op=ALU.max),
                        wait=[(V, vv)], inc=V)
                    vv = P.op("vector", lambda e: e.tensor_scalar(
                        out=s_cmb[:], in0=s_pr[:], scalar1=s_pc[:, 3:4],
                        scalar2=None, op0=ALU.is_equal), wait=[(V, vv)], inc=V)
                    vv = P.op("vector", lambda e: e.scalar_tensor_tensor(
                        out=s_pr[:], in0=s_cmb[:], scalar=-2.0, in1=s_pr[:],
                        op0=ALU.mult, op1=ALU.add), wait=[(V, vv)], inc=V)
                    vv = P.op("vector", lambda e: e.tensor_reduce(
                        out=s_pc[:, 4:5], in_=s_pr[:], axis=AX.X, op=ALU.max),
                        wait=[(V, vv)], inc=V)
                    vv = P.op("vector", lambda e: e.tensor_add(
                        out=s_pc[:, 5:6], in0=s_pc[:, 3:4], in1=s_pc[:, 4:5]),
                        wait=[(V, vv)], inc=V)
                    vv = P.recip(s_pc[:, 6:7], s_pc[:, 5:6], s_pc[:, 9:10],
                                 wait=[(V, vv)], inc=V)
                    vv = P.op("vector", lambda e: e.tensor_mul(
                        out=s_pc[:, 7:8], in0=s_pc[:, 4:5], in1=s_pc[:, 6:7]),
                        wait=[(V, vv)], inc=V)
                    vv = P.op("vector", lambda e: e.tensor_mul(
                        out=s_pc[:, 6:7], in0=s_pc[:, 3:4], in1=s_pc[:, 6:7]),
                        wait=[(V, vv)], inc=V)
                    vv = P.op("vector", lambda e: e.tensor_scalar(
                        out=s_cmb[:], in0=s_cmb[:], scalar1=s_pc[:, 6:7],
                        scalar2=None, op0=ALU.mult), wait=[(V, vv)], inc=V)
                    vv = P.op("vector", lambda e: e.tensor_scalar(
                        out=s_pr[:], in0=s_pr[:], scalar1=s_pc[:, 4:5],
                        scalar2=None, op0=ALU.is_equal), wait=[(V, vv)], inc=V)
                    vv = P.op("vector", lambda e: e.scalar_tensor_tensor(
                        out=s_cmb[:], in0=s_pr[:], scalar=s_pc[:, 7:8],
                        in1=s_cmb[:], op0=ALU.mult, op1=ALU.add),
                        wait=[(V, vv)], inc=V)
                    ttf = P.op("tensor", lambda e: e.transpose(
                        out=p_tf[0:16, :], in_=s_cmb[:], identity=s_idf[:]),
                        wait=[(V, vv), tf_free], inc=TE, drain_inc=True)
                    vv = P.op("vector", lambda e: e.tensor_copy(
                        out=s_cmbT[:], in_=p_tf[0:16, :]),
                        wait=[(TE, ttf), last_cmb_dma], inc=V)
                    tf_free = (V, vv)
                    dd1 = P.dma("sync", cmbT_sc[cq, i], s_cmbT[:],
                                wait=[(V, vv)], inc=DO)
                    last_cmb_dma = (DO, dd1)
                    sv = P.op("scalar", lambda e, i=i: e.activation(
                        out=s_gq[:, i:i + 1], in_=p_bc[:, 16:17], func=AF.Exp,
                        scale=-1.0), wait=[(TE, tg2)], inc=S)
                    vv = P.op("vector", lambda e, i=i: e.tensor_scalar(
                        out=s_gq[:, i:i + 1], in0=s_gq[:, i:i + 1],
                        scalar1=1.0, scalar2=None, op0=ALU.add),
                        wait=[(S, sv)], inc=V)
                    vv = P.op("vector", lambda e, i=i: e.reciprocal(
                        out=s_gq[:, i:i + 1], in_=s_gq[:, i:i + 1]),
                        wait=[(V, vv)], inc=V)
                    bc_free = (V, vv)
                dd2 = P.dma("sync", gate_sc[cq], s_gq[:],
                            wait=[(V, P.count["V"])], inc=DO)
            else:
                attn_done = (V, P.count["V"])
            ring_guard[(c - 1) % 2] = (TE, P.count["TE"])
            qf_guard = (TE, P.count["TE"])
            state["v"] = P.count["V"]
        if standalone:
            P.op("sync", lambda e: e.nop(), wait=[(DO, P.count["DO"])])
            P.build(es)
        scr2 = dict(scr or {})
        scr2.update(ao1_sc=ao1_sc, xn1T_sc=xn1T_sc, cmbT_sc=cmbT_sc,
                    gate_sc=gate_sc)
    return nc if standalone else scr2


def stage_c_inputs(inputs, core, n_own=9):
    import ml_dtypes
    KTC = 8
    flagv = np.full((128, 1), float(core % 2), np.float32)
    tri = np.zeros((128, 4, CH), np.float32)
    ar = np.arange(CH)
    for j in range(4):
        for p in range(128):
            tri[p, j, :] = (j * 128 + p <= ar)
    return dict(
        w1_in=np.asarray(inputs["L1_attn_w_qkv"], np.float32),
        wo1_in=np.asarray(inputs["L1_attn_w_out"], np.float32),
        gwT_in=np.ascontiguousarray(np.asarray(inputs["L1_ffn_gate_w"], np.float32).T),
        sgwT_in=np.ascontiguousarray(np.asarray(inputs["L1_ffn_shared_gate_w"], np.float32).reshape(KTC, 128).T),
        flag=flagv, tri=tri,
        ident_b=np.eye(128, dtype=np.float32).astype(ml_dtypes.bfloat16),
        ident_f=np.eye(128, dtype=np.float32))




def build_phase_d1(n_own=9, chain="ve", ctx=None, scr=None):
    standalone = ctx is None
    if standalone:
        nc = bass.Bass("TRN2", target_bir_lowering=False, debug=False)
    else:
        nc, P = ctx
    NQ = n_own - 1
    TQ = NQ * CH
    I = 2816
    IT = I // 128
    if standalone:
        xn1T_in = nc.dram_tensor("xn1T_in", [NQ, 128, KT, CH], BF16, kind="ExternalInput").ap()
        x1_in = nc.dram_tensor("x1_in", [n_own * CH, D], F32, kind="ExternalInput").ap()
        ao1_in = nc.dram_tensor("ao1_in", [TQ, D], F32, kind="ExternalInput").ap()
        gate_in = nc.dram_tensor("gate_in", [NQ, 128, 4], F32, kind="ExternalInput").ap()
    else:
        xn1T_in, x1_in = scr["xn1T_sc"], scr["x1_sc"]
        ao1_in, gate_in = scr["ao1_sc"], scr["gate_sc"]
    swgT_in = nc.dram_tensor("swgT_in", [D, I], F32, kind="ExternalInput").ap()
    swuT_in = nc.dram_tensor("swuT_in", [D, I], F32, kind="ExternalInput").ap()
    swdT_in = nc.dram_tensor("swdT_in", [I, D], F32, kind="ExternalInput").ap()
    po_sc = nc.dram_tensor(
        "po_sc", [TQ, D], F32,
        kind="ExternalOutput" if standalone else "Internal").ap()

    if standalone:
        P = Prog(nc, chain=chain)
    es = ExitStack()
    with es:
        sb = lambda n, s, d=F32: es.enter_context(nc.sbuf_tensor('pd_' + n, s, d))
        ps = lambda n, s, d=F32: es.enter_context(nc.psum_tensor('pd_' + n, s, d))
        s_swg = sb("s_swg", [128, KT, I], BF16)
        s_swu = sb("s_swu", [128, KT, I], BF16)
        s_swd = sb("s_swd", [128, IT, D], BF16)
        s_xT = sb("s_xT", [128, KT, CH], BF16)
        s_sh = sb("s_sh", [128, IT, CH], BF16)
        s_sg = sb("s_sg", [128, CH])
        s_x1 = sb("s_x1", [128, 4, D])
        s_ao = sb("s_ao", [128, 4, D])
        s_gt = sb("s_gt", [128, 4])
        p_g = ps("p_g", [128, CH])
        p_u = ps("p_u", [128, CH])
        p_fo = ps("p_fo", [128, CH])

        LDC = P.sem("LDC_D1")
        V, S, TE, DO = P.sem("V"), P.sem("S"), P.sem("TE"), P.sem("DO")
        DX = P.sem("DX_D1")
        dxc = [0]

        P.dma("gpsimd", s_swg[:], swgT_in.rearrange("(k p) n -> p k n", p=128), inc=LDC)
        P.dma("gpsimd", s_swu[:], swuT_in.rearrange("(k p) n -> p k n", p=128), inc=LDC)
        P.dma("gpsimd", s_swd[:], swdT_in.rearrange("(k p) n -> p k n", p=128), inc=LDC)
        LDA = [(LDC, 48)]

        mm_free_g = (S, 0)
        mm_free_u = (V, 0)
        fo_free = (V, 0)
        last_po_dma = (DO, 0)
        state_v = 0
        state_te = 0
        for cq in range(NQ):
            wl = [(TE, state_te), (V, state_v)] if cq else []
            dxc[0] += 16
            P.dma("sync", s_xT[:], xn1T_in[cq], wait=wl, inc=DX)
            dxc[0] += 16
            P.dma("sync", s_x1[:], rearr_tok(x1_in, (cq + 1) * CH, CH), inc=DX)
            dxc[0] += 16
            P.dma("sync", s_ao[:], rearr_tok(ao1_in, cq * CH, CH), inc=DX)
            dxc[0] += 16
            P.dma("sync", s_gt[:], gate_in[cq], inc=DX)
            ldx = (DX, dxc[0])
            for m in range(IT):
                tg = tu = None
                for k in range(KT):
                    w = ([ldx] + LDA if (cq == 0 and m == 0) else [ldx]) if k == 0 else []
                    if k == 0:
                        w = w + [mm_free_g]
                    r = P.op("tensor", lambda e, k=k, m=m: e.matmul(
                        out=p_g[:], lhsT=s_swg[:, k, m * 128:(m + 1) * 128],
                        rhs=s_xT[:, k, :], start=(k == 0), stop=(k == KT - 1)),
                        wait=w, inc=TE if k == KT - 1 else None)
                    if r is not None:
                        tg = r
                for k in range(KT):
                    w = [mm_free_u] if k == 0 else []
                    r = P.op("tensor", lambda e, k=k, m=m: e.matmul(
                        out=p_u[:], lhsT=s_swu[:, k, m * 128:(m + 1) * 128],
                        rhs=s_xT[:, k, :], start=(k == 0), stop=(k == KT - 1)),
                        wait=w, inc=TE if k == KT - 1 else None)
                    if r is not None:
                        tu = r
                ssg = P.op("scalar", lambda e: e.activation(
                    out=s_sg[:], in_=p_g[:], func=AF.Silu),
                    wait=[(TE, tg), mm_free_u], inc=S)
                mm_free_g = (S, ssg)
                vh = P.op("vector", lambda e, m=m: e.tensor_tensor(
                    out=s_sh[:, m, :], in0=p_u[:], in1=s_sg[:],
                    op=ALU.mult), wait=[(S, ssg), (TE, tu)], inc=V)
                mm_free_u = (V, vh)
            sh_done = (V, P.count["V"])
            lastv = None
            for i in range(4):
                for n2 in range(2):
                    tf = None
                    for k in range(IT):
                        w = [sh_done, fo_free] if k == 0 else []
                        r = P.op("tensor", lambda e, k=k, i=i, n2=n2: e.matmul(
                            out=p_fo[:],
                            lhsT=s_sh[:, k, i * 128:(i + 1) * 128],
                            rhs=s_swd[:, k, n2 * CH:(n2 + 1) * CH],
                            start=(k == 0), stop=(k == IT - 1)),
                            wait=w, inc=TE if k == IT - 1 else None)
                        if r is not None:
                            tf = r
                    vv = P.op("vector", lambda e, i=i, n2=n2: e.tensor_scalar(
                        out=s_sg[:], in0=p_fo[:], scalar1=s_gt[:, i:i + 1],
                        scalar2=None, op0=ALU.mult),
                        wait=[(TE, tf)], inc=V)
                    fo_free = (V, vv)
                    vv = P.op("vector", lambda e, i=i, n2=n2: e.tensor_tensor(
                        out=s_ao[:, i, n2 * CH:(n2 + 1) * CH], in0=s_sg[:],
                        in1=s_ao[:, i, n2 * CH:(n2 + 1) * CH], op=ALU.add),
                        wait=[last_po_dma] if (i == 0 and n2 == 0) else [], inc=V)
                    lastv = P.op("vector", lambda e, i=i, n2=n2: e.tensor_add(
                        out=s_ao[:, i, n2 * CH:(n2 + 1) * CH],
                        in0=s_ao[:, i, n2 * CH:(n2 + 1) * CH],
                        in1=s_x1[:, i, n2 * CH:(n2 + 1) * CH]), inc=V)
            dd = P.dma("sync", rearr_tok(po_sc, cq * CH, CH), s_ao[:],
                       wait=[(V, lastv)], inc=DO)
            last_po_dma = (DO, dd)
            state_v = P.count["V"]
            state_te = P.count["TE"]
        if standalone:
            P.op("sync", lambda e: e.nop(), wait=[(DO, P.count["DO"])])
            P.build(es)
        scr2 = dict(scr or {})
        scr2["po_sc"] = po_sc
    return nc if standalone else scr2


def build_phase_d2(n_own=9, chain="ve", ctx=None, scr=None):
    standalone = ctx is None
    if standalone:
        nc = bass.Bass("TRN2", target_bir_lowering=False, debug=False)
    else:
        nc, P = ctx
    NQ = n_own - 1
    TQ = NQ * CH
    NHALF = NQ // 2
    if standalone:
        xn1T_in = nc.dram_tensor("xn1T_in", [NQ, 128, KT, CH], BF16, kind="ExternalInput").ap()
        cmbT_in = nc.dram_tensor("cmbT_in", [NQ, 4, 16, 128], F32, kind="ExternalInput").ap()
        po_in = nc.dram_tensor("po_in", [TQ, D], F32, kind="ExternalInput").ap()
    else:
        xn1T_in, cmbT_in, po_in = scr["xn1T_sc"], scr["cmbT_sc"], scr["po_sc"]
    wgu_in = nc.dram_tensor("wgu16_in", [16, D, D], F32, kind="ExternalInput").ap()
    wd_in = nc.dram_tensor("wd16_in", [16, CH, D], F32, kind="ExternalInput").ap()
    out_sc = nc.dram_tensor("out_sc", [TQ, D], F32, kind="ExternalOutput").ap()

    if standalone:
        P = Prog(nc, chain=chain)
    es = ExitStack()
    with es:
        sb = lambda n, s, d=F32: es.enter_context(nc.sbuf_tensor('pe_' + n, s, d))
        ps = lambda n, s, d=F32: es.enter_context(nc.psum_tensor('pe_' + n, s, d))
        s_xh = sb("s_xh", [128, KT, NHALF * CH], BF16)
        s_acc = sb("s_acc", [128, NHALF * 4, D])
        s_wgu = [sb("s_wgu0", [128, KT, D], BF16), sb("s_wgu1", [128, KT, D], BF16)]
        s_wd = [sb("s_wd0", [128, 4, D], BF16), sb("s_wd1", [128, 4, D], BF16)]
        s_ce = sb("s_ce", [1, NHALF * CH])
        s_cb = sb("s_cb", [128, CH])
        s_sg = sb("s_sg", [128, 2, CH])
        s_hec = sb("s_hec", [128, 4, CH], BF16)
        s_ones = sb("s_ones1", [1, 128])
        s_po = sb("s_po", [128, 4, D])
        p_g = [ps("p_g0", [128, CH]), ps("p_g1", [128, CH])]
        p_u = [ps("p_u0", [128, CH]), ps("p_u1", [128, CH])]
        p_cb = ps("p_cb", [128, CH])
        p_o = [ps("p_o0", [128, CH]), ps("p_o1", [128, CH])]

        V, S, TE, DO = P.sem("V"), P.sem("S"), P.sem("TE"), P.sem("DO")
        DX = P.sem("DX_D2")
        DW = [P.sem("DW0_D2"), P.sem("DW1_D2")]
        DC = P.sem("DC_D2")
        dxc, dwc, dcc = [0], [0, 0], [0]
        state_v = 0
        state_te = 0
        mm_free_g = [(S, 0), (S, 0)]
        mm_free_u = [(V, 0), (V, 0)]
        cb_free = (V, 0)
        sg_free = [(V, 0), (V, 0)]
        o_free = [(V, 0), (V, 0)]
        hec_guard = (TE, 0)
        last_out_dma = (DO, 0)

        vo = P.op("vector", lambda e: e.memset(s_ones[:], 1.0), inc=V)
        ones_ready = (V, vo)

        for h2 in range(2):
            w = [(TE, state_te)] if h2 else []
            for q in range(NHALF):
                dxc[0] += 16
                P.dma("sync", s_xh[:, :, q * CH:(q + 1) * CH],
                      xn1T_in[h2 * NHALF + q], wait=w if q == 0 else [],
                      inc=DX)
            ldx = (DX, dxc[0])
            vz = P.op("vector", lambda e: e.memset(s_acc[:], 0.0),
                      wait=[(V, state_v)], inc=V)
            acc_ready = (V, vz)
            for e_ in range(16):
                buf = e_ % 2
                wld = [hec_guard] if (e_ >= 2 or h2) else []
                dwc[buf] += 16
                P.dma("gpsimd", s_wgu[buf][:],
                      wgu_in[e_].rearrange("(k p) n -> p k n", p=128),
                      wait=wld, inc=DW[buf])
                dwc[buf] += 16
                P.dma("gpsimd", s_wd[buf][:],
                      wd_in[e_].rearrange("(k p) n -> p k n", p=128),
                      inc=DW[buf])
                ldw = (DW[buf], dwc[buf])
                dcc[0] += 16
                P.dma("sync",
                      s_ce[:].rearrange("o (q i p) -> o q i p", q=NHALF, i=4),
                      cmbT_in[h2 * NHALF:(h2 + 1) * NHALF, :, e_, :].unsqueeze(0),
                      wait=[cb_free], inc=DC)
                ldc = (DC, dcc[0])
                for tg in range(NHALF):
                    tb = P.op("tensor", lambda e, tg=tg: e.matmul(
                        out=p_cb[:], lhsT=s_ones[:],
                        rhs=s_ce[0:1, tg * CH:(tg + 1) * CH],
                        start=True, stop=True),
                        wait=[ldc, cb_free, ones_ready], inc=TE,
                        drain_inc=True)
                    vcb = P.op("vector", lambda e: e.tensor_copy(
                        out=s_cb[:], in_=p_cb[:]), wait=[(TE, tb)], inc=V)
                    cb_free = (V, vcb)
                    for mg in range(4):
                        gb = mg % 2
                        tgm = tum = None
                        for k in range(KT):
                            w = [ldx, ldw, mm_free_g[gb]] if k == 0 else []
                            r = P.op("tensor", lambda e, k=k, mg=mg, tg=tg, buf=buf,
                                     gb=gb: e.matmul(
                                out=p_g[gb][:],
                                lhsT=s_wgu[buf][:, k, mg * 128:(mg + 1) * 128],
                                rhs=s_xh[:, k, tg * CH:(tg + 1) * CH],
                                start=(k == 0), stop=(k == KT - 1)),
                                wait=w, inc=TE if k == KT - 1 else None)
                            if r is not None:
                                tgm = r
                        for k in range(KT):
                            w = [mm_free_u[gb]] if k == 0 else []
                            r = P.op("tensor", lambda e, k=k, mg=mg, tg=tg, buf=buf,
                                     gb=gb: e.matmul(
                                out=p_u[gb][:],
                                lhsT=s_wgu[buf][:, k, CH + mg * 128:CH + (mg + 1) * 128],
                                rhs=s_xh[:, k, tg * CH:(tg + 1) * CH],
                                start=(k == 0), stop=(k == KT - 1)),
                                wait=w, inc=TE if k == KT - 1 else None)
                            if r is not None:
                                tum = r
                        ssg = P.op("scalar", lambda e, gb=gb, mg=mg: e.activation(
                            out=s_sg[:, mg % 2, :], in_=p_g[gb][:], func=AF.Silu),
                            wait=[(TE, tgm), sg_free[mg % 2]], inc=S)
                        mm_free_g[gb] = (S, ssg)
                        vh = P.op("vector", lambda e, gb=gb, mg=mg: e.tensor_tensor(
                            out=s_sg[:, mg % 2, :], in0=p_u[gb][:],
                            in1=s_sg[:, mg % 2, :],
                            op=ALU.mult), wait=[(S, ssg), (TE, tum)], inc=V)
                        mm_free_u[gb] = (V, vh)
                        vh = P.op("vector", lambda e, mg=mg: e.tensor_tensor(
                            out=s_hec[:, mg, :], in0=s_sg[:, mg % 2, :],
                            in1=s_cb[:], op=ALU.mult),
                            wait=[(V, vcb)] + ([hec_guard] if mg == 0 else []),
                            inc=V)
                        sg_free[mg % 2] = (V, vh)
                    hec_done = (V, P.count["V"])
                    for tt in range(4):
                        for n2 in range(2):
                            ob = (tt * 2 + n2) % 2
                            tf = None
                            for k in range(4):
                                w = [hec_done, o_free[ob]] if k == 0 else []
                                r = P.op("tensor", lambda e, k=k, tt=tt, n2=n2,
                                         ob=ob, buf=buf: e.matmul(
                                    out=p_o[ob][:],
                                    lhsT=s_hec[:, k, tt * 128:(tt + 1) * 128],
                                    rhs=s_wd[buf][:, k, n2 * CH:(n2 + 1) * CH],
                                    start=(k == 0), stop=(k == 3)),
                                    wait=w, inc=TE if k == 3 else None)
                                if r is not None:
                                    tf = r
                            vv = P.op("vector", lambda e, tg=tg, tt=tt, n2=n2,
                                      ob=ob: e.tensor_tensor(
                                out=s_acc[:, tg * 4 + tt, n2 * CH:(n2 + 1) * CH],
                                in0=p_o[ob][:],
                                in1=s_acc[:, tg * 4 + tt, n2 * CH:(n2 + 1) * CH],
                                op=ALU.add),
                                wait=[(TE, tf)] + ([acc_ready] if (e_ == 0 and tg == 0 and tt == 0 and n2 == 0) else []),
                                inc=V)
                            o_free[ob] = (V, vv)
                    hec_guard = (TE, P.count["TE"])
            acc_done = (V, P.count["V"])
            for q in range(NHALF):
                dxc[0] += 16
                dpo = P.dma("sync", s_po[:],
                            rearr_tok(po_in, (h2 * NHALF + q) * CH, CH),
                            wait=[(V, state_v)] if False else [], inc=DX)
                lastv = None
                for i in range(4):
                    lastv = P.op("vector", lambda e, q=q, i=i: e.tensor_tensor(
                        out=s_po[:, i, :], in0=s_po[:, i, :],
                        in1=s_acc[:, q * 4 + i, :], op=ALU.add),
                        wait=[(DX, dxc[0]), acc_done] if i == 0 else [],
                        inc=V)
                dd = P.dma("sync", rearr_tok(out_sc, (h2 * NHALF + q) * CH, CH),
                           s_po[:], wait=[(V, lastv)], inc=DO)
                last_out_dma = (DO, dd)
            state_v = P.count["V"]
            state_te = P.count["TE"]
        if standalone:
            P.op("sync", lambda e: e.nop(), wait=[(DO, P.count["DO"])])
            P.build(es)
    return nc


def _assemble_fused(n_pre, n_own, chain, plan=None):
    nc = bass.Bass("TRN2", target_bir_lowering=False, debug=False)
    P = Prog(nc, chain=chain)
    P.plan = plan
    ctx = (nc, P)
    scr = build_phase_a(n_pre=n_pre, n_own=n_own, ctx=ctx)
    P.barrier()
    scr = build_phase_b(n_own=n_own, ctx=ctx, scr=scr)
    P.barrier()
    scr = build_phase_c(n_own=n_own, ctx=ctx, scr=scr)
    P.barrier()
    scr = build_phase_d1(n_own=n_own, ctx=ctx, scr=scr)
    P.barrier()
    build_phase_d2(n_own=n_own, ctx=ctx, scr=scr)
    P.op("sync", lambda e: e.nop(),
         wait=[(P.sem("DO"), P.count["DO"]),
               (P.sem("DO_B"), P.count["DO_B"])])
    es = ExitStack()
    with es:
        P.build(es)
    return nc, P


def build_fused(n_pre=7, n_own=9, chain="sel"):
    if chain == "sel":
        _, P1 = _assemble_fused(n_pre, n_own, "analyze")
        plan = P1.compute_plan()
        nc, P = _assemble_fused(n_pre, n_own, "sel", plan=plan)
        P.plan_stats = P1.plan_stats
        return nc
    nc, _ = _assemble_fused(n_pre, n_own, chain)
    return nc


def _stage_d1_inputs(inputs):
    return dict(
        swgT_in=np.ascontiguousarray(np.asarray(inputs["L1_ffn_shared_w_gate"], np.float32).T),
        swuT_in=np.ascontiguousarray(np.asarray(inputs["L1_ffn_shared_w_up"], np.float32).T),
        swdT_in=np.ascontiguousarray(np.asarray(inputs["L1_ffn_shared_w_down"], np.float32).T))


def _stage_fused(inputs, core):
    m = {}
    m.update(stage_a_inputs(inputs, core))
    m.update(stage_b_inputs(inputs, core))
    mc = stage_c_inputs(inputs, core)
    for k in ("w1_in", "wo1_in", "gwT_in", "sgwT_in", "ident_f"):
        m[k] = mc[k]
    m.update(_stage_d1_inputs(inputs))
    m["wgu16_in"] = np.asarray(inputs["L1_ffn_w_gate_up"], np.float32)
    m["wd16_in"] = np.asarray(inputs["L1_ffn_w_down"], np.float32)
    return m


def kernel(**inputs):
    cores = list(range(NCORES))
    nc = build_fused()
    im = [_stage_fused(inputs, c) for c in cores]
    res = run_bass_kernel_spmd(nc, im, core_ids=cores).results
    out = np.zeros((4, 8192, D), np.float32)
    for c in cores:
        b, half = c // 2, c % 2
        out[b, half * 4096:(half + 1) * 4096] = res[c]["out_sc"]
    return out


# version 15
# speedup vs baseline: 1.1840x; 1.1840x over previous
import numpy as np
from contextlib import ExitStack

import concourse.bass as bass
import concourse.mybir as mybir
from concourse import dve_ops
from concourse.bass_utils import run_bass_kernel_spmd

F32 = mybir.dt.float32
BF16 = mybir.dt.bfloat16
AF = mybir.ActivationFunctionType
ALU = mybir.AluOpType
AX = mybir.AxisListType

NCORES = 8
D = 1024
CH = 512
NH = 16
HD = 64
KT = D // 128


_DT_SIZE = {"float32": 4, "int32": 4, "uint32": 4, "bfloat16": 2,
            "float16": 2, "int16": 2, "uint16": 2, "int8": 1, "uint8": 1,
            "float8_e4m3": 1, "float8_e5m2": 1, "float64": 8}


def _pap_span(pap):
    try:
        name = pap.memref
        esz = _DT_SIZE.get(str(pap.dtype).split(".")[-1], 8)
        off = int(pap.offset)
        pairs = [(int(p[0]), int(p[1])) for p in pap.ap]
        if not pairs:
            return (name, 0, 1 << 24, 0, 1 << 50, 0, 0, 0)
        ps, pn = pairs[0]
        rest = pairs[1:]
        neg = sum(min(0, s * (n - 1)) for s, n in rest)
        pos = sum(max(0, s * (n - 1)) for s, n in rest)
        if ps > 0:
            f0 = off % ps
            if f0 + neg >= 0 and f0 + pos < ps:
                p0 = off // ps
                st = re_ = w = 0
                if len(rest) == 1 and rest[0][0] > 1:
                    st = rest[0][0] * esz
                    re_ = (f0 * esz) % st
                    w = esz
                return (name, p0, p0 + pn - 1,
                        (f0 + neg) * esz, (f0 + pos + 1) * esz, st, re_, w)
        return (name, 0, 1 << 24, 0, 1 << 50, 0, 0, 0)
    except Exception:
        return None


def _op_spans(ins):
    try:
        bir = ins.ins
        opname = type(bir).__name__.lower()
        if "accum" in opname:
            return None
        rd, wr = [], []
        for a in bir.ins:
            if getattr(a, "memref", None) is None:
                continue
            s = _pap_span(a)
            if s is None:
                return None
            rd.append(s)
        for a in bir.outs:
            if getattr(a, "memref", None) is None:
                continue
            s = _pap_span(a)
            if s is None:
                return None
            wr.append(s)
        return (rd, wr, False)
    except Exception:
        return None


def _spans_overlap(spans, pending):
    for q in pending:
        if q is None:
            return bool(spans)
        for s in spans:
            if not (s[0] == q[0] and s[1] <= q[2] and q[1] <= s[2]
                    and s[3] < q[4] and q[3] < s[4]):
                continue
            if s[5] and s[5] == q[5]:
                st = s[5]
                d = (s[6] - q[6]) % st
                if d >= q[7] and st - d >= s[7]:
                    continue
            return True
    return False


class Prog:

    ENGINES = ("sync", "vector", "scalar", "tensor", "gpsimd")

    def __init__(self, nc, chain="ve"):
        self.nc = nc
        self.chain = chain
        self.ops = {e: [] for e in self.ENGINES}
        self.count = {}
        self.record = None
        self.plan = None
        self.plan_stats = None

    def sem(self, name):
        self.count.setdefault(name, 0)
        return name

    def op(self, engine, fn, wait=(), inc=None, inc_amt=1, is_dma=False,
           drain_inc=False, nodrain=False, bar=False):
        if inc is not None:
            self.sem(inc)
            self.count[inc] += inc_amt
        self.ops[engine].append(
            (fn, [(s, v) for (s, v) in wait if v > 0], inc, inc_amt,
             is_dma, drain_inc, nodrain,
             self.count[inc] if inc is not None else None, bar))
        return self.count[inc] if inc is not None else None

    def recip(self, out, in_, scratch, wait=(), inc=None):
        return self.op("vector", lambda e: e.reciprocal(out=out, in_=in_),
                       wait=wait, inc=inc)

    def barrier(self):
        snap = [(s, v) for s, v in self.count.items() if v > 0]
        for e in self.ENGINES:
            self.op(e, lambda en: en.nop(), wait=snap, bar=True)

    def dma(self, engine, out, in_, wait=(), inc=None):
        return self.op(engine, lambda e: e.dma_start(out=out, in_=in_),
                       wait=wait, inc=inc, inc_amt=16, is_dma=True)

    def build(self, es):
        drain_engines = ()
        if self.chain == "all":
            run = {n: 0 for n in self.count}
            for ename in self.ENGINES:
                self.sem("SELF_" + ename)
                run.setdefault("SELF_" + ename, 0)
                prev = None
                out = []
                for fn, waits, inc, amt, is_dma, di, nd, cum, bar in self.ops[ename]:
                    if inc is None:
                        inc = "SELF_" + ename
                        self.count[inc] = self.count.get(inc, 0) + 1
                        amt = 1
                    run[inc] = run.get(inc, 0) + amt
                    if prev is not None:
                        waits = waits + [prev]
                    out.append((fn, waits, inc, amt, is_dma, di, nd, cum, bar))
                    prev = (inc, run[inc])
                self.ops[ename] = out
        elif self.chain == "ve":
            drain_engines = ("vector", "scalar")
        analyze = self.chain == "analyze"
        sel = self.chain == "sel"
        if analyze:
            self.record = {e: [] for e in self.ENGINES}
        sems = {n: es.enter_context(self.nc.semaphore(n)) for n in self.count}
        block = es.enter_context(self.nc.Block())
        for ename in self.ENGINES:
            stream = self.ops[ename]
            use_drain = ename in drain_engines
            drain_always = use_drain and ename in ("vector", "scalar")

            chained = bool(self.chain) and self.chain != "all"

            def make(fstream, use_drain, drain_always, ename=ename):
                sel_ve = sel and ename in ("vector", "scalar")
                if sel_ve:
                    drain_before, drain_inc_set = self.plan[ename]
                rec = self.record[ename] if analyze else None

                def body(eng):
                    for i, (fn, waits, inc, amt, is_dma, di, nd, cum, bar) \
                            in enumerate(fstream):
                        if sel_ve and i in drain_before:
                            eng.drain(fusable=True)
                        for s, v in waits:
                            eng.wait_ge(sems[s], v)
                        ins = fn(eng)
                        if rec is not None:
                            rec.append(_op_spans(ins))
                        if is_dma:
                            if inc is not None:
                                ins.then_inc(sems[inc], amt)
                        elif sel_ve:
                            if inc is not None:
                                if i in drain_inc_set:
                                    eng.drain(fusable=True).then_inc(
                                        sems[inc], amt)
                                else:
                                    ins.then_inc(sems[inc], amt)
                        elif analyze:
                            if inc is not None:
                                ins.then_inc(sems[inc], amt)
                        elif nd and inc is None:
                            pass
                        elif use_drain and (drain_always or inc is not None):
                            d = eng.drain(fusable=True)
                            if inc is not None:
                                d.then_inc(sems[inc], amt)
                        elif chained and di and inc is not None:
                            eng.drain(fusable=False).then_inc(sems[inc], amt)
                        elif inc is not None:
                            ins.then_inc(sems[inc], amt)
                return body

            getattr(block, ename)(make(stream, use_drain, drain_always))

    def compute_plan(self):
        waited = {}
        inc_engines = {}
        for ename in self.ENGINES:
            for (fn, waits, inc, amt, is_dma, di, nd, cum, bar) in self.ops[ename]:
                for s, v in waits:
                    waited.setdefault(s, set()).add(v)
                if inc is not None:
                    inc_engines.setdefault(inc, set()).add(ename)
        mixed = {s for s, es_ in inc_engines.items() if len(es_) > 1}
        plan = {}
        stats = {}
        for ename in ("vector", "scalar"):
            rec = self.record[ename]
            ops = self.ops[ename]
            drain_before, drain_inc_set = set(), set()
            pending = []
            for i, (op_t, info) in enumerate(zip(ops, rec)):
                fn, waits, inc, amt, is_dma, di, nd, cum, bar = op_t
                if is_dma:
                    continue
                if bar:
                    if pending:
                        drain_before.add(i)
                        pending = []
                    continue
                if info is None:
                    if pending and not nd:
                        drain_before.add(i)
                    pending = [None]
                else:
                    rd, wr, uni = info
                    if uni:
                        if pending and not nd:
                            drain_before.add(i)
                        pending = [None]
                    else:
                        if not nd and _spans_overlap(rd + wr, pending):
                            drain_before.add(i)
                            pending = []
                        pending.extend(wr)
                if inc is not None:
                    drain_inc_set.add(i)
                    pending = []
            plan[ename] = (drain_before, drain_inc_set)
            stats[ename] = (len(ops), len(drain_before), len(drain_inc_set))
        self.plan_stats = stats
        return plan


def rearr_tok(dram_ap, t0, ntok):
    return dram_ap[t0:t0 + ntok, :].rearrange("(t p) d -> p t d", p=128)


def build_phase_a(n_pre=7, n_own=9, chain="ve", stages=6, ctx=None):
    standalone = ctx is None
    if standalone:
        nc = bass.Bass("TRN2", target_bir_lowering=False, debug=False)
        P = None
    else:
        nc, P = ctx
    T = (n_pre + n_own) * CH
    TO = n_own * CH

    xw = nc.dram_tensor("xw", [T, D], F32, kind="ExternalInput").ap()
    wqkvT = nc.dram_tensor("wqkvT", [D, 3 * D], F32, kind="ExternalInput").ap()
    bq_col = nc.dram_tensor("bq_col", [128, 24], F32, kind="ExternalInput").ap()
    bv_row = nc.dram_tensor("bv_row", [1, D], F32, kind="ExternalInput").ap()
    omega_bd = nc.dram_tensor("omega_bd", [8, 128, 128], F32, kind="ExternalInput").ap()
    quadv = nc.dram_tensor("quadv", [2, 1], F32, kind="ExternalInput").ap()
    quadw = nc.dram_tensor("quadw", [2, 1], F32, kind="ExternalInput").ap()
    selq = nc.dram_tensor("selq", [2, 128], F32, kind="ExternalInput").ap()
    sel2 = nc.dram_tensor("sel2", [2, 128], F32, kind="ExternalInput").ap()
    bd2 = nc.dram_tensor("bd2", [128, 2], F32, kind="ExternalInput").ap()
    flag = nc.dram_tensor("flag", [128, 1], F32, kind="ExternalInput").ap()
    flag_bf = nc.dram_tensor("flag_bf", [128, 1], BF16, kind="ExternalInput").ap()
    tri = nc.dram_tensor("tri", [128, 4, CH], F32, kind="ExternalInput").ap()
    ident_b = nc.dram_tensor("ident_b", [128, 128], BF16, kind="ExternalInput").ap()
    woutT = nc.dram_tensor("woutT", [D, D], F32, kind="ExternalInput").ap()
    bo_row = nc.dram_tensor("bo_row", [1, D], F32, kind="ExternalInput").ap()

    kind = "ExternalOutput" if standalone else "Internal"
    xn_sc = nc.dram_tensor("xn_sc", [TO, D], F32, kind=kind).ap()
    ao_sc = nc.dram_tensor("ao_sc", [TO, D], F32, kind=kind).ap()

    if standalone:
        P = Prog(nc, chain=chain)
    es = ExitStack()
    with es:
        sb = lambda n, s, d=F32: es.enter_context(nc.sbuf_tensor('pa_' + n, s, d))
        ps = lambda n, s, d=F32: es.enter_context(nc.psum_tensor('pa_' + n, s, d))

        s_wqkv = sb("s_wqkv", [128, KT, 3 * D], BF16)
        s_bq = sb("s_bq", [128, 24])
        s_bv = sb("s_bv", [1, D], BF16)
        s_om = sb("s_om", [128, 8, 128], BF16)
        s_qv = sb("s_qv", [2, 1])
        s_qw = sb("s_qw", [2, 1])
        s_selq = sb("s_selq", [2, 128])
        s_sel2 = sb("s_sel2", [2, 128])
        s_bd2 = sb("s_bd2", [128, 2])
        s_flag = sb("s_flag", [128, 1])
        s_flagb = sb("s_flagb", [128, 1], BF16)
        s_tri = sb("s_tri", [128, 4, CH])
        s_idb = sb("s_idb", [128, 128], BF16)
        s_wout = sb("s_wout", [128, KT, D], BF16)
        s_bo = sb("s_bo", [1, D], BF16)
        s_ones_b = sb("s_ones_b", [1, CH], BF16)
        s_ones64 = sb("s_ones64", [1, 64])
        s_cols = sb("s_cols", [128, 3])
        s_c40 = sb("s_c40", [128, 1])
        s_cm20 = sb("s_cm20", [128, 1])
        s_cscr = sb("s_cscr", [2, 1])
        s_cscr2 = sb("s_cscr2", [2, 1])

        s_x = sb("s_x", [128, 4, D])
        s_xn = sb("s_xn", [128, 4, D])
        s_xnb = sb("s_xnb", [128, 4, D], BF16)
        s_lncol = sb("s_lncol", [128, 8])
        s_xnT = sb("s_xnT", [128, KT, CH], BF16)
        s_zf = sb("s_zf", [128, 2, CH])
        s_zb = sb("s_zb", [128, 2, CH], BF16)
        s_z2 = sb("s_z2", [128, 2, CH])
        s_rs = sb("s_rs", [2, CH])
        s_rsb = sb("s_rsb", [128, 2, CH])
        s_pz = sb("s_pz", [128, 2, CH])
        s_fb = sb("s_fb", [128, 2, CH])
        s_kf = sb("s_kf", [128, 8, CH], BF16)
        s_qf = sb("s_qf", [128, 8, CH], BF16)
        s_vaug = sb("s_vaug", [128, 4, NH, HD + 1], BF16)
        s_kft = sb("s_kft", [128, 4, 8, 128], BF16)
        s_carry = sb("s_carry", [64, NH * (HD + 1)])
        s_carryb = sb("s_carryb", [128, HD + 1], BF16)
        s_sS = sb("s_sS", [128, 2, CH], BF16)
        s_dent = sb("s_dent", [1, CH])
        s_dscr = sb("s_dscr", [1, CH])
        s_drec = sb("s_drec", [1, CH])
        s_denb = sb("s_denb", [64, CH])
        s_attn = sb("s_attn", [128, KT, CH], BF16)
        s_ao = s_x

        _pt = ps("p_trb0", [128, 128], BF16)
        p_trb = [_pt, _pt]
        p_bc2 = ps("p_bc2", [128, CH])
        p_mm = [ps("p_mm0", [128, CH]), ps("p_mm1", [128, CH])]
        p_S = ps("p_S", [128, CH])
        p_ctx = ps("p_ctx", [HD + 1, CH])
        p_upd = ps("p_upd", [64, HD + 1])
        p_bc = ps("p_bc", [128, CH])

        LD = P.sem("LD")
        LDC = P.sem("LDC")
        nld = [0, 0]

        def load(out, in_, cast=False):
            if cast:
                P.dma("gpsimd", out, in_, inc=LDC)
                nld[1] += 1
            else:
                P.dma("sync", out, in_, inc=LD)
                nld[0] += 1

        load(s_wqkv[:], wqkvT.rearrange("(k p) n -> p k n", p=128), cast=True)
        load(s_bq[:], bq_col)
        load(s_bv[:], bv_row, cast=True)
        load(s_om[:], omega_bd.rearrange("a p n -> p a n"), cast=True)
        load(s_qv[:], quadv)
        load(s_qw[:], quadw)
        load(s_selq[:], selq)
        load(s_sel2[:], sel2)
        load(s_bd2[:], bd2)
        load(s_flag[:], flag)
        load(s_flagb[:], flag_bf)
        load(s_tri[:], tri)
        load(s_idb[:], ident_b)
        load(s_wout[:], woutT.rearrange("(k p) n -> p k n", p=128), cast=True)
        load(s_bo[:], bo_row, cast=True)
        LDA = [(LD, 16 * nld[0]), (LDC, 16 * nld[1])]

        V, S, TE, DO = P.sem("V"), P.sem("S"), P.sem("TE"), P.sem("DO")
        TRC, SAV = P.sem("TRC"), P.sem("SAV")
        CHV, CHT, CHU = P.sem("CHV"), P.sem("CHT"), P.sem("CHU")
        DX = [P.sem("DX0")]
        dx_cnt = [0]

        P.op("vector", lambda e: e.memset(s_ones_b[:], 1.0), wait=LDA)
        P.op("vector", lambda e: e.memset(s_ones64[:], 1.0))
        P.op("vector", lambda e: e.memset(s_c40[:], 40.0))
        P.op("vector", lambda e: e.memset(s_cm20[:], -20.0))
        P.op("vector", lambda e: e.memset(s_carry[:], 0.0))
        P.op("vector", lambda e: e.memset(s_vaug[:, :, :, HD:HD + 1], 1.0))
        P.op("vector", lambda e: e.tensor_scalar(out=s_cscr[:], in0=s_qv[:],
             scalar1=0.0, scalar2=2.0, op0=ALU.max, op1=ALU.mult))
        v1 = P.op("vector", lambda e: e.tensor_scalar(out=s_cscr[:], in0=s_cscr[:],
                  scalar1=1e-30, scalar2=None, op0=ALU.max), inc=V)
        s0 = P.op("scalar", lambda e: e.activation(out=s_cscr2[:], in_=s_cscr[:],
                  func=AF.Ln), wait=[(V, v1)], inc=S)
        s1 = P.op("scalar", lambda e: e.activation(out=s_cscr2[:], in_=s_cscr2[:],
                  func=AF.Exp, scale=0.5), wait=[(S, s0)], inc=S)
        v1 = P.op("vector", lambda e: e.tensor_scalar(out=s_cscr2[:], in0=s_cscr2[:],
                  scalar1=-1.0, scalar2=None, op0=ALU.mult), wait=[(S, s1)], inc=V)
        t1 = P.op("tensor", lambda e: e.matmul(out=p_bc[:, 0:1], lhsT=s_selq[:],
                  rhs=s_cscr2[:], start=True, stop=True), wait=[(V, v1)],
                  inc=TE, drain_inc=True)
        v1 = P.op("vector", lambda e: e.tensor_copy(out=s_cols[:, 0:1],
                  in_=p_bc[:, 0:1]), wait=[(TE, t1)], inc=V)
        v2 = P.op("vector", lambda e: e.tensor_scalar(out=s_cscr[:], in0=s_qv[:],
                  scalar1=20.0, scalar2=None, op0=ALU.add), inc=V)
        t1 = P.op("tensor", lambda e: e.matmul(out=p_bc[:, 1:2], lhsT=s_selq[:],
                  rhs=s_cscr[:], start=True, stop=True), wait=[(V, v2)],
                  inc=TE, drain_inc=True)
        v1 = P.op("vector", lambda e: e.tensor_copy(out=s_cols[:, 1:2],
                  in_=p_bc[:, 1:2]), wait=[(TE, t1)], inc=V)
        v2 = P.op("vector", lambda e: e.tensor_scalar(out=s_cscr[:], in0=s_qw[:],
                  scalar1=0.0, scalar2=1e-60, op0=ALU.max, op1=ALU.max),
                  wait=[(TE, t1)], inc=V)
        s2 = P.op("scalar", lambda e: e.activation(out=s_cscr[:], in_=s_cscr[:],
                  func=AF.Ln), wait=[(V, v2)], inc=S)
        s2 = P.op("scalar", lambda e: e.activation(out=s_cscr[:], in_=s_cscr[:],
                  func=AF.Exp, scale=0.5), wait=[(S, s2)], inc=S)
        v2 = P.op("vector", lambda e: e.tensor_scalar(out=s_cscr[:], in0=s_cscr[:],
                  scalar1=1.0 / 32.0, scalar2=None, op0=ALU.mult),
                  wait=[(S, s2)], inc=V)
        t1 = P.op("tensor", lambda e: e.matmul(out=p_bc[:, 2:3], lhsT=s_selq[:],
                  rhs=s_cscr[:], start=True, stop=True), wait=[(V, v2)],
                  inc=TE, drain_inc=True)
        v1 = P.op("vector", lambda e: e.tensor_copy(out=s_cols[:, 2:3],
                  in_=p_bc[:, 2:3]), wait=[(TE, t1)], inc=V)
        setup_v = (V, P.count["V"])

        mm_free = [(V, 0), (V, 0)]
        S_free = (V, 0)
        sS_loc = [0, 0]
        ctx_free = (V, 0)
        upd_free = (V, 0)
        bc_free = (V, P.count["V"])
        bc2_free = (V, 0)
        nsq_free = (V, 0)
        last_xn_dma = (DO, 0)
        last_nsq = (TE, 0)
        last_rs_mm = (TE, 0)
        last_feat = (V, 0)
        last_ao_dma = (DO, 0)
        last_ao_read = (TE, 0)
        ln_scalar_last = (S, 0)
        mm_turn = [0]
        pz_par = [0]

        def mm_bank():
            b = mm_turn[0] % 2
            mm_turn[0] += 1
            return b

        nch = n_pre + n_own
        for c in range(nch):
            own = c >= n_pre
            o = c - n_pre
            buf = c % 2
            wl1 = [(CHV, c)] if c >= 1 else []
            if c >= 1:
                wl1.append(last_ao_dma)
            dx_cnt[0] += 16
            P.dma("sync", s_x[:], rearr_tok(xw, c * CH, CH),
                  wait=wl1, inc=DX[0])
            ldx = (DX[0], dx_cnt[0])

            if stages == 0:
                vv = None
                for i in range(4):
                    vv = P.op("vector", lambda e, i=i: e.tensor_copy(
                        out=s_xn[:, i, :], in_=s_x[:, i, :]),
                        wait=[ldx] if i == 0 else [], inc=V)
                if own:
                    dd = P.dma("sync", rearr_tok(xn_sc, o * CH, CH), s_xn[:],
                               wait=[(V, vv)], inc=DO)
                    last_xn_dma = (DO, dd)
                P.op("tensor", lambda e: e.nop(), inc=CHT)
                P.op("vector", lambda e: e.nop(), inc=CHV)
                continue
            if stages == 10:
                vv = None
                for i in range(4):
                    vr0 = P.op("vector", lambda e, i=i: e.tensor_reduce(
                        out=s_lncol[:, 2 * i:2 * i + 1], in_=s_x[:, i, :],
                        axis=AX.X, op=ALU.add), wait=[ldx] if i == 0 else [],
                        inc=V)
                    vm = P.op("vector", lambda e, i=i, vr0=vr0: e.tensor_scalar(
                        out=s_lncol[:, 2 * i:2 * i + 1],
                        in0=s_lncol[:, 2 * i:2 * i + 1],
                        scalar1=1.0 / D, scalar2=None, op0=ALU.mult),
                        wait=[(V, vr0)], inc=V)
                    vv = P.op("vector", lambda e, i=i: e.tensor_scalar(
                        out=s_xn[:, i, :], in0=s_x[:, i, :],
                        scalar1=s_lncol[:, 2 * i:2 * i + 1], scalar2=None,
                        op0=ALU.subtract), wait=[(V, vm)], inc=V)
                if own:
                    dd = P.dma("sync", rearr_tok(xn_sc, o * CH, CH), s_xn[:],
                               wait=[(V, vv)], inc=DO)
                    last_xn_dma = (DO, dd)
                P.op("tensor", lambda e: e.nop(), inc=CHT)
                P.op("vector", lambda e: e.nop(), inc=CHV)
                continue
            if stages == 11:
                vv = None
                for i in range(4):
                    P.op("vector", lambda e, i=i: e.tensor_reduce(
                        out=s_lncol[:, 2 * i:2 * i + 1], in_=s_x[:, i, :],
                        axis=AX.X, op=ALU.add), wait=[ldx] if i == 0 else [])
                    P.op("vector", lambda e, i=i: e.tensor_scalar(
                        out=s_lncol[:, 2 * i:2 * i + 1],
                        in0=s_lncol[:, 2 * i:2 * i + 1],
                        scalar1=1.0 / D, scalar2=None, op0=ALU.mult))
                    va = P.op("vector", lambda e, i=i: e.tensor_scalar(
                        out=s_xn[:, i, :], in0=s_x[:, i, :],
                        scalar1=s_lncol[:, 2 * i:2 * i + 1], scalar2=None,
                        op0=ALU.subtract), inc=V)
                    ssq = P.op("scalar", lambda e, i=i: e.activation(
                        out=s_z2[:].rearrange("p a b -> p (a b)"),
                        in_=s_xn[:, i, :], func=AF.Square,
                        accum_out=s_lncol[:, 2 * i + 1:2 * i + 2]),
                        wait=[(V, va)], inc=S)
                    vv = P.op("vector", lambda e, i=i: e.tensor_scalar(
                        out=s_xn[:, i, :], in0=s_xn[:, i, :],
                        scalar1=s_lncol[:, 2 * i + 1:2 * i + 2], scalar2=None,
                        op0=ALU.bypass if False else ALU.mult),
                        wait=[(S, ssq)], inc=V)
                if own:
                    dd = P.dma("sync", rearr_tok(xn_sc, o * CH, CH), s_xn[:],
                               wait=[(V, vv)], inc=DO)
                    last_xn_dma = (DO, dd)
                P.op("tensor", lambda e: e.nop(), inc=CHT)
                P.op("vector", lambda e: e.nop(), inc=CHV)
                continue
            prev_S = ln_scalar_last
            for i in range(4):
                w0 = [ldx] if i == 0 else []
                if i == 0 and c == 0:
                    w0.append(setup_v)
                P.op("vector", lambda e, i=i: e.tensor_reduce(
                    out=s_lncol[:, 2 * i:2 * i + 1], in_=s_x[:, i, :],
                    axis=AX.X, op=ALU.add), wait=w0)
                P.op("vector", lambda e, i=i: e.tensor_scalar(
                    out=s_lncol[:, 2 * i:2 * i + 1],
                    in0=s_lncol[:, 2 * i:2 * i + 1],
                    scalar1=1.0 / D, scalar2=None, op0=ALU.mult))
                wxc = [prev_S, last_nsq] if i == 0 else []
                if i == 0 and own and o >= 1:
                    wxc.append(last_xn_dma)
                vv = P.op("vector", lambda e, i=i: e.tensor_scalar(
                    out=s_xn[:, i, :], in0=s_x[:, i, :],
                    scalar1=s_lncol[:, 2 * i:2 * i + 1], scalar2=None,
                    op0=ALU.subtract),
                    wait=wxc, inc=V)
                ssq = P.op("scalar", lambda e, i=i: e.activation(
                    out=s_z2[:].rearrange("p a b -> p (a b)"),
                    in_=s_xn[:, i, :], func=AF.Square,
                    accum_out=s_lncol[:, 2 * i + 1:2 * i + 2]),
                    wait=[(V, vv)], inc=S)
                vv = P.op("vector", lambda e, i=i: e.tensor_scalar(
                    out=s_lncol[:, 2 * i + 1:2 * i + 2],
                    in0=s_lncol[:, 2 * i + 1:2 * i + 2],
                    scalar1=1.0 / D, scalar2=1e-5, op0=ALU.mult, op1=ALU.add),
                    wait=[(S, ssq)], inc=V)
                ss = P.op("scalar", lambda e, i=i: e.activation(
                    out=s_lncol[:, 2 * i + 1:2 * i + 2],
                    in_=s_lncol[:, 2 * i + 1:2 * i + 2], func=AF.Ln),
                    wait=[(V, vv)], inc=S)
                ss = P.op("scalar", lambda e, i=i: e.activation(
                    out=s_lncol[:, 2 * i + 1:2 * i + 2],
                    in_=s_lncol[:, 2 * i + 1:2 * i + 2], func=AF.Exp,
                    scale=-0.5), wait=[(S, ss)], inc=S)
                vv = P.op("vector", lambda e, i=i: e.tensor_scalar(
                    out=s_xn[:, i, :], in0=s_xn[:, i, :],
                    scalar1=s_lncol[:, 2 * i + 1:2 * i + 2], scalar2=None,
                    op0=ALU.mult), wait=[(S, ss)], inc=V)
                sc = P.op("scalar", lambda e, i=i: e.activation(
                    out=s_xnb[:, i, :], in_=s_xn[:, i, :], func=AF.Copy),
                    wait=[(V, vv)], inc=S)
            ln_scalar_last = (S, P.count["S"])
            ln_done_v = (V, P.count["V"])
            ln_done_s = (S, P.count["S"])

            if own:
                dd = P.dma("sync", rearr_tok(xn_sc, o * CH, CH), s_xn[:],
                           wait=[ln_done_v], inc=DO)
                last_xn_dma = (DO, dd)

            tloc = []
            for i in range(4) if stages >= 2 else []:
                for j in range(KT):
                    idx = i * KT + j
                    w = []
                    if idx == 0:
                        w.append(ln_done_s)
                        if c >= 1:
                            w.append((CHT, c))
                    if idx >= 1:
                        w.append((TRC, tloc[idx - 1]))
                    tt = P.op("tensor", lambda e, i=i, j=j, idx=idx: e.transpose(
                        out=p_trb[0][:],
                        in_=s_xnb[:, i, j * 128:(j + 1) * 128],
                        identity=s_idb[:]), wait=w, inc=TE, drain_inc=True)
                    cc = P.op("scalar", lambda e, i=i, j=j, idx=idx: e.activation(
                        out=s_xnT[:, j, i * 128:(i + 1) * 128],
                        in_=p_trb[0][:], func=AF.Identity),
                        wait=[(TE, tt)], inc=TRC)
                    tloc.append(cc)
            xnT_done = (TRC, P.count["TRC"])

            sides = [("k", s_kf, D)] if not own else [("q", s_qf, 0), ("k", s_kf, D)]
            if stages < 3:
                sides = []
            for sname, sdst, coff in sides:
                for hp in range(8):
                    par = pz_par[0] % 2
                    pz_par[0] += 1
                    b = mm_bank()
                    tt = None
                    for k in range(KT):
                        w = []
                        if k == 0:
                            w = [xnT_done, mm_free[b]]
                        r = P.op("tensor", lambda e, k=k, hp=hp, b=b, coff=coff:
                                 e.matmul(out=p_mm[b][:],
                                          lhsT=s_wqkv[:, k, coff + hp * 128:coff + (hp + 1) * 128],
                                          rhs=s_xnT[:, k, :],
                                          start=(k == 0), stop=(k == KT - 1)),
                                 wait=w, inc=TE if k == KT - 1 else None)
                        if r is not None:
                            tt = r
                    mcol = (coff + hp * 128) // 128
                    ev = P.op("vector", lambda e, b=b, par=par, mcol=mcol:
                              e.tensor_scalar(out=s_zf[:, par, :], in0=p_mm[b][:],
                                              scalar1=s_bq[:, mcol:mcol + 1],
                                              scalar2=None, op0=ALU.add),
                              wait=[(TE, tt)], inc=V)
                    sc1 = P.op("scalar", lambda e, par=par: e.activation(
                        out=s_zb[:, par, :], in_=s_zf[:, par, :], func=AF.Copy),
                        wait=[(V, ev)], inc=S)
                    sc2 = P.op("scalar", lambda e, par=par: e.activation(
                        out=s_z2[:, par, :], in_=s_zf[:, par, :], func=AF.Square),
                        inc=S)
                    tn = P.op("tensor", lambda e, par=par: e.matmul(
                        out=p_bc[0:2, :], lhsT=s_bd2[:], rhs=s_z2[:, par, :],
                        start=True, stop=True),
                        wait=[(S, sc2), nsq_free, bc_free], inc=TE,
                        drain_inc=True)
                    vr = P.op("vector", lambda e: e.tensor_scalar(
                        out=s_rs[:], in0=p_bc[0:2, :], scalar1=1e-24,
                        scalar2=None, op0=ALU.max),
                        wait=[(TE, tn), last_rs_mm], inc=V)
                    nsq_free = (V, vr)
                    sr = P.op("scalar", lambda e: e.activation(
                        out=s_rs[:], in_=s_rs[:], func=AF.Ln),
                        wait=[(V, vr)], inc=S)
                    sr = P.op("scalar", lambda e: e.activation(
                        out=s_rs[:], in_=s_rs[:], func=AF.Exp,
                        scale=-0.5), wait=[(S, sr)], inc=S)
                    tb = P.op("tensor", lambda e: e.matmul(
                        out=p_bc[:], lhsT=s_sel2[:], rhs=s_rs[:],
                        start=True, stop=True), wait=[(S, sr), bc_free],
                        inc=TE, drain_inc=True)
                    last_rs_mm = (TE, tb)
                    vb = P.op("vector", lambda e, par=par: e.tensor_copy(
                        out=s_rsb[:, par, :], in_=p_bc[:]), wait=[(TE, tb)],
                        inc=V)
                    bc_free = (V, vb)
                    tp = P.op("tensor", lambda e, hp=hp, b=b, par=par: e.matmul(
                        out=p_mm[b][:], lhsT=s_om[:, hp, :], rhs=s_zb[:, par, :],
                        start=True, stop=True), wait=[(S, sc1)], inc=TE,
                        drain_inc=True)
                    vp = P.op("vector", lambda e, b=b, par=par: e.tensor_tensor(
                        out=s_pz[:, par, :], in0=p_mm[b][:],
                        in1=s_rsb[:, par, :], op=ALU.mult),
                        wait=[(TE, tp), (V, vb)], inc=V)
                    mm_free[b] = (V, vp)
                    sf = P.op("scalar", lambda e, par=par: e.activation(
                        out=s_fb[:, 0, :], in_=s_pz[:, par, :], func=AF.Relu,
                        scale=s_cols[:, 0:1], bias=s_cols[:, 1:2]),
                        wait=[(V, vp), last_feat] if True else [], inc=S)
                    sf = P.op("scalar", lambda e: e.activation(
                        out=s_fb[:, 0, :], in_=s_fb[:, 0, :], func=AF.Relu,
                        scale=-1.0, bias=s_c40[:, 0:1]), wait=[(S, sf)], inc=S)
                    sf = P.op("scalar", lambda e: e.activation(
                        out=s_fb[:, 0, :], in_=s_fb[:, 0, :], func=AF.Exp,
                        bias=s_cm20[:, 0:1]), wait=[(S, sf)], inc=S)
                    sf = P.op("scalar", lambda e, par=par: e.activation(
                        out=s_fb[:, 1, :], in_=s_pz[:, par, :], func=AF.Square),
                        inc=S)
                    wf = [(S, sf)]
                    if hp == 0 and c >= 1:
                        wf.append((CHT, c))
                    lf = P.op("vector", lambda e, sdst=sdst, hp=hp: e.scalar_tensor_tensor(
                        out=sdst[:, hp, :], in0=s_fb[:, 1, :],
                        scalar=s_cols[:, 2:3], in1=s_fb[:, 0, :],
                        op0=ALU.mult, op1=ALU.mult), wait=wf, inc=V)
                    last_feat = (V, lf)
            feat_done = (V, P.count["V"])
            last_nsq = (TE, P.count["TE"])

            for i in range(4) if stages >= 4 else []:
                for n in range(2):
                    b = mm_bank()
                    for k in range(KT):
                        w = [xnT_done, mm_free[b]] if k == 0 else []
                        P.op("tensor", lambda e, i=i, n=n, k=k, b=b: e.matmul(
                            out=p_mm[b][:],
                            lhsT=s_xnT[:, k, i * 128:(i + 1) * 128],
                            rhs=s_wqkv[:, k, 2 * D + n * CH:2 * D + (n + 1) * CH],
                            start=(k == 0), stop=False), wait=w)
                    tt = P.op("tensor", lambda e, n=n, b=b: e.matmul(
                        out=p_mm[b][:], lhsT=s_ones_b[0:1, 0:128],
                        rhs=s_bv[0:1, n * CH:(n + 1) * CH],
                        start=False, stop=True), inc=TE)
                    w = [(TE, tt)]
                    if i == 0 and n == 0 and c >= 1:
                        w.append((CHU, c))
                    vv = P.op("vector", lambda e, i=i, n=n, b=b: e.tensor_copy(
                        out=s_vaug[:, i, 8 * n:8 * (n + 1), 0:HD],
                        in_=p_mm[b][:].rearrange("p (h d) -> p h d", d=HD)),
                        wait=w, inc=V)
                    mm_free[b] = (V, vv)
            v_done = (V, P.count["V"])

            kloc = []
            for hp in range(8) if stages >= 4 else []:
                for j in range(4):
                    idx = hp * 4 + j
                    w = []
                    if idx == 0:
                        w.append(feat_done)
                        w.append((TRC, P.count["TRC"]))
                    if idx >= 1:
                        w.append((TRC, kloc[idx - 1]))
                    tt = P.op("tensor", lambda e, hp=hp, j=j, idx=idx: e.transpose(
                        out=p_trb[0][:],
                        in_=s_kf[:, hp, j * 128:(j + 1) * 128],
                        identity=s_idb[:]), wait=w, inc=TE, drain_inc=True)
                    cc = P.op("scalar", lambda e, hp=hp, j=j, idx=idx: e.activation(
                        out=s_kft[:, j, hp, :], in_=p_trb[0][:],
                        func=AF.Identity), wait=[(TE, tt)], inc=TRC)
                    kloc.append(cc)
            kft_done = (TRC, P.count["TRC"])

            if c == n_pre and stages >= 4:
                P.op("vector", lambda e: e.tensor_scalar(
                    out=s_kft[:], in0=s_kft[:], scalar1=s_flag[:, 0:1],
                    scalar2=None, op0=ALU.mult), wait=[kft_done], inc=TRC)
                kft_done = (TRC, P.count["TRC"])

            for h in range(NH) if stages >= 5 else []:
                hp, half = h // 2, h % 2
                r0 = 64 * half
                if own:
                    vcb = P.op("vector", lambda e, h=h, r0=r0: e.tensor_copy(
                        out=s_carryb[r0:r0 + 64, :],
                        in_=s_carry[:, h * 65:(h + 1) * 65]),
                        wait=[upd_free, ctx_free], inc=V)
                    P.op("tensor", lambda e, hp=hp, r0=r0: e.matmul(
                        out=p_ctx[:], lhsT=s_carryb[r0:r0 + 64, :],
                        rhs=s_qf[r0:r0 + 64, hp, :], start=True, stop=False),
                        wait=[(V, vcb), feat_done])
                    for j in range(4):
                        tt = P.op("tensor", lambda e, hp=hp, r0=r0, j=j: e.matmul(
                            out=p_S[:],
                            lhsT=s_kf[r0:r0 + 64, hp, j * 128:(j + 1) * 128],
                            rhs=s_qf[r0:r0 + 64, hp, :], start=True, stop=True),
                            wait=[S_free], inc=TE)
                        wv = [(TE, tt)]
                        if sS_loc[j % 2] > 0:
                            wv.append((SAV, sS_loc[j % 2]))
                        vv = P.op("vector", lambda e, j=j: e.tensor_tensor(
                            out=s_sS[:, j % 2, :], in0=p_S[:],
                            in1=s_tri[:, j, :], op=ALU.mult), wait=wv, inc=V)
                        S_free = (V, vv)
                        av = P.op("tensor", lambda e, h=h, j=j: e.matmul(
                            out=p_ctx[:], lhsT=s_vaug[:, j, h, :],
                            rhs=s_sS[:, j % 2, :], start=False, stop=(j == 3)),
                            wait=[(V, vv), v_done], inc=SAV)
                        sS_loc[j % 2] = av
                    ctx_t = (SAV, P.count["SAV"])
                ut = None
                for j in range(4):
                    w = []
                    if j == 0:
                        w = [kft_done, v_done, upd_free]
                    r = P.op("tensor", lambda e, hp=hp, r0=r0, j=j, h=h: e.matmul(
                        out=p_upd[:], lhsT=s_kft[:, j, hp, r0:r0 + 64],
                        rhs=s_vaug[:, j, h, :], start=(j == 0), stop=(j == 3)),
                        wait=w, inc=TE if j == 3 else None, drain_inc=True)
                    if r is not None:
                        ut = r
                if own:
                    vv = P.op("vector", lambda e: e.tensor_copy(
                        out=s_dent[:], in_=p_ctx[HD:HD + 1, :]),
                        wait=[ctx_t], inc=V)
                    vv = P.op("vector", lambda e: e.tensor_scalar(
                        out=s_dent[:], in0=s_dent[:], scalar1=1e-6,
                        scalar2=None, op0=ALU.max), inc=V)
                    vv = P.op("vector", lambda e: e.reciprocal(
                        out=s_dent[:], in_=s_dent[:]), inc=V)
                    tb = P.op("tensor", lambda e: e.matmul(
                        out=p_bc[0:64, :], lhsT=s_ones64[:], rhs=s_dent[:],
                        start=True, stop=True), wait=[(V, vv), bc_free],
                        inc=TE, drain_inc=True)
                    vv = P.op("vector", lambda e: e.tensor_copy(
                        out=s_denb[:], in_=p_bc[0:64, :]), wait=[(TE, tb)], inc=V)
                    bc_free = (V, vv)
                    wat = [(V, vv)]
                    if h == 0:
                        wat.append(last_ao_read)
                    av = P.op("vector", lambda e, hp=hp, r0=r0: e.tensor_tensor(
                        out=s_attn[r0:r0 + 64, hp, :], in0=p_ctx[0:HD, :],
                        in1=s_denb[:], op=ALU.mult), wait=wat, inc=V)
                    ctx_free = (V, av)
                vv = P.op("vector", lambda e, h=h: e.tensor_tensor(
                    out=s_carry[:, h * 65:(h + 1) * 65],
                    in0=s_carry[:, h * 65:(h + 1) * 65], in1=p_upd[:],
                    op=ALU.add), wait=[(TE, ut)], inc=V)
                upd_free = (V, vv)
            attn_done = (V, P.count["V"])
            P.op("vector", lambda e: e.nop(), inc=CHU)

            if c == n_pre - 1 and stages >= 5:
                vv = P.op("vector", lambda e: e.tensor_scalar(
                    out=s_carry[:], in0=s_carry[:], scalar1=s_flag[0:64, 0:1],
                    scalar2=None, op0=ALU.mult), wait=[upd_free], inc=V)
                upd_free = (V, vv)

            if own and stages >= 6:
                last_ev = None
                for i in range(4):
                    for n in range(2):
                        b = mm_bank()
                        for k in range(KT):
                            w = [attn_done, mm_free[b]] if k == 0 else []
                            P.op("tensor", lambda e, i=i, n=n, k=k, b=b: e.matmul(
                                out=p_mm[b][:],
                                lhsT=s_attn[:, k, i * 128:(i + 1) * 128],
                                rhs=s_wout[:, k, n * CH:(n + 1) * CH],
                                start=(k == 0), stop=False), wait=w)
                        tt = P.op("tensor", lambda e, n=n, b=b: e.matmul(
                            out=p_mm[b][:], lhsT=s_ones_b[0:1, 0:128],
                            rhs=s_bo[0:1, n * CH:(n + 1) * CH],
                            start=False, stop=True), inc=TE)
                        w = [(TE, tt)]
                        last_ev = P.op("vector", lambda e, i=i, n=n, b=b:
                                       e.tensor_copy(out=s_ao[:, i, n * CH:(n + 1) * CH],
                                                     in_=p_mm[b][:]),
                                       wait=w, inc=V)
                        mm_free[b] = (V, last_ev)
                last_ao_read = (TE, P.count["TE"])
                dd = P.dma("sync", rearr_tok(ao_sc, o * CH, CH), s_ao[:],
                           wait=[(V, last_ev)], inc=DO)
                last_ao_dma = (DO, dd)
            P.op("tensor", lambda e: e.nop(), inc=CHT)
            P.op("vector", lambda e: e.nop(), inc=CHV)

        if standalone:
            P.op("sync", lambda e: e.nop(), wait=[(DO, P.count["DO"])])
            P.build(es)
        scr = dict(xn_sc=xn_sc, ao_sc=ao_sc, ident_b=ident_b, flag=flag,
                   tri=tri)
    return nc if standalone else scr


def stage_a_inputs(inputs, core, n_pre=7, n_own=9):
    import ml_dtypes
    x = np.asarray(inputs["L0_attn_x"])
    b, half = core // 2, core % 2
    seq = x[b]
    TO = n_own * CH
    T = (n_pre + n_own) * CH
    if half == 1:
        xw = seq[8192 - T:] if T < 8192 else seq
    else:
        xw = np.concatenate(
            [np.zeros((T - TO + CH, D), np.float32), seq[:TO - CH]], 0)
    wqkvT = np.ascontiguousarray(np.asarray(inputs["L0_attn_w_qkv"]).T)
    bq = np.ascontiguousarray(np.asarray(inputs["L0_attn_b_qkv"]).reshape(24, 128).T)
    bv = np.asarray(inputs["L0_attn_b_qkv"])[2 * D:].reshape(1, D)
    om = np.asarray(inputs["L0_attn_omega"])
    om_bd = np.zeros((8, 128, 128), np.float32)
    for hp in range(8):
        for t in range(2):
            h = 2 * hp + t
            blk = np.concatenate([om[0, h], om[1, h]], axis=1)
            om_bd[hp, 64 * t:64 * t + 64, 64 * t:64 * t + 64] = blk
    qv = np.asarray(inputs["L0_attn_quad_nodes"]).reshape(2, 1).astype(np.float32)
    qw = np.asarray(inputs["L0_attn_quad_weights"]).reshape(2, 1).astype(np.float32)
    selq = np.zeros((2, 128), np.float32)
    for p in range(128):
        selq[(p // 32) % 2, p] = 1.0
    sel2 = np.zeros((2, 128), np.float32)
    sel2[0, 0:64] = 1.0
    sel2[1, 64:128] = 1.0
    bd2 = np.zeros((128, 2), np.float32)
    bd2[0:64, 0] = 1.0
    bd2[64:128, 1] = 1.0
    flagv = np.full((128, 1), float(half), np.float32)
    tri = np.zeros((128, 4, CH), np.float32)
    ar = np.arange(CH)
    for j in range(4):
        for p in range(128):
            tri[p, j, :] = (j * 128 + p <= ar)
    woutT = np.ascontiguousarray(np.asarray(inputs["L0_attn_w_out"]).T)
    bo = np.asarray(inputs["L0_attn_b_out"]).reshape(1, D)
    return dict(
        xw=np.ascontiguousarray(xw, dtype=np.float32),
        wqkvT=wqkvT.astype(np.float32), bq_col=bq.astype(np.float32),
        bv_row=bv.astype(np.float32), omega_bd=om_bd, quadv=qv, quadw=qw,
        selq=selq, sel2=sel2, bd2=bd2, flag=flagv,
        flag_bf=flagv.astype(ml_dtypes.bfloat16), tri=tri,
        ident_b=np.eye(128, dtype=np.float32).astype(ml_dtypes.bfloat16),
        woutT=woutT.astype(np.float32), bo_row=bo.astype(np.float32))




def build_phase_b(n_own=9, chain="ve", ctx=None, scr=None):
    standalone = ctx is None
    if standalone:
        nc = bass.Bass("TRN2", target_bir_lowering=False, debug=False)
    else:
        nc, P = ctx
    TO = n_own * CH
    I = 2816
    IT = I // 128
    HW = 1152
    if standalone:
        xn_in = nc.dram_tensor("xn_in", [TO, D], F32, kind="ExternalInput").ap()
        ao_in = nc.dram_tensor("ao_in", [TO, D], F32, kind="ExternalInput").ap()
    else:
        xn_in, ao_in = scr["xn_sc"], scr["ao_sc"]
    x_in = nc.dram_tensor("x_in", [TO, D], F32, kind="ExternalInput").ap()
    wg_in = nc.dram_tensor("wg_in", [I, D], F32, kind="ExternalInput").ap()
    wu_in = nc.dram_tensor("wu_in", [I, D], F32, kind="ExternalInput").ap()
    wd_in = nc.dram_tensor("wd_in", [D, I], F32, kind="ExternalInput").ap()
    if standalone:
        ident_b = nc.dram_tensor("ident_b", [128, 128], BF16, kind="ExternalInput").ap()
    else:
        ident_b = scr["ident_b"]
    x1_sc = nc.dram_tensor(
        "x1_sc", [TO, D], F32,
        kind="ExternalOutput" if standalone else "Internal").ap()

    if standalone:
        P = Prog(nc, chain=chain)
    es = ExitStack()
    with es:
        sb = lambda n, s, d=F32: es.enter_context(nc.sbuf_tensor('pb_' + n, s, d))
        ps = lambda n, s, d=F32: es.enter_context(nc.psum_tensor('pb_' + n, s, d))

        s_wgq = sb("s_wgq", [128, KT, I], BF16)
        s_wuq = sb("s_wuq", [128, KT, I], BF16)
        s_wdq = sb("s_wdq", [128, IT, D], BF16)
        s_idb = sb("s_idb", [128, 128], BF16)
        s_h = sb("s_h", [128, 2, I])
        s_u = sb("s_u", [128, HW])
        s_au = sb("s_au", [128, HW])
        s_c = sb("s_c", [128, HW])
        s_d = sb("s_d", [128, HW])
        s_m = sb("s_m", [128, 72])
        s_s = sb("s_s", [128, 72])
        s_rcp = sb("s_rcp", [128, 72])
        s_sh = sb("s_sh", [128, 72])
        s_qh = sb("s_qh", [128, 2, I], BF16)
        s_xqT = sb("s_xqT", [128, KT, 128], BF16)
        s_hqT = sb("s_hqT", [128, IT, 128], BF16)
        s_xni = sb("s_xni", [128, D])
        s_aoi = sb("s_aoi", [128, D])
        s_xi = sb("s_xi", [128, D])
        s_sg = sb("s_sg", [128, CH])

        p_g = ps("p_g", [128, CH])
        p_u = ps("p_u", [128, CH])
        p_t = [ps("p_t0", [128, 128], BF16), ps("p_t1", [128, 128], BF16)]
        p_fo = [ps("p_fo0", [128, CH]), ps("p_fo1", [128, CH])]

        LD = P.sem("LD_B")
        V, S, TE, DO = P.sem("V"), P.sem("S"), P.sem("TE"), P.sem("DO")
        TRC = P.sem("TRC")
        DW = P.sem("DW_B")
        DOB = P.sem("DO_B")
        DXN = P.sem("DXN_B")
        DAO = P.sem("DAO_B")
        DXI = P.sem("DXI_B")

        P.dma("sync", s_idb[:], ident_b, inc=LD)
        ld0 = P.count["LD_B"]

        st = {"v": P.count["V"], "te": P.count["TE"]}
        tr_guard = {}

        def quant(src_ap, dst_ap, W, wait_src=(), gwait=(), dst_guard=()):
            nh = (W + HW - 1) // HW
            last = None
            for hh in range(nh):
                w0, w1 = hh * HW, min((hh + 1) * HW, W)
                ww = w1 - w0
                G = ww // 16
                src3 = src_ap[:, w0:w1].rearrange("p (g k) -> p g k", k=16)
                dst3 = dst_ap[:, w0:w1].rearrange("p (g k) -> p g k", k=16)
                wfirst = (list(wait_src) + list(gwait)) if hh == 0 else []
                P.op("vector", lambda e, src3=src3, G=G: e.tensor_reduce(
                    out=s_m[:, 0:G], in_=src3, axis=AX.X, op=ALU.max,
                    apply_absolute_value=True), wait=wfirst)
                P.op("vector", lambda e, G=G: e.tensor_scalar(
                    out=s_s[:, 0:G], in0=s_m[:, 0:G], scalar1=1.0 / 6.0,
                    scalar2=1e-12, op0=ALU.mult, op1=ALU.max))
                P.recip(s_rcp[:, 0:G], s_s[:, 0:G], None)
                P.op("vector", lambda e, G=G: e.tensor_scalar(
                    out=s_sh[:, 0:G], in0=s_s[:, 0:G], scalar1=0.5,
                    scalar2=None, op0=ALU.mult))
                vv = P.op("vector", lambda e, src3=src3, G=G, ww=ww:
                          e.tensor_tensor(
                    out=s_u[:, 0:ww].rearrange("p (g k) -> p g k", k=16),
                    in0=src3,
                    in1=s_rcp[:, 0:G].unsqueeze(-1).broadcast_to((128, G, 16)),
                    op=ALU.mult), inc=V)
                sa = P.op("scalar", lambda e, ww=ww: e.activation(
                    out=s_au[:, 0:ww], in_=s_u[:, 0:ww], func=AF.Abs),
                    wait=[(V, vv)], inc=S)
                P.op("vector", lambda e, ww=ww: e.tensor_scalar(
                    out=s_c[:, 0:ww], in0=s_au[:, 0:ww], scalar1=0.25,
                    scalar2=None, op0=ALU.is_gt), wait=[(S, sa)])
                for t in (0.75, 1.25, 1.75):
                    P.op("vector", lambda e, t=t, ww=ww: e.scalar_tensor_tensor(
                        out=s_c[:, 0:ww], in0=s_au[:, 0:ww], scalar=t,
                        in1=s_c[:, 0:ww], op0=ALU.is_gt, op1=ALU.add))
                P.op("vector", lambda e, ww=ww: e.tensor_scalar(
                    out=s_d[:, 0:ww], in0=s_au[:, 0:ww], scalar1=2.5,
                    scalar2=None, op0=ALU.is_gt))
                P.op("vector", lambda e, ww=ww: e.scalar_tensor_tensor(
                    out=s_d[:, 0:ww], in0=s_au[:, 0:ww], scalar=3.5,
                    in1=s_d[:, 0:ww], op0=ALU.is_gt, op1=ALU.add))
                P.op("vector", lambda e, ww=ww: e.scalar_tensor_tensor(
                    out=s_c[:, 0:ww], in0=s_d[:, 0:ww], scalar=2.0,
                    in1=s_c[:, 0:ww], op0=ALU.mult, op1=ALU.add))
                P.op("vector", lambda e, ww=ww: e.tensor_scalar(
                    out=s_d[:, 0:ww], in0=s_au[:, 0:ww], scalar1=5.0,
                    scalar2=4.0, op0=ALU.is_gt, op1=ALU.mult))
                vv = P.op("vector", lambda e, ww=ww: e.tensor_add(
                    out=s_c[:, 0:ww], in0=s_c[:, 0:ww], in1=s_d[:, 0:ww]),
                    inc=V)
                sg_ = P.op("scalar", lambda e, ww=ww: e.sign(
                    out=s_au[:, 0:ww], in_=s_u[:, 0:ww]), wait=[(V, vv)], inc=S)
                P.op("vector", lambda e, ww=ww: e.tensor_mul(
                    out=s_c[:, 0:ww], in0=s_c[:, 0:ww], in1=s_au[:, 0:ww]),
                    wait=[(S, sg_)] + (list(dst_guard) if hh == 0 else []))
                last = P.op("vector", lambda e, dst3=dst3, G=G, ww=ww:
                            e.tensor_tensor(
                    out=dst3,
                    in0=s_c[:, 0:ww].rearrange("p (g k) -> p g k", k=16),
                    in1=s_sh[:, 0:G].unsqueeze(-1).broadcast_to((128, G, 16)),
                    op=ALU.mult), inc=V)
            return last

        dwc = [0]
        wq_v = {}
        slabs = ([(wg_in, s_wgq, sl, D, True) for sl in range(IT)]
                 + [(wu_in, s_wuq, sl, D, True) for sl in range(IT)]
                 + [(wd_in, s_wdq, sl, I, False) for sl in range(KT)])
        for gi, (wsrc, wdst, sl, wwidth, d_is_g) in enumerate(slabs):
            slot = gi % 2
            wld = [(V, wq_v.get(gi - 2, 0))]
            if gi == 0:
                wld.append((LD, ld0))
            dwc[0] += 16
            P.dma("sync", s_h[:, slot, 0:wwidth],
                  wsrc[sl * 128:(sl + 1) * 128, :], wait=wld, inc=DW)
            lastq = quant(s_h[:, slot, :], s_qh[:, slot, :], wwidth,
                          wait_src=[(DW, dwc[0])],
                          dst_guard=[(TE, st["te"])])
            wq_v[gi] = lastq
            ntr = wwidth // 128
            for j in range(ntr):
                idx = j % 2
                w = [(V, lastq)] if j == 0 else []
                g = tr_guard.get(idx)
                if g is not None:
                    w.append((TRC, g))
                tt = P.op("tensor", lambda e, j=j, idx=idx, slot=slot: e.transpose(
                    out=p_t[idx][:], in_=s_qh[:, slot, j * 128:(j + 1) * 128],
                    identity=s_idb[:]), wait=w, inc=TE)
                cc = P.op("scalar", lambda e, j=j, idx=idx, sl=sl, wdst=wdst:
                          e.activation(
                    out=wdst[:, j, sl * 128:(sl + 1) * 128],
                    in_=p_t[idx][:], func=AF.Identity),
                    wait=[(TE, tt)], inc=TRC)
                tr_guard[idx] = cc
            st["te"] = P.count["TE"]
        wq_done = (TRC, P.count["TRC"])

        tiles = n_own * 4
        mm_free_g = (S, 0)
        mm_free_u = (V, 0)
        fo_free = [(V, 0), (V, 0)]
        last_x1_dma = {}
        dxn_c, dao_c, dxi_c = [0], [0], [0]
        xnq_done = {}
        xqT_done = {}
        xq_tr_te = {}
        xq_used = {}
        te_after_def = {}
        hq_done = {}
        ev_done = {}
        dao_t = {}
        dxi_t = {}

        def stage_a(t):
            w = [(V, xnq_done.get(t - 1, 0))]
            if t == 0:
                w.append(wq_done)
            dxn_c[0] += 16
            P.dma("sync", s_xni[:], xn_in[t * 128:(t + 1) * 128, :],
                  wait=w, inc=DXN)
            lastq = quant(s_xni, s_qh[:, t % 2, :], D,
                          wait_src=[(DXN, dxn_c[0])],
                          dst_guard=[(TE, te_after_def.get(t - 2, 0))])
            xnq_done[t] = lastq


        def stage_b(t):
            for j in range(KT):
                idx = j % 2
                w = [(V, xnq_done[t])] if j == 0 else []
                if j == 0:
                    w.append((TE, xq_used.get(t - 2, 0)))
                g = tr_guard.get(idx)
                if g is not None:
                    w.append((TRC, g))
                tt = P.op("tensor", lambda e, j=j, idx=idx, t=t: e.transpose(
                    out=p_t[idx][:], in_=s_qh[:, t % 2, j * 128:(j + 1) * 128],
                    identity=s_idb[:]), wait=w, inc=TE)
                cc = P.op("scalar", lambda e, j=j, idx=idx, t=t: e.activation(
                    out=s_xqT[:, j, :], in_=p_t[idx][:], func=AF.Identity),
                    wait=[(TE, tt)], inc=TRC)
                tr_guard[idx] = cc
            xqT_done[t] = P.count["TRC"]
            xq_tr_te[t] = P.count["TE"]

        def stage_c(t):
            for nt in range(6):
                n0 = nt * CH
                nn_ = min(n0 + CH, I) - n0
                tg = tu = None
                for k in range(KT):
                    w = [(TRC, xqT_done[t]), mm_free_g2[0]] if k == 0 else []
                    r = P.op("tensor", lambda e, k=k, n0=n0, nn_=nn_, t=t: e.matmul(
                        out=p_g[:, 0:nn_], lhsT=s_xqT[:, k, :],
                        rhs=s_wgq[:, k, n0:n0 + nn_],
                        start=(k == 0), stop=(k == KT - 1)),
                        wait=w, inc=TE if k == KT - 1 else None)
                    if r is not None:
                        tg = r
                for k in range(KT):
                    w = [mm_free_u2[0]] if k == 0 else []
                    r = P.op("tensor", lambda e, k=k, n0=n0, nn_=nn_, t=t: e.matmul(
                        out=p_u[:, 0:nn_], lhsT=s_xqT[:, k, :],
                        rhs=s_wuq[:, k, n0:n0 + nn_],
                        start=(k == 0), stop=(k == KT - 1)),
                        wait=w, inc=TE if k == KT - 1 else None)
                    if r is not None:
                        tu = r
                ssg = P.op("scalar", lambda e, nn_=nn_: e.activation(
                    out=s_sg[:, 0:nn_], in_=p_g[:, 0:nn_], func=AF.Silu),
                    wait=[(TE, tg), mm_free_u2[0]], inc=S)
                mm_free_g2[0] = (S, ssg)
                vh = P.op("vector", lambda e, n0=n0, nn_=nn_, t=t: e.tensor_tensor(
                    out=s_h[:, t % 2, n0:n0 + nn_], in0=p_u[:, 0:nn_],
                    in1=s_sg[:, 0:nn_], op=ALU.mult),
                    wait=[(S, ssg), (TE, tu)], inc=V)
                mm_free_u2[0] = (V, vh)
            ev_done[t] = P.count["V"]
            xq_used[t] = P.count["TE"]

        def stage_d(t):
            dxi_c[0] += 16
            P.dma("gpsimd", s_xi[:], x_in[t * 128:(t + 1) * 128, :],
                  wait=[(V, last_lastv[0])], inc=DXI)
            dxi_t[t] = dxi_c[0]
            dao_c[0] += 16
            P.dma("gpsimd", s_aoi[:], ao_in[t * 128:(t + 1) * 128, :],
                  wait=[(DOB, last_x1_dma.get(t - 1, 0))], inc=DAO)
            dao_t[t] = dao_c[0]
            lastq = quant(s_h[:, t % 2, :], s_qh[:, t % 2, :], I,
                          wait_src=[(V, ev_done[t])],
                          dst_guard=[(TE, xq_tr_te[t])])
            hq_done[t] = lastq

        def stage_ef(t):
            lastq = hq_done[t]
            for j in range(IT):
                idx = j % 2
                w = [(V, lastq)] if j == 0 else []
                g = tr_guard.get(idx)
                if g is not None:
                    w.append((TRC, g))
                tt = P.op("tensor", lambda e, j=j, idx=idx, t=t: e.transpose(
                    out=p_t[idx][:], in_=s_qh[:, t % 2, j * 128:(j + 1) * 128],
                    identity=s_idb[:]), wait=w, inc=TE)
                cc = P.op("scalar", lambda e, j=j, idx=idx: e.activation(
                    out=s_hqT[:, j, :], in_=p_t[idx][:], func=AF.Identity),
                    wait=[(TE, tt)], inc=TRC)
                tr_guard[idx] = cc
            hqT_t = P.count["TRC"]
            lastv = None
            for n2 in range(2):
                tf = None
                for k in range(IT):
                    w = [(TRC, hqT_t), fo_free[n2]] if k == 0 else []
                    r = P.op("tensor", lambda e, k=k, n2=n2: e.matmul(
                        out=p_fo[n2][:], lhsT=s_hqT[:, k, :],
                        rhs=s_wdq[:, k, n2 * CH:(n2 + 1) * CH],
                        start=(k == 0), stop=(k == IT - 1)),
                        wait=w, inc=TE if k == IT - 1 else None)
                    if r is not None:
                        tf = r
                vv = P.op("vector", lambda e, n2=n2, t=t: e.tensor_tensor(
                    out=s_aoi[:, n2 * CH:(n2 + 1) * CH],
                    in0=p_fo[n2][:],
                    in1=s_aoi[:, n2 * CH:(n2 + 1) * CH],
                    op=ALU.add), wait=[(TE, tf), (DAO, dao_t[t])], inc=V)
                fo_free[n2] = (V, vv)
                lastv = P.op("vector", lambda e, n2=n2, t=t: e.tensor_add(
                    out=s_aoi[:, n2 * CH:(n2 + 1) * CH],
                    in0=s_aoi[:, n2 * CH:(n2 + 1) * CH],
                    in1=s_xi[:, n2 * CH:(n2 + 1) * CH]),
                    wait=[(DXI, dxi_t[t])] if n2 == 0 else [], inc=V)
                last_lastv[0] = lastv
            dd = P.dma("gpsimd", x1_sc[t * 128:(t + 1) * 128, :],
                       s_aoi[:], wait=[(V, lastv)], inc=DOB)
            last_x1_dma[t] = dd
            te_after_def[t] = P.count["TE"]

        mm_free_g2 = [mm_free_g]
        mm_free_u2 = [mm_free_u]
        last_lastv = [0]
        for t in range(tiles + 1):
            if t < tiles:
                stage_a(t)
                stage_b(t)
            if t >= 1:
                stage_d(t - 1)
            if t < tiles:
                stage_c(t)
            if t >= 1:
                stage_ef(t - 1)
        if standalone:
            P.op("sync", lambda e: e.nop(), wait=[(DOB, P.count["DO_B"])])
            P.build(es)
        scr2 = dict(scr or {})
        scr2["x1_sc"] = x1_sc
    return nc if standalone else scr2


def stage_b_inputs(inputs, core, n_own=9):
    import ml_dtypes
    x = np.asarray(inputs["L0_attn_x"])
    b, half = core // 2, core % 2
    seq = x[b]
    TO = n_own * CH
    if half == 1:
        x_own = seq[8192 - TO:]
    else:
        x_own = np.concatenate([np.zeros((CH, D), np.float32),
                                seq[:TO - CH]], 0)
    return dict(
        x_in=np.ascontiguousarray(x_own, dtype=np.float32),
        wg_in=np.asarray(inputs["L0_ffn_w_gate"], np.float32),
        wu_in=np.asarray(inputs["L0_ffn_w_up"], np.float32),
        wd_in=np.asarray(inputs["L0_ffn_w_down"], np.float32),
        ident_b=np.eye(128, dtype=np.float32).astype(ml_dtypes.bfloat16))




def build_phase_c(n_own=9, chain="ve", ctx=None, scr=None):
    standalone = ctx is None
    if standalone:
        nc = bass.Bass("TRN2", target_bir_lowering=False, debug=False)
    else:
        nc, P = ctx
    TO = n_own * CH
    NQ = n_own - 1
    if standalone:
        x1_in = nc.dram_tensor("x1_in", [TO, D], F32, kind="ExternalInput").ap()
    else:
        x1_in = scr["x1_sc"]
    w1_in = nc.dram_tensor("w1_in", [D, 3 * D], F32, kind="ExternalInput").ap()
    wo1_in = nc.dram_tensor("wo1_in", [D, D], F32, kind="ExternalInput").ap()
    gwT_in = nc.dram_tensor("gwT_in", [D, 16], F32, kind="ExternalInput").ap()
    sgwT_in = nc.dram_tensor("sgwT_in", [128, KT], F32, kind="ExternalInput").ap()
    if standalone:
        flag = nc.dram_tensor("flag", [128, 1], F32, kind="ExternalInput").ap()
        tri = nc.dram_tensor("tri", [128, 4, CH], F32, kind="ExternalInput").ap()
        ident_b = nc.dram_tensor("ident_b", [128, 128], BF16, kind="ExternalInput").ap()
    else:
        flag, tri, ident_b = scr["flag"], scr["tri"], scr["ident_b"]
    ident_f = nc.dram_tensor("ident_f", [128, 128], F32, kind="ExternalInput").ap()

    kind = "ExternalOutput" if standalone else "Internal"
    ao1_sc = nc.dram_tensor("ao1_sc", [NQ * CH, D], F32, kind=kind).ap()
    xn1T_sc = nc.dram_tensor("xn1T_sc", [NQ, 128, KT, CH], BF16, kind=kind).ap()
    cmbT_sc = nc.dram_tensor("cmbT_sc", [NQ, 4, 16, 128], F32, kind=kind).ap()
    gate_sc = nc.dram_tensor("gate_sc", [NQ, 128, 4], F32, kind=kind).ap()

    if standalone:
        P = Prog(nc, chain=chain)
    es = ExitStack()
    with es:
        sb = lambda n, s, d=F32: es.enter_context(nc.sbuf_tensor('pc_' + n, s, d))
        ps = lambda n, s, d=F32: es.enter_context(nc.psum_tensor('pc_' + n, s, d))

        s_w1 = sb("s_w1", [128, KT, 3 * D], BF16)
        s_wo1 = sb("s_wo1", [128, KT, D], BF16)
        s_gwT = sb("s_gwT", [128, KT, 16])
        s_sgwT = sb("s_sgwT", [128, KT])
        s_flag = sb("s_flag", [128, 1])
        s_tri = sb("s_tri", [128, 4, CH])
        s_idb = sb("s_idb", [128, 128], BF16)
        s_idf = sb("s_idf", [128, 128])
        s_x1 = sb("s_x1", [128, 4, D])
        s_xn1 = sb("s_xn1", [128, 4, D])
        s_xnb = sb("s_xnb", [128, 4, D], BF16)
        s_lncol = sb("s_lncol", [128, 8])
        s_sq2 = sb("s_sq2", [128, D])
        s_xn1T = sb("s_xn1T", [128, KT, CH], BF16)
        s_xTf = sb("s_xTf", [128, KT, 128])
        s_qf = sb("s_qf", [128, KT, CH], BF16)
        s_kr = [sb("s_kr0", [128, KT, CH], BF16), sb("s_kr1", [128, KT, CH], BF16)]
        s_vr = [sb("s_vr0", [128, 4, NH, HD + 1], BF16),
                sb("s_vr1", [128, 4, NH, HD + 1], BF16)]
        s_eSf = sb("s_eSf", [128, CH])
        s_eSb = sb("s_eSb", [128, 2, CH], BF16)
        s_dent = sb("s_dent", [1, CH])
        s_dscr = sb("s_dscr", [1, CH])
        s_drec = sb("s_drec", [1, CH])
        s_denb = sb("s_denb", [64, CH])
        s_attn = sb("s_attn", [128, KT, CH], BF16)
        s_pr = sb("s_pr", [128, 16])
        s_pc = sb("s_pc", [128, 10])
        s_cmb = sb("s_cmb", [128, 16])
        s_cmbT = sb("s_cmbT", [16, 128])
        s_gq = sb("s_gq", [128, 4])
        s_ones64 = sb("s_ones64c", [1, 64])

        _pt0 = ps("p_t0", [128, 128], BF16)
        p_t = [_pt0, _pt0]
        p_tf = ps("p_tf", [128, 128])
        p_S = [ps("p_S0", [128, CH]), ps("p_S1", [128, CH])]
        p_ctx = ps("p_ctx", [HD + 1, CH])
        p_mm = [ps("p_mm0", [128, CH]), ps("p_mm1", [128, CH])]
        p_bc = ps("p_bc", [128, CH])

        LD = P.sem("LD_C")
        LDC = P.sem("LDC_C")
        V, S, TE, DO = P.sem("V"), P.sem("S"), P.sem("TE"), P.sem("DO")
        TRC, SAV = P.sem("TRC"), P.sem("SAV")
        DX1 = P.sem("DX_C")
        nld = [0, 0]

        def load(out, in_, cast=False):
            if cast:
                P.dma("gpsimd", out, in_, inc=LDC)
                nld[1] += 1
            else:
                P.dma("sync", out, in_, inc=LD)
                nld[0] += 1

        load(s_w1[:], w1_in.rearrange("(k p) n -> p k n", p=128), cast=True)
        load(s_wo1[:], wo1_in.rearrange("(k p) n -> p k n", p=128), cast=True)
        load(s_gwT[:], gwT_in.rearrange("(k p) n -> p k n", p=128))
        load(s_sgwT[:], sgwT_in)
        load(s_flag[:], flag)
        load(s_tri[:], tri)
        load(s_idb[:], ident_b)
        load(s_idf[:], ident_f)
        LDA = [(LD, 16 * nld[0]), (LDC, 16 * nld[1])]

        P.op("vector", lambda e: e.memset(s_vr[0][:, :, :, HD:HD + 1], 1.0),
             wait=LDA)
        P.op("vector", lambda e: e.memset(s_vr[1][:, :, :, HD:HD + 1], 1.0))
        P.op("vector", lambda e: e.memset(s_ones64[:], 1.0))
        setup_v = (V, P.op("vector", lambda e: e.nop(), inc=V))

        tr_guard = {}
        dx_cnt = [0]
        state = {"te": 0, "v": 0}
        mm_free = [(V, 0), (V, 0)]
        S_free = [(S, 0), (S, 0)]
        eSb_guard = [0, 0]
        ctx_free = (V, 0)
        bc_free = (V, 0)
        tf_free = (V, 0)
        ring_guard = [(TE, 0), (TE, 0)]
        qf_guard = (TE, 0)
        attn_guard = (TE, 0)
        last_ao_dma = (DO, 0)
        last_xn1T_dma = (DO, 0)
        last_cmb_dma = (DO, 0)
        prev_S_ln = (S, 0)
        mm_turn = [0]

        def mm_bank():
            b = mm_turn[0] % 2
            mm_turn[0] += 1
            return b

        for c in range(n_own):
            slot = c % 2
            cq = c - 1
            wl = [(V, state["v"])] if c >= 1 else []
            if c >= 1:
                wl.append(last_ao_dma)
            dx_cnt[0] += 16
            P.dma("sync", s_x1[:], rearr_tok(x1_in, c * CH, CH),
                  wait=wl, inc=DX1)
            ldx = (DX1, dx_cnt[0])

            for i in range(4):
                w0 = [ldx] if i == 0 else []
                if i == 0 and c == 0:
                    w0.append(setup_v)
                vr0 = P.op("vector", lambda e, i=i: e.tensor_reduce(
                    out=s_lncol[:, 2 * i:2 * i + 1], in_=s_x1[:, i, :],
                    axis=AX.X, op=ALU.add), wait=w0, inc=V)
                vm = P.op("vector", lambda e, i=i: e.tensor_scalar(
                    out=s_lncol[:, 2 * i:2 * i + 1],
                    in0=s_lncol[:, 2 * i:2 * i + 1],
                    scalar1=1.0 / D, scalar2=None, op0=ALU.mult),
                    wait=[(V, vr0)], inc=V)
                wxc = [(V, vm)]
                if i == 0:
                    wxc.append(prev_S_ln)
                vv = P.op("vector", lambda e, i=i: e.tensor_scalar(
                    out=s_xn1[:, i, :], in0=s_x1[:, i, :],
                    scalar1=s_lncol[:, 2 * i:2 * i + 1], scalar2=None,
                    op0=ALU.subtract), wait=wxc, inc=V)
                ssq = P.op("scalar", lambda e, i=i: e.activation(
                    out=s_sq2[:], in_=s_xn1[:, i, :], func=AF.Square,
                    accum_out=s_lncol[:, 2 * i + 1:2 * i + 2]),
                    wait=[(V, vv)], inc=S)
                vv = P.op("vector", lambda e, i=i: e.tensor_scalar(
                    out=s_lncol[:, 2 * i + 1:2 * i + 2],
                    in0=s_lncol[:, 2 * i + 1:2 * i + 2],
                    scalar1=1.0 / D, scalar2=1e-5, op0=ALU.mult, op1=ALU.add),
                    wait=[(S, ssq)], inc=S if False else V)
                ss = P.op("scalar", lambda e, i=i: e.activation(
                    out=s_lncol[:, 2 * i + 1:2 * i + 2],
                    in_=s_lncol[:, 2 * i + 1:2 * i + 2], func=AF.Ln),
                    wait=[(V, vv)], inc=S)
                ss = P.op("scalar", lambda e, i=i: e.activation(
                    out=s_lncol[:, 2 * i + 1:2 * i + 2],
                    in_=s_lncol[:, 2 * i + 1:2 * i + 2], func=AF.Exp,
                    scale=-0.5), wait=[(S, ss)], inc=S)
                vv = P.op("vector", lambda e, i=i: e.tensor_scalar(
                    out=s_xn1[:, i, :], in0=s_xn1[:, i, :],
                    scalar1=s_lncol[:, 2 * i + 1:2 * i + 2], scalar2=None,
                    op0=ALU.mult), wait=[(S, ss)], inc=V)
                P.op("scalar", lambda e, i=i: e.activation(
                    out=s_xnb[:, i, :], in_=s_xn1[:, i, :], func=AF.Copy),
                    wait=[(V, vv)], inc=S)
            prev_S_ln = (S, P.count["S"])
            ln_s = (S, P.count["S"])

            tloc = []
            for i in range(4):
                for j in range(KT):
                    idx = i * KT + j
                    w = []
                    if idx == 0:
                        w.append(ln_s)
                        w.append(qf_guard)
                        if c >= 2:
                            w.append(last_xn1T_dma)
                    g = tr_guard.get(0)
                    if g is not None:
                        w.append((TRC, g))
                    tt = P.op("tensor", lambda e, i=i, j=j, idx=idx: e.transpose(
                        out=p_t[0][:],
                        in_=s_xnb[:, i, j * 128:(j + 1) * 128],
                        identity=s_idb[:]), wait=w, inc=TE)
                    cc = P.op("scalar", lambda e, i=i, j=j, idx=idx: e.activation(
                        out=s_xn1T[:, j, i * 128:(i + 1) * 128],
                        in_=p_t[0][:], func=AF.Identity),
                        wait=[(TE, tt)], inc=TRC)
                    tr_guard[0] = cc
                    tloc.append(cc)
            xn1T_done = (TRC, P.count["TRC"])
            if c >= 1:
                dd = P.dma("sync", xn1T_sc[cq], s_xn1T[:],
                           wait=[xn1T_done], inc=DO)
                last_xn1T_dma = (DO, dd)

            for part, coff in (("q", 0), ("k", D)):
                for hp in range(8):
                    b = mm_bank()
                    tt = None
                    for k in range(KT):
                        w = [xn1T_done, mm_free[b]] if k == 0 else []
                        if k == 0 and part == "k":
                            w.append(ring_guard[slot])
                        r = P.op("tensor", lambda e, k=k, hp=hp, b=b, coff=coff:
                                 e.matmul(out=p_mm[b][:],
                                          lhsT=s_w1[:, k, coff + hp * 128:coff + (hp + 1) * 128],
                                          rhs=s_xn1T[:, k, :],
                                          start=(k == 0), stop=(k == KT - 1)),
                                 wait=w, inc=TE if k == KT - 1 else None)
                        if r is not None:
                            tt = r
                    dst = s_qf if part == "q" else s_kr[slot]
                    vv = P.op("vector", lambda e, b=b, hp=hp, dst=dst: e.tensor_copy(
                        out=dst[:, hp, :], in_=p_mm[b][:]),
                        wait=[(TE, tt)] + ([qf_guard] if (part == "q" and hp == 0) else []),
                        inc=V)
                    mm_free[b] = (V, vv)
            for i in range(4):
                for n in range(2):
                    b = mm_bank()
                    tt = None
                    for k in range(KT):
                        w = [xn1T_done, mm_free[b]] if k == 0 else []
                        r = P.op("tensor", lambda e, i=i, n=n, k=k, b=b: e.matmul(
                            out=p_mm[b][:],
                            lhsT=s_xn1T[:, k, i * 128:(i + 1) * 128],
                            rhs=s_w1[:, k, 2 * D + n * CH:2 * D + (n + 1) * CH],
                            start=(k == 0), stop=(k == KT - 1)),
                            wait=w, inc=TE if k == KT - 1 else None)
                        if r is not None:
                            tt = r
                    w = [(TE, tt)]
                    if i == 0 and n == 0:
                        w.append(ring_guard[slot])
                    vv = P.op("scalar", lambda e, i=i, n=n, b=b, slot=slot:
                              e.activation(
                        out=s_vr[slot][:, i, 8 * n:8 * (n + 1), 0:HD],
                        in_=p_mm[b][:].rearrange("p (h d) -> p h d", d=HD),
                        func=AF.Identity), wait=w, inc=V)
                    mm_free[b] = (V, vv)
            kv_done = (V, P.count["V"])

            if c >= 1:
                for h in range(NH):
                    hp, half = h // 2, h % 2
                    r0 = 64 * half
                    ctx_started = False
                    for js in range(8):
                        ring = (c - 1 + js // 4) % 2
                        j = js % 4
                        eslot = js % 2
                        tt = P.op("tensor", lambda e, hp=hp, r0=r0, j=j,
                                  ring=ring, eslot=eslot: e.matmul(
                            out=p_S[eslot][:],
                            lhsT=s_kr[ring][r0:r0 + 64, hp, j * 128:(j + 1) * 128],
                            rhs=s_qf[r0:r0 + 64, hp, :],
                            start=True, stop=True),
                            wait=[S_free[eslot], kv_done] if js == 0 else [S_free[eslot]],
                            inc=TE)
                        wv = [(TE, tt)]
                        if eSb_guard[eslot] > 0:
                            wv.append((SAV, eSb_guard[eslot]))
                        masked = (js < 4 and c == 1) or js >= 4
                        if js < 4 and c == 1:
                            se = P.op("scalar", lambda e, eslot=eslot: e.activation(
                                out=s_eSf[:], in_=p_S[eslot][:], func=AF.Exp,
                                scale=0.125), wait=wv, inc=S)
                            S_free[eslot] = (S, se)
                            vv = P.op("vector", lambda e, eslot=eslot: e.tensor_scalar(
                                out=s_eSb[:, eslot, :], in0=s_eSf[:],
                                scalar1=s_flag[:, 0:1], scalar2=None,
                                op0=ALU.mult), wait=[(S, se)], inc=V)
                        elif js >= 4:
                            se = P.op("scalar", lambda e, eslot=eslot: e.activation(
                                out=s_eSf[:], in_=p_S[eslot][:], func=AF.Exp,
                                scale=0.125), wait=wv, inc=S)
                            S_free[eslot] = (S, se)
                            vv = P.op("vector", lambda e, eslot=eslot, j=j:
                                      e.tensor_tensor(
                                out=s_eSb[:, eslot, :], in0=s_eSf[:],
                                in1=s_tri[:, j, :], op=ALU.mult),
                                wait=[(S, se)], inc=V)
                        else:
                            se = P.op("scalar", lambda e, eslot=eslot: e.activation(
                                out=s_eSb[:, eslot, :], in_=p_S[eslot][:], func=AF.Exp,
                                scale=0.125), wait=wv, inc=S)
                            S_free[eslot] = (S, se)
                            vv = P.op("vector", lambda e: e.nop(),
                                      wait=[(S, se)], inc=V)
                        av = P.op("tensor", lambda e, h=h, j=j, ring=ring,
                                  eslot=eslot, st=(not ctx_started),
                                  sp=(js == 7): e.matmul(
                            out=p_ctx[:], lhsT=s_vr[ring][:, j, h, :],
                            rhs=s_eSb[:, eslot, :], start=st, stop=sp),
                            wait=[(V, vv)] + ([ctx_free] if not ctx_started else []),
                            inc=SAV)
                        eSb_guard[eslot] = av
                        ctx_started = True
                    ctx_t = (SAV, P.count["SAV"])
                    vv = P.op("vector", lambda e: e.tensor_copy(
                        out=s_dent[:], in_=p_ctx[HD:HD + 1, :]),
                        wait=[ctx_t], inc=V)
                    sd_ = P.op("scalar", lambda e: e.activation(
                        out=s_dscr[:], in_=s_dent[:], func=AF.Ln),
                        wait=[(V, vv)], inc=S)
                    sd_ = P.op("scalar", lambda e: e.activation(
                        out=s_drec[:], in_=s_dscr[:], func=AF.Exp, scale=-1.0),
                        wait=[(S, sd_)], inc=S)
                    tb = P.op("tensor", lambda e: e.matmul(
                        out=p_bc[0:64, :], lhsT=s_ones64[:], rhs=s_drec[:],
                        start=True, stop=True), wait=[(S, sd_), bc_free],
                        inc=TE, drain_inc=True)
                    vv = P.op("vector", lambda e: e.tensor_copy(
                        out=s_denb[:], in_=p_bc[0:64, :]), wait=[(TE, tb)], inc=V)
                    bc_free = (V, vv)
                    wat = [(V, vv)]
                    if h == 0:
                        wat.append(attn_guard)
                    av2 = P.op("vector", lambda e, hp=hp, r0=r0: e.tensor_tensor(
                        out=s_attn[r0:r0 + 64, hp, :], in0=p_ctx[0:HD, :],
                        in1=s_denb[:], op=ALU.mult), wait=wat, inc=V)
                    ctx_free = (V, av2)
                attn_done = (V, P.count["V"])
                last_ev = None
                for i in range(4):
                    for n in range(2):
                        b = mm_bank()
                        tt = None
                        for k in range(KT):
                            w = [attn_done, mm_free[b]] if k == 0 else []
                            r = P.op("tensor", lambda e, i=i, n=n, k=k, b=b: e.matmul(
                                out=p_mm[b][:],
                                lhsT=s_attn[:, k, i * 128:(i + 1) * 128],
                                rhs=s_wo1[:, k, n * CH:(n + 1) * CH],
                                start=(k == 0), stop=(k == KT - 1)),
                                wait=w, inc=TE if k == KT - 1 else None)
                            if r is not None:
                                tt = r
                        last_ev = P.op("vector", lambda e, i=i, n=n, b=b:
                                       e.tensor_copy(
                            out=s_x1[:, i, n * CH:(n + 1) * CH],
                            in_=p_mm[b][:]), wait=[(TE, tt)], inc=V)
                        mm_free[b] = (V, last_ev)
                attn_guard = (TE, P.count["TE"])
                dd = P.dma("sync", rearr_tok(ao1_sc, cq * CH, CH), s_x1[:],
                           wait=[(V, last_ev)], inc=DO)
                last_ao_dma = (DO, dd)

                for i in range(4):
                    lt = None
                    for j in range(KT):
                        w = [ln_s] if j == 0 else []
                        if j == 0:
                            w.append(tf_free)
                        if j >= 1:
                            w.append((TRC, lt))
                        ttf = P.op("tensor", lambda e, i=i, j=j: e.transpose(
                            out=p_tf[:], in_=s_xn1[:, i, j * 128:(j + 1) * 128],
                            identity=s_idf[:]), wait=w, inc=TE, drain_inc=True)
                        lt = P.op("vector", lambda e, j=j: e.tensor_copy(
                            out=s_xTf[:, j, :], in_=p_tf[:]),
                            wait=[(TE, ttf)], inc=TRC)
                    tf_free = (TRC, lt)
                    tl = None
                    for k in range(KT):
                        w = [(TRC, lt), bc_free] if k == 0 else []
                        r = P.op("tensor", lambda e, k=k: e.matmul(
                            out=p_bc[:, 0:16], lhsT=s_xTf[:, k, :],
                            rhs=s_gwT[:, k, :], start=(k == 0),
                            stop=(k == KT - 1)),
                            wait=w, inc=TE if k == KT - 1 else None)
                        if r is not None:
                            tl = r
                    tg2 = None
                    for k in range(KT):
                        r = P.op("tensor", lambda e, k=k: e.matmul(
                            out=p_bc[:, 16:17], lhsT=s_xTf[:, k, :],
                            rhs=s_sgwT[:, k:k + 1], start=(k == 0),
                            stop=(k == KT - 1)),
                            inc=TE if k == KT - 1 else None, drain_inc=True)
                        if r is not None:
                            tg2 = r
                    vv = P.op("vector", lambda e: e.tensor_copy(
                        out=s_pr[:], in_=p_bc[:, 0:16]), wait=[(TE, tl)], inc=V)
                    vv = P.op("vector", lambda e: e.tensor_reduce(
                        out=s_pc[:, 0:1], in_=s_pr[:], axis=AX.X, op=ALU.max),
                        wait=[(V, vv)], inc=V)
                    vv = P.op("vector", lambda e: e.tensor_scalar(
                        out=s_pc[:, 8:9], in0=s_pc[:, 0:1], scalar1=-1.0,
                        scalar2=None, op0=ALU.mult), wait=[(V, vv)], inc=V)
                    sv = P.op("scalar", lambda e: e.activation(
                        out=s_pr[:], in_=s_pr[:], func=AF.Exp,
                        bias=s_pc[:, 8:9]), wait=[(V, vv)], inc=S)
                    vv = P.op("vector", lambda e: e.tensor_reduce(
                        out=s_pc[:, 1:2], in_=s_pr[:], axis=AX.X, op=ALU.add),
                        wait=[(S, sv)], inc=V)
                    vv = P.recip(s_pc[:, 2:3], s_pc[:, 1:2], s_pc[:, 9:10],
                                 wait=[(V, vv)], inc=V)
                    vv = P.op("vector", lambda e: e.tensor_scalar(
                        out=s_pr[:], in0=s_pr[:], scalar1=s_pc[:, 2:3],
                        scalar2=None, op0=ALU.mult), wait=[(V, vv)], inc=V)
                    vv = P.op("vector", lambda e: e.tensor_reduce(
                        out=s_pc[:, 3:4], in_=s_pr[:], axis=AX.X, op=ALU.max),
                        wait=[(V, vv)], inc=V)
                    vv = P.op("vector", lambda e: e.tensor_scalar(
                        out=s_cmb[:], in0=s_pr[:], scalar1=s_pc[:, 3:4],
                        scalar2=None, op0=ALU.is_equal), wait=[(V, vv)], inc=V)
                    vv = P.op("vector", lambda e: e.scalar_tensor_tensor(
                        out=s_pr[:], in0=s_cmb[:], scalar=-2.0, in1=s_pr[:],
                        op0=ALU.mult, op1=ALU.add), wait=[(V, vv)], inc=V)
                    vv = P.op("vector", lambda e: e.tensor_reduce(
                        out=s_pc[:, 4:5], in_=s_pr[:], axis=AX.X, op=ALU.max),
                        wait=[(V, vv)], inc=V)
                    vv = P.op("vector", lambda e: e.tensor_add(
                        out=s_pc[:, 5:6], in0=s_pc[:, 3:4], in1=s_pc[:, 4:5]),
                        wait=[(V, vv)], inc=V)
                    vv = P.recip(s_pc[:, 6:7], s_pc[:, 5:6], s_pc[:, 9:10],
                                 wait=[(V, vv)], inc=V)
                    vv = P.op("vector", lambda e: e.tensor_mul(
                        out=s_pc[:, 7:8], in0=s_pc[:, 4:5], in1=s_pc[:, 6:7]),
                        wait=[(V, vv)], inc=V)
                    vv = P.op("vector", lambda e: e.tensor_mul(
                        out=s_pc[:, 6:7], in0=s_pc[:, 3:4], in1=s_pc[:, 6:7]),
                        wait=[(V, vv)], inc=V)
                    vv = P.op("vector", lambda e: e.tensor_scalar(
                        out=s_cmb[:], in0=s_cmb[:], scalar1=s_pc[:, 6:7],
                        scalar2=None, op0=ALU.mult), wait=[(V, vv)], inc=V)
                    vv = P.op("vector", lambda e: e.tensor_scalar(
                        out=s_pr[:], in0=s_pr[:], scalar1=s_pc[:, 4:5],
                        scalar2=None, op0=ALU.is_equal), wait=[(V, vv)], inc=V)
                    vv = P.op("vector", lambda e: e.scalar_tensor_tensor(
                        out=s_cmb[:], in0=s_pr[:], scalar=s_pc[:, 7:8],
                        in1=s_cmb[:], op0=ALU.mult, op1=ALU.add),
                        wait=[(V, vv)], inc=V)
                    ttf = P.op("tensor", lambda e: e.transpose(
                        out=p_tf[0:16, :], in_=s_cmb[:], identity=s_idf[:]),
                        wait=[(V, vv), tf_free], inc=TE, drain_inc=True)
                    vv = P.op("vector", lambda e: e.tensor_copy(
                        out=s_cmbT[:], in_=p_tf[0:16, :]),
                        wait=[(TE, ttf), last_cmb_dma], inc=V)
                    tf_free = (V, vv)
                    dd1 = P.dma("sync", cmbT_sc[cq, i], s_cmbT[:],
                                wait=[(V, vv)], inc=DO)
                    last_cmb_dma = (DO, dd1)
                    sv = P.op("scalar", lambda e, i=i: e.activation(
                        out=s_gq[:, i:i + 1], in_=p_bc[:, 16:17], func=AF.Exp,
                        scale=-1.0), wait=[(TE, tg2)], inc=S)
                    vv = P.op("vector", lambda e, i=i: e.tensor_scalar(
                        out=s_gq[:, i:i + 1], in0=s_gq[:, i:i + 1],
                        scalar1=1.0, scalar2=None, op0=ALU.add),
                        wait=[(S, sv)], inc=V)
                    vv = P.op("vector", lambda e, i=i: e.reciprocal(
                        out=s_gq[:, i:i + 1], in_=s_gq[:, i:i + 1]),
                        wait=[(V, vv)], inc=V)
                    bc_free = (V, vv)
                dd2 = P.dma("sync", gate_sc[cq], s_gq[:],
                            wait=[(V, P.count["V"])], inc=DO)
            else:
                attn_done = (V, P.count["V"])
            ring_guard[(c - 1) % 2] = (TE, P.count["TE"])
            qf_guard = (TE, P.count["TE"])
            state["v"] = P.count["V"]
        if standalone:
            P.op("sync", lambda e: e.nop(), wait=[(DO, P.count["DO"])])
            P.build(es)
        scr2 = dict(scr or {})
        scr2.update(ao1_sc=ao1_sc, xn1T_sc=xn1T_sc, cmbT_sc=cmbT_sc,
                    gate_sc=gate_sc)
    return nc if standalone else scr2


def stage_c_inputs(inputs, core, n_own=9):
    import ml_dtypes
    KTC = 8
    flagv = np.full((128, 1), float(core % 2), np.float32)
    tri = np.zeros((128, 4, CH), np.float32)
    ar = np.arange(CH)
    for j in range(4):
        for p in range(128):
            tri[p, j, :] = (j * 128 + p <= ar)
    return dict(
        w1_in=np.asarray(inputs["L1_attn_w_qkv"], np.float32),
        wo1_in=np.asarray(inputs["L1_attn_w_out"], np.float32),
        gwT_in=np.ascontiguousarray(np.asarray(inputs["L1_ffn_gate_w"], np.float32).T),
        sgwT_in=np.ascontiguousarray(np.asarray(inputs["L1_ffn_shared_gate_w"], np.float32).reshape(KTC, 128).T),
        flag=flagv, tri=tri,
        ident_b=np.eye(128, dtype=np.float32).astype(ml_dtypes.bfloat16),
        ident_f=np.eye(128, dtype=np.float32))




def build_phase_d1(n_own=9, chain="ve", ctx=None, scr=None):
    standalone = ctx is None
    if standalone:
        nc = bass.Bass("TRN2", target_bir_lowering=False, debug=False)
    else:
        nc, P = ctx
    NQ = n_own - 1
    TQ = NQ * CH
    I = 2816
    IT = I // 128
    if standalone:
        xn1T_in = nc.dram_tensor("xn1T_in", [NQ, 128, KT, CH], BF16, kind="ExternalInput").ap()
        x1_in = nc.dram_tensor("x1_in", [n_own * CH, D], F32, kind="ExternalInput").ap()
        ao1_in = nc.dram_tensor("ao1_in", [TQ, D], F32, kind="ExternalInput").ap()
        gate_in = nc.dram_tensor("gate_in", [NQ, 128, 4], F32, kind="ExternalInput").ap()
    else:
        xn1T_in, x1_in = scr["xn1T_sc"], scr["x1_sc"]
        ao1_in, gate_in = scr["ao1_sc"], scr["gate_sc"]
    swgT_in = nc.dram_tensor("swgT_in", [D, I], F32, kind="ExternalInput").ap()
    swuT_in = nc.dram_tensor("swuT_in", [D, I], F32, kind="ExternalInput").ap()
    swdT_in = nc.dram_tensor("swdT_in", [I, D], F32, kind="ExternalInput").ap()
    po_sc = nc.dram_tensor(
        "po_sc", [TQ, D], F32,
        kind="ExternalOutput" if standalone else "Internal").ap()

    if standalone:
        P = Prog(nc, chain=chain)
    es = ExitStack()
    with es:
        sb = lambda n, s, d=F32: es.enter_context(nc.sbuf_tensor('pd_' + n, s, d))
        ps = lambda n, s, d=F32: es.enter_context(nc.psum_tensor('pd_' + n, s, d))
        s_swg = sb("s_swg", [128, KT, I], BF16)
        s_swu = sb("s_swu", [128, KT, I], BF16)
        s_swd = sb("s_swd", [128, IT, D], BF16)
        s_xT = sb("s_xT", [128, KT, CH], BF16)
        s_sh = sb("s_sh", [128, IT, CH], BF16)
        s_sg = sb("s_sg", [128, CH])
        s_x1 = sb("s_x1", [128, 4, D])
        s_ao = sb("s_ao", [128, 4, D])
        s_gt = sb("s_gt", [128, 4])
        p_g = ps("p_g", [128, CH])
        p_u = ps("p_u", [128, CH])
        p_fo = ps("p_fo", [128, CH])

        LDC = P.sem("LDC_D1")
        V, S, TE, DO = P.sem("V"), P.sem("S"), P.sem("TE"), P.sem("DO")
        DX = P.sem("DX_D1")
        dxc = [0]

        P.dma("gpsimd", s_swg[:], swgT_in.rearrange("(k p) n -> p k n", p=128), inc=LDC)
        P.dma("gpsimd", s_swu[:], swuT_in.rearrange("(k p) n -> p k n", p=128), inc=LDC)
        P.dma("gpsimd", s_swd[:], swdT_in.rearrange("(k p) n -> p k n", p=128), inc=LDC)
        LDA = [(LDC, 48)]

        mm_free_g = (S, 0)
        mm_free_u = (V, 0)
        fo_free = (V, 0)
        last_po_dma = (DO, 0)
        state_v = 0
        state_te = 0
        for cq in range(NQ):
            wl = [(TE, state_te), (V, state_v)] if cq else []
            dxc[0] += 16
            P.dma("sync", s_xT[:], xn1T_in[cq], wait=wl, inc=DX)
            dxc[0] += 16
            P.dma("sync", s_x1[:], rearr_tok(x1_in, (cq + 1) * CH, CH), inc=DX)
            dxc[0] += 16
            P.dma("sync", s_ao[:], rearr_tok(ao1_in, cq * CH, CH), inc=DX)
            dxc[0] += 16
            P.dma("sync", s_gt[:], gate_in[cq], inc=DX)
            ldx = (DX, dxc[0])
            for m in range(IT):
                tg = tu = None
                for k in range(KT):
                    w = ([ldx] + LDA if (cq == 0 and m == 0) else [ldx]) if k == 0 else []
                    if k == 0:
                        w = w + [mm_free_g]
                    r = P.op("tensor", lambda e, k=k, m=m: e.matmul(
                        out=p_g[:], lhsT=s_swg[:, k, m * 128:(m + 1) * 128],
                        rhs=s_xT[:, k, :], start=(k == 0), stop=(k == KT - 1)),
                        wait=w, inc=TE if k == KT - 1 else None)
                    if r is not None:
                        tg = r
                for k in range(KT):
                    w = [mm_free_u] if k == 0 else []
                    r = P.op("tensor", lambda e, k=k, m=m: e.matmul(
                        out=p_u[:], lhsT=s_swu[:, k, m * 128:(m + 1) * 128],
                        rhs=s_xT[:, k, :], start=(k == 0), stop=(k == KT - 1)),
                        wait=w, inc=TE if k == KT - 1 else None)
                    if r is not None:
                        tu = r
                ssg = P.op("scalar", lambda e: e.activation(
                    out=s_sg[:], in_=p_g[:], func=AF.Silu),
                    wait=[(TE, tg), mm_free_u], inc=S)
                mm_free_g = (S, ssg)
                vh = P.op("vector", lambda e, m=m: e.tensor_tensor(
                    out=s_sh[:, m, :], in0=p_u[:], in1=s_sg[:],
                    op=ALU.mult), wait=[(S, ssg), (TE, tu)], inc=V)
                mm_free_u = (V, vh)
            sh_done = (V, P.count["V"])
            lastv = None
            for i in range(4):
                for n2 in range(2):
                    tf = None
                    for k in range(IT):
                        w = [sh_done, fo_free] if k == 0 else []
                        r = P.op("tensor", lambda e, k=k, i=i, n2=n2: e.matmul(
                            out=p_fo[:],
                            lhsT=s_sh[:, k, i * 128:(i + 1) * 128],
                            rhs=s_swd[:, k, n2 * CH:(n2 + 1) * CH],
                            start=(k == 0), stop=(k == IT - 1)),
                            wait=w, inc=TE if k == IT - 1 else None)
                        if r is not None:
                            tf = r
                    vv = P.op("vector", lambda e, i=i, n2=n2: e.tensor_scalar(
                        out=s_sg[:], in0=p_fo[:], scalar1=s_gt[:, i:i + 1],
                        scalar2=None, op0=ALU.mult),
                        wait=[(TE, tf)], inc=V)
                    fo_free = (V, vv)
                    vv = P.op("vector", lambda e, i=i, n2=n2: e.tensor_tensor(
                        out=s_ao[:, i, n2 * CH:(n2 + 1) * CH], in0=s_sg[:],
                        in1=s_ao[:, i, n2 * CH:(n2 + 1) * CH], op=ALU.add),
                        wait=[last_po_dma] if (i == 0 and n2 == 0) else [], inc=V)
                    lastv = P.op("vector", lambda e, i=i, n2=n2: e.tensor_add(
                        out=s_ao[:, i, n2 * CH:(n2 + 1) * CH],
                        in0=s_ao[:, i, n2 * CH:(n2 + 1) * CH],
                        in1=s_x1[:, i, n2 * CH:(n2 + 1) * CH]), inc=V)
            dd = P.dma("sync", rearr_tok(po_sc, cq * CH, CH), s_ao[:],
                       wait=[(V, lastv)], inc=DO)
            last_po_dma = (DO, dd)
            state_v = P.count["V"]
            state_te = P.count["TE"]
        if standalone:
            P.op("sync", lambda e: e.nop(), wait=[(DO, P.count["DO"])])
            P.build(es)
        scr2 = dict(scr or {})
        scr2["po_sc"] = po_sc
    return nc if standalone else scr2


def build_phase_d2(n_own=9, chain="ve", ctx=None, scr=None):
    standalone = ctx is None
    if standalone:
        nc = bass.Bass("TRN2", target_bir_lowering=False, debug=False)
    else:
        nc, P = ctx
    NQ = n_own - 1
    TQ = NQ * CH
    NHALF = NQ // 2
    if standalone:
        xn1T_in = nc.dram_tensor("xn1T_in", [NQ, 128, KT, CH], BF16, kind="ExternalInput").ap()
        cmbT_in = nc.dram_tensor("cmbT_in", [NQ, 4, 16, 128], F32, kind="ExternalInput").ap()
        po_in = nc.dram_tensor("po_in", [TQ, D], F32, kind="ExternalInput").ap()
    else:
        xn1T_in, cmbT_in, po_in = scr["xn1T_sc"], scr["cmbT_sc"], scr["po_sc"]
    wgu_in = nc.dram_tensor("wgu16_in", [16, D, D], F32, kind="ExternalInput").ap()
    wd_in = nc.dram_tensor("wd16_in", [16, CH, D], F32, kind="ExternalInput").ap()
    out_sc = nc.dram_tensor("out_sc", [TQ, D], F32, kind="ExternalOutput").ap()

    if standalone:
        P = Prog(nc, chain=chain)
    es = ExitStack()
    with es:
        sb = lambda n, s, d=F32: es.enter_context(nc.sbuf_tensor('pe_' + n, s, d))
        ps = lambda n, s, d=F32: es.enter_context(nc.psum_tensor('pe_' + n, s, d))
        s_xh = sb("s_xh", [128, KT, NHALF * CH], BF16)
        s_acc = sb("s_acc", [128, NHALF * 4, D])
        s_wgu = [sb("s_wgu0", [128, KT, D], BF16), sb("s_wgu1", [128, KT, D], BF16)]
        s_wd = [sb("s_wd0", [128, 4, D], BF16), sb("s_wd1", [128, 4, D], BF16)]
        s_ce = sb("s_ce", [1, NHALF * CH])
        s_cb = sb("s_cb", [128, CH])
        s_sg = sb("s_sg", [128, 2, CH])
        s_hec = sb("s_hec", [128, 4, CH], BF16)
        s_ones = sb("s_ones1", [1, 128])
        s_po = sb("s_po", [128, 4, D])
        p_g = [ps("p_g0", [128, CH]), ps("p_g1", [128, CH])]
        p_u = [ps("p_u0", [128, CH]), ps("p_u1", [128, CH])]
        p_cb = ps("p_cb", [128, CH])
        p_o = [ps("p_o0", [128, CH]), ps("p_o1", [128, CH])]

        V, S, TE, DO = P.sem("V"), P.sem("S"), P.sem("TE"), P.sem("DO")
        DX = P.sem("DX_D2")
        DW = [P.sem("DW0_D2"), P.sem("DW1_D2")]
        DC = P.sem("DC_D2")
        dxc, dwc, dcc = [0], [0, 0], [0]
        state_v = 0
        state_te = 0
        mm_free_g = [(S, 0), (S, 0)]
        mm_free_u = [(V, 0), (V, 0)]
        cb_free = (V, 0)
        sg_free = [(V, 0), (V, 0)]
        o_free = [(V, 0), (V, 0)]
        hec_guard = (TE, 0)
        last_out_dma = (DO, 0)

        vo = P.op("vector", lambda e: e.memset(s_ones[:], 1.0), inc=V)
        ones_ready = (V, vo)

        for h2 in range(2):
            w = [(TE, state_te)] if h2 else []
            for q in range(NHALF):
                dxc[0] += 16
                P.dma("sync", s_xh[:, :, q * CH:(q + 1) * CH],
                      xn1T_in[h2 * NHALF + q], wait=w if q == 0 else [],
                      inc=DX)
            ldx = (DX, dxc[0])
            vz = P.op("vector", lambda e: e.memset(s_acc[:], 0.0),
                      wait=[(V, state_v)], inc=V)
            acc_ready = (V, vz)
            for e_ in range(16):
                buf = e_ % 2
                wld = [hec_guard] if (e_ >= 2 or h2) else []
                dwc[buf] += 16
                P.dma("gpsimd", s_wgu[buf][:],
                      wgu_in[e_].rearrange("(k p) n -> p k n", p=128),
                      wait=wld, inc=DW[buf])
                dwc[buf] += 16
                P.dma("gpsimd", s_wd[buf][:],
                      wd_in[e_].rearrange("(k p) n -> p k n", p=128),
                      inc=DW[buf])
                ldw = (DW[buf], dwc[buf])
                dcc[0] += 16
                P.dma("sync",
                      s_ce[:].rearrange("o (q i p) -> o q i p", q=NHALF, i=4),
                      cmbT_in[h2 * NHALF:(h2 + 1) * NHALF, :, e_, :].unsqueeze(0),
                      wait=[cb_free], inc=DC)
                ldc = (DC, dcc[0])
                for tg in range(NHALF):
                    tb = P.op("tensor", lambda e, tg=tg: e.matmul(
                        out=p_cb[:], lhsT=s_ones[:],
                        rhs=s_ce[0:1, tg * CH:(tg + 1) * CH],
                        start=True, stop=True),
                        wait=[ldc, cb_free, ones_ready], inc=TE,
                        drain_inc=True)
                    vcb = P.op("vector", lambda e: e.tensor_copy(
                        out=s_cb[:], in_=p_cb[:]), wait=[(TE, tb)], inc=V)
                    cb_free = (V, vcb)
                    for mg in range(4):
                        gb = mg % 2
                        tgm = tum = None
                        for k in range(KT):
                            w = [ldx, ldw, mm_free_g[gb]] if k == 0 else []
                            r = P.op("tensor", lambda e, k=k, mg=mg, tg=tg, buf=buf,
                                     gb=gb: e.matmul(
                                out=p_g[gb][:],
                                lhsT=s_wgu[buf][:, k, mg * 128:(mg + 1) * 128],
                                rhs=s_xh[:, k, tg * CH:(tg + 1) * CH],
                                start=(k == 0), stop=(k == KT - 1)),
                                wait=w, inc=TE if k == KT - 1 else None)
                            if r is not None:
                                tgm = r
                        for k in range(KT):
                            w = [mm_free_u[gb]] if k == 0 else []
                            r = P.op("tensor", lambda e, k=k, mg=mg, tg=tg, buf=buf,
                                     gb=gb: e.matmul(
                                out=p_u[gb][:],
                                lhsT=s_wgu[buf][:, k, CH + mg * 128:CH + (mg + 1) * 128],
                                rhs=s_xh[:, k, tg * CH:(tg + 1) * CH],
                                start=(k == 0), stop=(k == KT - 1)),
                                wait=w, inc=TE if k == KT - 1 else None)
                            if r is not None:
                                tum = r
                        ssg = P.op("scalar", lambda e, gb=gb, mg=mg: e.activation(
                            out=s_sg[:, mg % 2, :], in_=p_g[gb][:], func=AF.Silu),
                            wait=[(TE, tgm), sg_free[mg % 2]], inc=S)
                        mm_free_g[gb] = (S, ssg)
                        vh = P.op("vector", lambda e, gb=gb, mg=mg: e.tensor_tensor(
                            out=s_sg[:, mg % 2, :], in0=p_u[gb][:],
                            in1=s_sg[:, mg % 2, :],
                            op=ALU.mult), wait=[(S, ssg), (TE, tum)], inc=V)
                        mm_free_u[gb] = (V, vh)
                        vh = P.op("vector", lambda e, mg=mg: e.tensor_tensor(
                            out=s_hec[:, mg, :], in0=s_sg[:, mg % 2, :],
                            in1=s_cb[:], op=ALU.mult),
                            wait=[(V, vcb)] + ([hec_guard] if mg == 0 else []),
                            inc=V)
                        sg_free[mg % 2] = (V, vh)
                    hec_done = (V, P.count["V"])
                    for tt in range(4):
                        for n2 in range(2):
                            ob = (tt * 2 + n2) % 2
                            tf = None
                            for k in range(4):
                                w = [hec_done, o_free[ob]] if k == 0 else []
                                r = P.op("tensor", lambda e, k=k, tt=tt, n2=n2,
                                         ob=ob, buf=buf: e.matmul(
                                    out=p_o[ob][:],
                                    lhsT=s_hec[:, k, tt * 128:(tt + 1) * 128],
                                    rhs=s_wd[buf][:, k, n2 * CH:(n2 + 1) * CH],
                                    start=(k == 0), stop=(k == 3)),
                                    wait=w, inc=TE if k == 3 else None)
                                if r is not None:
                                    tf = r
                            vv = P.op("vector", lambda e, tg=tg, tt=tt, n2=n2,
                                      ob=ob: e.tensor_tensor(
                                out=s_acc[:, tg * 4 + tt, n2 * CH:(n2 + 1) * CH],
                                in0=p_o[ob][:],
                                in1=s_acc[:, tg * 4 + tt, n2 * CH:(n2 + 1) * CH],
                                op=ALU.add),
                                wait=[(TE, tf)] + ([acc_ready] if (e_ == 0 and tg == 0 and tt == 0 and n2 == 0) else []),
                                inc=V)
                            o_free[ob] = (V, vv)
                    hec_guard = (TE, P.count["TE"])
            acc_done = (V, P.count["V"])
            for q in range(NHALF):
                dxc[0] += 16
                dpo = P.dma("sync", s_po[:],
                            rearr_tok(po_in, (h2 * NHALF + q) * CH, CH),
                            wait=[(V, state_v)] if False else [], inc=DX)
                lastv = None
                for i in range(4):
                    lastv = P.op("vector", lambda e, q=q, i=i: e.tensor_tensor(
                        out=s_po[:, i, :], in0=s_po[:, i, :],
                        in1=s_acc[:, q * 4 + i, :], op=ALU.add),
                        wait=[(DX, dxc[0]), acc_done] if i == 0 else [],
                        inc=V)
                dd = P.dma("sync", rearr_tok(out_sc, (h2 * NHALF + q) * CH, CH),
                           s_po[:], wait=[(V, lastv)], inc=DO)
                last_out_dma = (DO, dd)
            state_v = P.count["V"]
            state_te = P.count["TE"]
        if standalone:
            P.op("sync", lambda e: e.nop(), wait=[(DO, P.count["DO"])])
            P.build(es)
    return nc


def _assemble_fused(n_pre, n_own, chain, plan=None):
    nc = bass.Bass("TRN2", target_bir_lowering=False, debug=False)
    P = Prog(nc, chain=chain)
    P.plan = plan
    ctx = (nc, P)
    scr = build_phase_a(n_pre=n_pre, n_own=n_own, ctx=ctx)
    P.barrier()
    scr = build_phase_b(n_own=n_own, ctx=ctx, scr=scr)
    P.barrier()
    scr = build_phase_c(n_own=n_own, ctx=ctx, scr=scr)
    P.barrier()
    scr = build_phase_d1(n_own=n_own, ctx=ctx, scr=scr)
    P.barrier()
    build_phase_d2(n_own=n_own, ctx=ctx, scr=scr)
    P.op("sync", lambda e: e.nop(),
         wait=[(P.sem("DO"), P.count["DO"]),
               (P.sem("DO_B"), P.count["DO_B"])])
    es = ExitStack()
    with es:
        P.build(es)
    return nc, P


def build_fused(n_pre=7, n_own=9, chain="sel"):
    if chain == "sel":
        _, P1 = _assemble_fused(n_pre, n_own, "analyze")
        plan = P1.compute_plan()
        nc, P = _assemble_fused(n_pre, n_own, "sel", plan=plan)
        P.plan_stats = P1.plan_stats
        return nc
    nc, _ = _assemble_fused(n_pre, n_own, chain)
    return nc


def _stage_d1_inputs(inputs):
    return dict(
        swgT_in=np.ascontiguousarray(np.asarray(inputs["L1_ffn_shared_w_gate"], np.float32).T),
        swuT_in=np.ascontiguousarray(np.asarray(inputs["L1_ffn_shared_w_up"], np.float32).T),
        swdT_in=np.ascontiguousarray(np.asarray(inputs["L1_ffn_shared_w_down"], np.float32).T))


def _stage_fused(inputs, core):
    m = {}
    m.update(stage_a_inputs(inputs, core))
    m.update(stage_b_inputs(inputs, core))
    mc = stage_c_inputs(inputs, core)
    for k in ("w1_in", "wo1_in", "gwT_in", "sgwT_in", "ident_f"):
        m[k] = mc[k]
    m.update(_stage_d1_inputs(inputs))
    m["wgu16_in"] = np.asarray(inputs["L1_ffn_w_gate_up"], np.float32)
    m["wd16_in"] = np.asarray(inputs["L1_ffn_w_down"], np.float32)
    return m


def kernel(**inputs):
    cores = list(range(NCORES))
    nc = build_fused()
    im = [_stage_fused(inputs, c) for c in cores]
    res = run_bass_kernel_spmd(nc, im, core_ids=cores).results
    out = np.zeros((4, 8192, D), np.float32)
    for c in cores:
        b, half = c // 2, c % 2
        out[b, half * 4096:(half + 1) * 4096] = res[c]["out_sc"]
    return out


# version 16 (final)
# speedup vs baseline: 1.2026x; 1.0157x over previous
import numpy as np
from contextlib import ExitStack

import concourse.bass as bass
import concourse.mybir as mybir
from concourse import dve_ops
from concourse.bass_utils import run_bass_kernel_spmd

F32 = mybir.dt.float32
BF16 = mybir.dt.bfloat16
AF = mybir.ActivationFunctionType
ALU = mybir.AluOpType
AX = mybir.AxisListType

NCORES = 8
D = 1024
CH = 512
NH = 16
HD = 64
KT = D // 128


_DT_SIZE = {"float32": 4, "int32": 4, "uint32": 4, "bfloat16": 2,
            "float16": 2, "int16": 2, "uint16": 2, "int8": 1, "uint8": 1,
            "float8_e4m3": 1, "float8_e5m2": 1, "float64": 8}


def _pap_span(pap):
    try:
        name = pap.memref
        esz = _DT_SIZE.get(str(pap.dtype).split(".")[-1], 8)
        off = int(pap.offset)
        pairs = [(int(p[0]), int(p[1])) for p in pap.ap]
        if not pairs:
            return (name, 0, 1 << 24, 0, 1 << 50, 0, 0, 0)
        ps, pn = pairs[0]
        rest = pairs[1:]
        neg = sum(min(0, s * (n - 1)) for s, n in rest)
        pos = sum(max(0, s * (n - 1)) for s, n in rest)
        if ps > 0:
            f0 = off % ps
            if f0 + neg >= 0 and f0 + pos < ps:
                p0 = off // ps
                st = re_ = w = 0
                if len(rest) == 1 and rest[0][0] > 1:
                    st = rest[0][0] * esz
                    re_ = (f0 * esz) % st
                    w = esz
                return (name, p0, p0 + pn - 1,
                        (f0 + neg) * esz, (f0 + pos + 1) * esz, st, re_, w)
        return (name, 0, 1 << 24, 0, 1 << 50, 0, 0, 0)
    except Exception:
        return None


def _op_spans(ins):
    try:
        bir = ins.ins
        opname = type(bir).__name__.lower()
        if "accum" in opname:
            return None
        rd, wr = [], []
        for a in bir.ins:
            if getattr(a, "memref", None) is None:
                continue
            s = _pap_span(a)
            if s is None:
                return None
            rd.append(s)
        for a in bir.outs:
            if getattr(a, "memref", None) is None:
                continue
            s = _pap_span(a)
            if s is None:
                return None
            wr.append(s)
        return (rd, wr, False)
    except Exception:
        return None


def _spans_overlap(spans, pending):
    for q in pending:
        if q is None:
            return bool(spans)
        for s in spans:
            if not (s[0] == q[0] and s[1] <= q[2] and q[1] <= s[2]
                    and s[3] < q[4] and q[3] < s[4]):
                continue
            if s[5] and s[5] == q[5]:
                st = s[5]
                d = (s[6] - q[6]) % st
                if d >= q[7] and st - d >= s[7]:
                    continue
            return True
    return False


class Prog:

    ENGINES = ("sync", "vector", "scalar", "tensor", "gpsimd")

    def __init__(self, nc, chain="ve"):
        self.nc = nc
        self.chain = chain
        self.ops = {e: [] for e in self.ENGINES}
        self.count = {}
        self.record = None
        self.plan = None
        self.plan_stats = None

    def sem(self, name):
        self.count.setdefault(name, 0)
        return name

    def op(self, engine, fn, wait=(), inc=None, inc_amt=1, is_dma=False,
           drain_inc=False, nodrain=False, bar=False):
        if inc is not None:
            self.sem(inc)
            self.count[inc] += inc_amt
        self.ops[engine].append(
            (fn, [(s, v) for (s, v) in wait if v > 0], inc, inc_amt,
             is_dma, drain_inc, nodrain,
             self.count[inc] if inc is not None else None, bar))
        return self.count[inc] if inc is not None else None

    def recip(self, out, in_, scratch, wait=(), inc=None):
        return self.op("vector", lambda e: e.reciprocal(out=out, in_=in_),
                       wait=wait, inc=inc)

    def barrier(self):
        snap = [(s, v) for s, v in self.count.items() if v > 0]
        for e in self.ENGINES:
            self.op(e, lambda en: en.nop(), wait=snap, bar=True)

    def dma(self, engine, out, in_, wait=(), inc=None):
        return self.op(engine, lambda e: e.dma_start(out=out, in_=in_),
                       wait=wait, inc=inc, inc_amt=16, is_dma=True)

    def build(self, es):
        drain_engines = ()
        if self.chain == "all":
            run = {n: 0 for n in self.count}
            for ename in self.ENGINES:
                self.sem("SELF_" + ename)
                run.setdefault("SELF_" + ename, 0)
                prev = None
                out = []
                for fn, waits, inc, amt, is_dma, di, nd, cum, bar in self.ops[ename]:
                    if inc is None:
                        inc = "SELF_" + ename
                        self.count[inc] = self.count.get(inc, 0) + 1
                        amt = 1
                    run[inc] = run.get(inc, 0) + amt
                    if prev is not None:
                        waits = waits + [prev]
                    out.append((fn, waits, inc, amt, is_dma, di, nd, cum, bar))
                    prev = (inc, run[inc])
                self.ops[ename] = out
        elif self.chain == "ve":
            drain_engines = ("vector", "scalar")
        analyze = self.chain == "analyze"
        sel = self.chain == "sel"
        if analyze:
            self.record = {e: [] for e in self.ENGINES}
        sems = {n: es.enter_context(self.nc.semaphore(n)) for n in self.count}
        block = es.enter_context(self.nc.Block())
        for ename in self.ENGINES:
            stream = self.ops[ename]
            use_drain = ename in drain_engines
            drain_always = use_drain and ename in ("vector", "scalar")

            chained = bool(self.chain) and self.chain != "all"

            def make(fstream, use_drain, drain_always, ename=ename):
                sel_ve = sel and ename in ("vector", "scalar")
                if sel_ve:
                    drain_before, drain_inc_set = self.plan[ename]
                rec = self.record[ename] if analyze else None

                def body(eng):
                    for i, (fn, waits, inc, amt, is_dma, di, nd, cum, bar) \
                            in enumerate(fstream):
                        if sel_ve and i in drain_before:
                            eng.drain(fusable=True)
                        for s, v in waits:
                            eng.wait_ge(sems[s], v)
                        ins = fn(eng)
                        if rec is not None:
                            rec.append(_op_spans(ins))
                        if is_dma:
                            if inc is not None:
                                ins.then_inc(sems[inc], amt)
                        elif sel_ve:
                            if inc is not None:
                                if i in drain_inc_set:
                                    eng.drain(fusable=True).then_inc(
                                        sems[inc], amt)
                                else:
                                    ins.then_inc(sems[inc], amt)
                        elif analyze:
                            if inc is not None:
                                ins.then_inc(sems[inc], amt)
                        elif nd and inc is None:
                            pass
                        elif use_drain and (drain_always or inc is not None):
                            d = eng.drain(fusable=True)
                            if inc is not None:
                                d.then_inc(sems[inc], amt)
                        elif chained and di and inc is not None:
                            eng.drain(fusable=False).then_inc(sems[inc], amt)
                        elif inc is not None:
                            ins.then_inc(sems[inc], amt)
                return body

            getattr(block, ename)(make(stream, use_drain, drain_always))

    def compute_plan(self):
        waited = {}
        inc_engines = {}
        for ename in self.ENGINES:
            for (fn, waits, inc, amt, is_dma, di, nd, cum, bar) in self.ops[ename]:
                for s, v in waits:
                    waited.setdefault(s, set()).add(v)
                if inc is not None:
                    inc_engines.setdefault(inc, set()).add(ename)
        mixed = {s for s, es_ in inc_engines.items() if len(es_) > 1}
        plan = {}
        stats = {}
        for ename in ("vector", "scalar"):
            rec = self.record[ename]
            ops = self.ops[ename]
            drain_before, drain_inc_set = set(), set()
            pending = []
            for i, (op_t, info) in enumerate(zip(ops, rec)):
                fn, waits, inc, amt, is_dma, di, nd, cum, bar = op_t
                if is_dma:
                    continue
                if bar:
                    if pending:
                        drain_before.add(i)
                        pending = []
                    continue
                if info is None:
                    if pending and not nd:
                        drain_before.add(i)
                    pending = [None]
                else:
                    rd, wr, uni = info
                    if uni:
                        if pending and not nd:
                            drain_before.add(i)
                        pending = [None]
                    else:
                        if not nd and _spans_overlap(rd + wr, pending):
                            drain_before.add(i)
                            pending = []
                        pending.extend(wr)
                if inc is not None:
                    drain_inc_set.add(i)
                    pending = []
            plan[ename] = (drain_before, drain_inc_set)
            stats[ename] = (len(ops), len(drain_before), len(drain_inc_set))
        self.plan_stats = stats
        return plan


def rearr_tok(dram_ap, t0, ntok):
    return dram_ap[t0:t0 + ntok, :].rearrange("(t p) d -> p t d", p=128)


def build_phase_a(n_pre=7, n_own=9, chain="ve", stages=6, ctx=None):
    standalone = ctx is None
    if standalone:
        nc = bass.Bass("TRN2", target_bir_lowering=False, debug=False)
        P = None
    else:
        nc, P = ctx
    T = (n_pre + n_own) * CH
    TO = n_own * CH

    xw = nc.dram_tensor("xw", [T, D], F32, kind="ExternalInput").ap()
    wqkvT = nc.dram_tensor("wqkvT", [D, 3 * D], F32, kind="ExternalInput").ap()
    bq_col = nc.dram_tensor("bq_col", [128, 24], F32, kind="ExternalInput").ap()
    bv_row = nc.dram_tensor("bv_row", [1, D], F32, kind="ExternalInput").ap()
    omega_bd = nc.dram_tensor("omega_bd", [8, 128, 128], F32, kind="ExternalInput").ap()
    quadv = nc.dram_tensor("quadv", [2, 1], F32, kind="ExternalInput").ap()
    quadw = nc.dram_tensor("quadw", [2, 1], F32, kind="ExternalInput").ap()
    selq = nc.dram_tensor("selq", [2, 128], F32, kind="ExternalInput").ap()
    sel2 = nc.dram_tensor("sel2", [2, 128], F32, kind="ExternalInput").ap()
    bd2 = nc.dram_tensor("bd2", [128, 2], F32, kind="ExternalInput").ap()
    flag = nc.dram_tensor("flag", [128, 1], F32, kind="ExternalInput").ap()
    flag_bf = nc.dram_tensor("flag_bf", [128, 1], BF16, kind="ExternalInput").ap()
    tri = nc.dram_tensor("tri", [128, 4, CH], F32, kind="ExternalInput").ap()
    ident_b = nc.dram_tensor("ident_b", [128, 128], BF16, kind="ExternalInput").ap()
    woutT = nc.dram_tensor("woutT", [D, D], F32, kind="ExternalInput").ap()
    bo_row = nc.dram_tensor("bo_row", [1, D], F32, kind="ExternalInput").ap()

    kind = "ExternalOutput" if standalone else "Internal"
    xn_sc = nc.dram_tensor("xn_sc", [TO, D], F32, kind=kind).ap()
    ao_sc = nc.dram_tensor("ao_sc", [TO, D], F32, kind=kind).ap()

    if standalone:
        P = Prog(nc, chain=chain)
    es = ExitStack()
    with es:
        sb = lambda n, s, d=F32: es.enter_context(nc.sbuf_tensor('pa_' + n, s, d))
        ps = lambda n, s, d=F32: es.enter_context(nc.psum_tensor('pa_' + n, s, d))

        s_wqkv = sb("s_wqkv", [128, KT, 3 * D], BF16)
        s_bq = sb("s_bq", [128, 24])
        s_bv = sb("s_bv", [1, D], BF16)
        s_om = sb("s_om", [128, 8, 128], BF16)
        s_qv = sb("s_qv", [2, 1])
        s_qw = sb("s_qw", [2, 1])
        s_selq = sb("s_selq", [2, 128])
        s_sel2 = sb("s_sel2", [2, 128])
        s_bd2 = sb("s_bd2", [128, 2])
        s_flag = sb("s_flag", [128, 1])
        s_flagb = sb("s_flagb", [128, 1], BF16)
        s_tri = sb("s_tri", [128, 4, CH])
        s_idb = sb("s_idb", [128, 128], BF16)
        s_wout = sb("s_wout", [128, KT, D], BF16)
        s_bo = sb("s_bo", [1, D], BF16)
        s_ones_b = sb("s_ones_b", [1, CH], BF16)
        s_ones64 = sb("s_ones64", [1, 64])
        s_cols = sb("s_cols", [128, 3])
        s_c40 = sb("s_c40", [128, 1])
        s_cm20 = sb("s_cm20", [128, 1])
        s_cscr = sb("s_cscr", [2, 1])
        s_cscr2 = sb("s_cscr2", [2, 1])

        s_x = sb("s_x", [128, 4, D])
        s_xn = sb("s_xn", [128, 4, D])
        s_xnb = sb("s_xnb", [128, 4, D], BF16)
        s_lncol = sb("s_lncol", [128, 8])
        s_xnT = sb("s_xnT", [128, KT, CH], BF16)
        s_zf = sb("s_zf", [128, 2, CH])
        s_zb = sb("s_zb", [128, 2, CH], BF16)
        s_z2 = sb("s_z2", [128, 2, CH])
        s_rs = sb("s_rs", [2, CH])
        s_rsb = sb("s_rsb", [128, 2, CH])
        s_pz = sb("s_pz", [128, 2, CH])
        s_fb = sb("s_fb", [128, 2, CH])
        s_kf = sb("s_kf", [128, 8, CH], BF16)
        s_qf = sb("s_qf", [128, 8, CH], BF16)
        s_vaug = sb("s_vaug", [128, 4, NH, HD + 1], BF16)
        s_kft = sb("s_kft", [128, 4, 8, 128], BF16)
        s_carry = sb("s_carry", [64, NH * (HD + 1)])
        s_carryb = sb("s_carryb", [128, HD + 1], BF16)
        s_sS = sb("s_sS", [128, 2, CH], BF16)
        s_dent = sb("s_dent", [1, CH])
        s_dscr = sb("s_dscr", [1, CH])
        s_drec = sb("s_drec", [1, CH])
        s_denb = sb("s_denb", [64, CH])
        s_attn = sb("s_attn", [128, KT, CH], BF16)
        s_ao = s_x

        _pt = ps("p_trb0", [128, 128], BF16)
        p_trb = [_pt, _pt]
        p_bc2 = ps("p_bc2", [128, CH])
        p_mm = [ps("p_mm0", [128, CH]), ps("p_mm1", [128, CH])]
        p_S = ps("p_S", [128, CH])
        p_ctx = ps("p_ctx", [HD + 1, CH])
        p_upd = ps("p_upd", [64, HD + 1])
        p_bc = ps("p_bc", [128, CH])

        LD = P.sem("LD")
        LDC = P.sem("LDC")
        nld = [0, 0]

        def load(out, in_, cast=False):
            if cast:
                P.dma("gpsimd", out, in_, inc=LDC)
                nld[1] += 1
            else:
                P.dma("sync", out, in_, inc=LD)
                nld[0] += 1

        load(s_wqkv[:], wqkvT.rearrange("(k p) n -> p k n", p=128), cast=True)
        load(s_bq[:], bq_col)
        load(s_bv[:], bv_row, cast=True)
        load(s_om[:], omega_bd.rearrange("a p n -> p a n"), cast=True)
        load(s_qv[:], quadv)
        load(s_qw[:], quadw)
        load(s_selq[:], selq)
        load(s_sel2[:], sel2)
        load(s_bd2[:], bd2)
        load(s_flag[:], flag)
        load(s_flagb[:], flag_bf)
        load(s_tri[:], tri)
        load(s_idb[:], ident_b)
        load(s_wout[:], woutT.rearrange("(k p) n -> p k n", p=128), cast=True)
        load(s_bo[:], bo_row, cast=True)
        LDA = [(LD, 16 * nld[0]), (LDC, 16 * nld[1])]

        V, S, TE, DO = P.sem("V"), P.sem("S"), P.sem("TE"), P.sem("DO")
        TRC, SAV = P.sem("TRC"), P.sem("SAV")
        CHV, CHT, CHU = P.sem("CHV"), P.sem("CHT"), P.sem("CHU")
        DX = [P.sem("DX0")]
        dx_cnt = [0]

        P.op("vector", lambda e: e.memset(s_ones_b[:], 1.0), wait=LDA)
        P.op("vector", lambda e: e.memset(s_ones64[:], 1.0))
        P.op("vector", lambda e: e.memset(s_c40[:], 40.0))
        P.op("vector", lambda e: e.memset(s_cm20[:], -20.0))
        P.op("vector", lambda e: e.memset(s_carry[:], 0.0))
        P.op("vector", lambda e: e.memset(s_vaug[:, :, :, HD:HD + 1], 1.0))
        P.op("vector", lambda e: e.tensor_scalar(out=s_cscr[:], in0=s_qv[:],
             scalar1=0.0, scalar2=2.0, op0=ALU.max, op1=ALU.mult))
        v1 = P.op("vector", lambda e: e.tensor_scalar(out=s_cscr[:], in0=s_cscr[:],
                  scalar1=1e-30, scalar2=None, op0=ALU.max), inc=V)
        s0 = P.op("scalar", lambda e: e.activation(out=s_cscr2[:], in_=s_cscr[:],
                  func=AF.Ln), wait=[(V, v1)], inc=S)
        s1 = P.op("scalar", lambda e: e.activation(out=s_cscr2[:], in_=s_cscr2[:],
                  func=AF.Exp, scale=0.5), wait=[(S, s0)], inc=S)
        v1 = P.op("vector", lambda e: e.tensor_scalar(out=s_cscr2[:], in0=s_cscr2[:],
                  scalar1=-1.0, scalar2=None, op0=ALU.mult), wait=[(S, s1)], inc=V)
        t1 = P.op("tensor", lambda e: e.matmul(out=p_bc[:, 0:1], lhsT=s_selq[:],
                  rhs=s_cscr2[:], start=True, stop=True), wait=[(V, v1)],
                  inc=TE, drain_inc=True)
        v1 = P.op("vector", lambda e: e.tensor_copy(out=s_cols[:, 0:1],
                  in_=p_bc[:, 0:1]), wait=[(TE, t1)], inc=V)
        v2 = P.op("vector", lambda e: e.tensor_scalar(out=s_cscr[:], in0=s_qv[:],
                  scalar1=20.0, scalar2=None, op0=ALU.add), inc=V)
        t1 = P.op("tensor", lambda e: e.matmul(out=p_bc[:, 1:2], lhsT=s_selq[:],
                  rhs=s_cscr[:], start=True, stop=True), wait=[(V, v2)],
                  inc=TE, drain_inc=True)
        v1 = P.op("vector", lambda e: e.tensor_copy(out=s_cols[:, 1:2],
                  in_=p_bc[:, 1:2]), wait=[(TE, t1)], inc=V)
        v2 = P.op("vector", lambda e: e.tensor_scalar(out=s_cscr[:], in0=s_qw[:],
                  scalar1=0.0, scalar2=1e-60, op0=ALU.max, op1=ALU.max),
                  wait=[(TE, t1)], inc=V)
        s2 = P.op("scalar", lambda e: e.activation(out=s_cscr[:], in_=s_cscr[:],
                  func=AF.Ln), wait=[(V, v2)], inc=S)
        s2 = P.op("scalar", lambda e: e.activation(out=s_cscr[:], in_=s_cscr[:],
                  func=AF.Exp, scale=0.5), wait=[(S, s2)], inc=S)
        v2 = P.op("vector", lambda e: e.tensor_scalar(out=s_cscr[:], in0=s_cscr[:],
                  scalar1=1.0 / 32.0, scalar2=None, op0=ALU.mult),
                  wait=[(S, s2)], inc=V)
        t1 = P.op("tensor", lambda e: e.matmul(out=p_bc[:, 2:3], lhsT=s_selq[:],
                  rhs=s_cscr[:], start=True, stop=True), wait=[(V, v2)],
                  inc=TE, drain_inc=True)
        v1 = P.op("vector", lambda e: e.tensor_copy(out=s_cols[:, 2:3],
                  in_=p_bc[:, 2:3]), wait=[(TE, t1)], inc=V)
        setup_v = (V, P.count["V"])

        mm_free = [(V, 0), (V, 0)]
        S_free = (V, 0)
        sS_loc = [0, 0]
        ctx_free = (V, 0)
        upd_free = (V, 0)
        bc_free = (V, P.count["V"])
        bc2_free = (V, 0)
        nsq_free = (V, 0)
        last_xn_dma = (DO, 0)
        last_nsq = (TE, 0)
        last_rs_mm = (TE, 0)
        last_feat = (V, 0)
        last_ao_dma = (DO, 0)
        last_ao_read = (TE, 0)
        ln_scalar_last = (S, 0)
        mm_turn = [0]
        pz_par = [0]

        def mm_bank():
            b = mm_turn[0] % 2
            mm_turn[0] += 1
            return b

        nch = n_pre + n_own
        for c in range(nch):
            own = c >= n_pre
            o = c - n_pre
            buf = c % 2
            wl1 = [(CHV, c)] if c >= 1 else []
            if c >= 1:
                wl1.append(last_ao_dma)
            dx_cnt[0] += 16
            P.dma("sync", s_x[:], rearr_tok(xw, c * CH, CH),
                  wait=wl1, inc=DX[0])
            ldx = (DX[0], dx_cnt[0])

            if stages == 0:
                vv = None
                for i in range(4):
                    vv = P.op("vector", lambda e, i=i: e.tensor_copy(
                        out=s_xn[:, i, :], in_=s_x[:, i, :]),
                        wait=[ldx] if i == 0 else [], inc=V)
                if own:
                    dd = P.dma("sync", rearr_tok(xn_sc, o * CH, CH), s_xn[:],
                               wait=[(V, vv)], inc=DO)
                    last_xn_dma = (DO, dd)
                P.op("tensor", lambda e: e.nop(), inc=CHT)
                P.op("vector", lambda e: e.nop(), inc=CHV)
                continue
            if stages == 10:
                vv = None
                for i in range(4):
                    vr0 = P.op("vector", lambda e, i=i: e.tensor_reduce(
                        out=s_lncol[:, 2 * i:2 * i + 1], in_=s_x[:, i, :],
                        axis=AX.X, op=ALU.add), wait=[ldx] if i == 0 else [],
                        inc=V)
                    vm = P.op("vector", lambda e, i=i, vr0=vr0: e.tensor_scalar(
                        out=s_lncol[:, 2 * i:2 * i + 1],
                        in0=s_lncol[:, 2 * i:2 * i + 1],
                        scalar1=1.0 / D, scalar2=None, op0=ALU.mult),
                        wait=[(V, vr0)], inc=V)
                    vv = P.op("vector", lambda e, i=i: e.tensor_scalar(
                        out=s_xn[:, i, :], in0=s_x[:, i, :],
                        scalar1=s_lncol[:, 2 * i:2 * i + 1], scalar2=None,
                        op0=ALU.subtract), wait=[(V, vm)], inc=V)
                if own:
                    dd = P.dma("sync", rearr_tok(xn_sc, o * CH, CH), s_xn[:],
                               wait=[(V, vv)], inc=DO)
                    last_xn_dma = (DO, dd)
                P.op("tensor", lambda e: e.nop(), inc=CHT)
                P.op("vector", lambda e: e.nop(), inc=CHV)
                continue
            if stages == 11:
                vv = None
                for i in range(4):
                    P.op("vector", lambda e, i=i: e.tensor_reduce(
                        out=s_lncol[:, 2 * i:2 * i + 1], in_=s_x[:, i, :],
                        axis=AX.X, op=ALU.add), wait=[ldx] if i == 0 else [])
                    P.op("vector", lambda e, i=i: e.tensor_scalar(
                        out=s_lncol[:, 2 * i:2 * i + 1],
                        in0=s_lncol[:, 2 * i:2 * i + 1],
                        scalar1=1.0 / D, scalar2=None, op0=ALU.mult))
                    va = P.op("vector", lambda e, i=i: e.tensor_scalar(
                        out=s_xn[:, i, :], in0=s_x[:, i, :],
                        scalar1=s_lncol[:, 2 * i:2 * i + 1], scalar2=None,
                        op0=ALU.subtract), inc=V)
                    ssq = P.op("scalar", lambda e, i=i: e.activation(
                        out=s_z2[:].rearrange("p a b -> p (a b)"),
                        in_=s_xn[:, i, :], func=AF.Square,
                        accum_out=s_lncol[:, 2 * i + 1:2 * i + 2]),
                        wait=[(V, va)], inc=S)
                    vv = P.op("vector", lambda e, i=i: e.tensor_scalar(
                        out=s_xn[:, i, :], in0=s_xn[:, i, :],
                        scalar1=s_lncol[:, 2 * i + 1:2 * i + 2], scalar2=None,
                        op0=ALU.bypass if False else ALU.mult),
                        wait=[(S, ssq)], inc=V)
                if own:
                    dd = P.dma("sync", rearr_tok(xn_sc, o * CH, CH), s_xn[:],
                               wait=[(V, vv)], inc=DO)
                    last_xn_dma = (DO, dd)
                P.op("tensor", lambda e: e.nop(), inc=CHT)
                P.op("vector", lambda e: e.nop(), inc=CHV)
                continue
            prev_S = ln_scalar_last
            for i in range(4):
                w0 = [ldx] if i == 0 else []
                if i == 0 and c == 0:
                    w0.append(setup_v)
                P.op("vector", lambda e, i=i: e.tensor_reduce(
                    out=s_lncol[:, 2 * i:2 * i + 1], in_=s_x[:, i, :],
                    axis=AX.X, op=ALU.add), wait=w0)
                P.op("vector", lambda e, i=i: e.tensor_scalar(
                    out=s_lncol[:, 2 * i:2 * i + 1],
                    in0=s_lncol[:, 2 * i:2 * i + 1],
                    scalar1=1.0 / D, scalar2=None, op0=ALU.mult))
                wxc = [prev_S, last_nsq] if i == 0 else []
                if i == 0 and own and o >= 1:
                    wxc.append(last_xn_dma)
                vv = P.op("vector", lambda e, i=i: e.tensor_scalar(
                    out=s_xn[:, i, :], in0=s_x[:, i, :],
                    scalar1=s_lncol[:, 2 * i:2 * i + 1], scalar2=None,
                    op0=ALU.subtract),
                    wait=wxc, inc=V)
                ssq = P.op("scalar", lambda e, i=i: e.activation(
                    out=s_z2[:].rearrange("p a b -> p (a b)"),
                    in_=s_xn[:, i, :], func=AF.Square,
                    accum_out=s_lncol[:, 2 * i + 1:2 * i + 2]),
                    wait=[(V, vv)], inc=S)
                vv = P.op("vector", lambda e, i=i: e.tensor_scalar(
                    out=s_lncol[:, 2 * i + 1:2 * i + 2],
                    in0=s_lncol[:, 2 * i + 1:2 * i + 2],
                    scalar1=1.0 / D, scalar2=1e-5, op0=ALU.mult, op1=ALU.add),
                    wait=[(S, ssq)], inc=V)
                ss = P.op("scalar", lambda e, i=i: e.activation(
                    out=s_lncol[:, 2 * i + 1:2 * i + 2],
                    in_=s_lncol[:, 2 * i + 1:2 * i + 2], func=AF.Ln),
                    wait=[(V, vv)], inc=S)
                ss = P.op("scalar", lambda e, i=i: e.activation(
                    out=s_lncol[:, 2 * i + 1:2 * i + 2],
                    in_=s_lncol[:, 2 * i + 1:2 * i + 2], func=AF.Exp,
                    scale=-0.5), wait=[(S, ss)], inc=S)
                vv = P.op("vector", lambda e, i=i: e.tensor_scalar(
                    out=s_xn[:, i, :], in0=s_xn[:, i, :],
                    scalar1=s_lncol[:, 2 * i + 1:2 * i + 2], scalar2=None,
                    op0=ALU.mult), wait=[(S, ss)], inc=V)
                sc = P.op("scalar", lambda e, i=i: e.activation(
                    out=s_xnb[:, i, :], in_=s_xn[:, i, :], func=AF.Copy),
                    wait=[(V, vv)], inc=S)
            ln_scalar_last = (S, P.count["S"])
            ln_done_v = (V, P.count["V"])
            ln_done_s = (S, P.count["S"])

            if own:
                dd = P.dma("sync", rearr_tok(xn_sc, o * CH, CH), s_xn[:],
                           wait=[ln_done_v], inc=DO)
                last_xn_dma = (DO, dd)

            tloc = []
            for i in range(4) if stages >= 2 else []:
                for j in range(KT):
                    idx = i * KT + j
                    w = []
                    if idx == 0:
                        w.append(ln_done_s)
                        if c >= 1:
                            w.append((CHT, c))
                    if idx >= 1:
                        w.append((TRC, tloc[idx - 1]))
                    tt = P.op("tensor", lambda e, i=i, j=j, idx=idx: e.transpose(
                        out=p_trb[0][:],
                        in_=s_xnb[:, i, j * 128:(j + 1) * 128],
                        identity=s_idb[:]), wait=w, inc=TE, drain_inc=True)
                    cc = P.op("vector", lambda e, i=i, j=j, idx=idx: e.tensor_copy(
                        out=s_xnT[:, j, i * 128:(i + 1) * 128],
                        in_=p_trb[0][:]), wait=[(TE, tt)], inc=TRC)
                    tloc.append(cc)
            xnT_done = (TRC, P.count["TRC"])

            sides = [("k", s_kf, D)] if not own else [("q", s_qf, 0), ("k", s_kf, D)]
            if stages < 3:
                sides = []
            for sname, sdst, coff in sides:
                for hp in range(8):
                    par = pz_par[0] % 2
                    pz_par[0] += 1
                    b = mm_bank()
                    tt = None
                    for k in range(KT):
                        w = []
                        if k == 0:
                            w = [xnT_done, mm_free[b]]
                        r = P.op("tensor", lambda e, k=k, hp=hp, b=b, coff=coff:
                                 e.matmul(out=p_mm[b][:],
                                          lhsT=s_wqkv[:, k, coff + hp * 128:coff + (hp + 1) * 128],
                                          rhs=s_xnT[:, k, :],
                                          start=(k == 0), stop=(k == KT - 1)),
                                 wait=w, inc=TE if k == KT - 1 else None)
                        if r is not None:
                            tt = r
                    mcol = (coff + hp * 128) // 128
                    ev = P.op("vector", lambda e, b=b, par=par, mcol=mcol:
                              e.tensor_scalar(out=s_zf[:, par, :], in0=p_mm[b][:],
                                              scalar1=s_bq[:, mcol:mcol + 1],
                                              scalar2=None, op0=ALU.add),
                              wait=[(TE, tt)], inc=V)
                    sc1 = P.op("scalar", lambda e, par=par: e.activation(
                        out=s_zb[:, par, :], in_=s_zf[:, par, :], func=AF.Copy),
                        wait=[(V, ev)], inc=S)
                    sc2 = P.op("scalar", lambda e, par=par: e.activation(
                        out=s_z2[:, par, :], in_=s_zf[:, par, :], func=AF.Square),
                        inc=S)
                    tn = P.op("tensor", lambda e, par=par: e.matmul(
                        out=p_bc[0:2, :], lhsT=s_bd2[:], rhs=s_z2[:, par, :],
                        start=True, stop=True),
                        wait=[(S, sc2), nsq_free, bc_free], inc=TE,
                        drain_inc=True)
                    vr = P.op("vector", lambda e: e.tensor_scalar(
                        out=s_rs[:], in0=p_bc[0:2, :], scalar1=1e-24,
                        scalar2=None, op0=ALU.max),
                        wait=[(TE, tn), last_rs_mm], inc=V)
                    nsq_free = (V, vr)
                    sr = P.op("scalar", lambda e: e.activation(
                        out=s_rs[:], in_=s_rs[:], func=AF.Ln),
                        wait=[(V, vr)], inc=S)
                    sr = P.op("scalar", lambda e: e.activation(
                        out=s_rs[:], in_=s_rs[:], func=AF.Exp,
                        scale=-0.5), wait=[(S, sr)], inc=S)
                    tb = P.op("tensor", lambda e: e.matmul(
                        out=p_bc[:], lhsT=s_sel2[:], rhs=s_rs[:],
                        start=True, stop=True), wait=[(S, sr), bc_free],
                        inc=TE, drain_inc=True)
                    last_rs_mm = (TE, tb)
                    vb = P.op("vector", lambda e, par=par: e.tensor_copy(
                        out=s_rsb[:, par, :], in_=p_bc[:]), wait=[(TE, tb)],
                        inc=V)
                    bc_free = (V, vb)
                    tp = P.op("tensor", lambda e, hp=hp, b=b, par=par: e.matmul(
                        out=p_mm[b][:], lhsT=s_om[:, hp, :], rhs=s_zb[:, par, :],
                        start=True, stop=True), wait=[(S, sc1)], inc=TE,
                        drain_inc=True)
                    vp = P.op("vector", lambda e, b=b, par=par: e.tensor_tensor(
                        out=s_pz[:, par, :], in0=p_mm[b][:],
                        in1=s_rsb[:, par, :], op=ALU.mult),
                        wait=[(TE, tp), (V, vb)], inc=V)
                    mm_free[b] = (V, vp)
                    sf = P.op("scalar", lambda e, par=par: e.activation(
                        out=s_fb[:, 0, :], in_=s_pz[:, par, :], func=AF.Relu,
                        scale=s_cols[:, 0:1], bias=s_cols[:, 1:2]),
                        wait=[(V, vp), last_feat] if True else [], inc=S)
                    sf = P.op("scalar", lambda e: e.activation(
                        out=s_fb[:, 0, :], in_=s_fb[:, 0, :], func=AF.Relu,
                        scale=-1.0, bias=s_c40[:, 0:1]), wait=[(S, sf)], inc=S)
                    sf = P.op("scalar", lambda e: e.activation(
                        out=s_fb[:, 0, :], in_=s_fb[:, 0, :], func=AF.Exp,
                        bias=s_cm20[:, 0:1]), wait=[(S, sf)], inc=S)
                    sf = P.op("scalar", lambda e, par=par: e.activation(
                        out=s_fb[:, 1, :], in_=s_pz[:, par, :], func=AF.Square),
                        inc=S)
                    wf = [(S, sf)]
                    if hp == 0 and c >= 1:
                        wf.append((CHT, c))
                    lf = P.op("vector", lambda e, sdst=sdst, hp=hp: e.scalar_tensor_tensor(
                        out=sdst[:, hp, :], in0=s_fb[:, 1, :],
                        scalar=s_cols[:, 2:3], in1=s_fb[:, 0, :],
                        op0=ALU.mult, op1=ALU.mult), wait=wf, inc=V)
                    last_feat = (V, lf)
            feat_done = (V, P.count["V"])
            last_nsq = (TE, P.count["TE"])

            for i in range(4) if stages >= 4 else []:
                for n in range(2):
                    b = mm_bank()
                    for k in range(KT):
                        w = [xnT_done, mm_free[b]] if k == 0 else []
                        P.op("tensor", lambda e, i=i, n=n, k=k, b=b: e.matmul(
                            out=p_mm[b][:],
                            lhsT=s_xnT[:, k, i * 128:(i + 1) * 128],
                            rhs=s_wqkv[:, k, 2 * D + n * CH:2 * D + (n + 1) * CH],
                            start=(k == 0), stop=False), wait=w)
                    tt = P.op("tensor", lambda e, n=n, b=b: e.matmul(
                        out=p_mm[b][:], lhsT=s_ones_b[0:1, 0:128],
                        rhs=s_bv[0:1, n * CH:(n + 1) * CH],
                        start=False, stop=True), inc=TE)
                    w = [(TE, tt)]
                    if i == 0 and n == 0 and c >= 1:
                        w.append((CHU, c))
                    vv = P.op("vector", lambda e, i=i, n=n, b=b: e.tensor_copy(
                        out=s_vaug[:, i, 8 * n:8 * (n + 1), 0:HD],
                        in_=p_mm[b][:].rearrange("p (h d) -> p h d", d=HD)),
                        wait=w, inc=V)
                    mm_free[b] = (V, vv)
            v_done = (V, P.count["V"])

            kloc = []
            for hp in range(8) if stages >= 4 else []:
                for j in range(4):
                    idx = hp * 4 + j
                    w = []
                    if idx == 0:
                        w.append(feat_done)
                        w.append((TRC, P.count["TRC"]))
                    if idx >= 1:
                        w.append((TRC, kloc[idx - 1]))
                    tt = P.op("tensor", lambda e, hp=hp, j=j, idx=idx: e.transpose(
                        out=p_trb[0][:],
                        in_=s_kf[:, hp, j * 128:(j + 1) * 128],
                        identity=s_idb[:]), wait=w, inc=TE, drain_inc=True)
                    cc = P.op("vector", lambda e, hp=hp, j=j, idx=idx: e.tensor_copy(
                        out=s_kft[:, j, hp, :], in_=p_trb[0][:]),
                        wait=[(TE, tt)], inc=TRC)
                    kloc.append(cc)
            kft_done = (TRC, P.count["TRC"])

            if c == n_pre and stages >= 4:
                P.op("vector", lambda e: e.tensor_scalar(
                    out=s_kft[:], in0=s_kft[:], scalar1=s_flag[:, 0:1],
                    scalar2=None, op0=ALU.mult), wait=[kft_done], inc=TRC)
                kft_done = (TRC, P.count["TRC"])

            for h in range(NH) if stages >= 5 else []:
                hp, half = h // 2, h % 2
                r0 = 64 * half
                if own:
                    vcb = P.op("vector", lambda e, h=h, r0=r0: e.tensor_copy(
                        out=s_carryb[r0:r0 + 64, :],
                        in_=s_carry[:, h * 65:(h + 1) * 65]),
                        wait=[upd_free, ctx_free], inc=V)
                    P.op("tensor", lambda e, hp=hp, r0=r0: e.matmul(
                        out=p_ctx[:], lhsT=s_carryb[r0:r0 + 64, :],
                        rhs=s_qf[r0:r0 + 64, hp, :], start=True, stop=False),
                        wait=[(V, vcb), feat_done])
                    for j in range(4):
                        tt = P.op("tensor", lambda e, hp=hp, r0=r0, j=j: e.matmul(
                            out=p_S[:],
                            lhsT=s_kf[r0:r0 + 64, hp, j * 128:(j + 1) * 128],
                            rhs=s_qf[r0:r0 + 64, hp, :], start=True, stop=True),
                            wait=[S_free], inc=TE)
                        wv = [(TE, tt)]
                        if sS_loc[j % 2] > 0:
                            wv.append((SAV, sS_loc[j % 2]))
                        vv = P.op("vector", lambda e, j=j: e.tensor_tensor(
                            out=s_sS[:, j % 2, :], in0=p_S[:],
                            in1=s_tri[:, j, :], op=ALU.mult), wait=wv, inc=V)
                        S_free = (V, vv)
                        av = P.op("tensor", lambda e, h=h, j=j: e.matmul(
                            out=p_ctx[:], lhsT=s_vaug[:, j, h, :],
                            rhs=s_sS[:, j % 2, :], start=False, stop=(j == 3)),
                            wait=[(V, vv), v_done], inc=SAV)
                        sS_loc[j % 2] = av
                    ctx_t = (SAV, P.count["SAV"])
                ut = None
                for j in range(4):
                    w = []
                    if j == 0:
                        w = [kft_done, v_done, upd_free]
                    r = P.op("tensor", lambda e, hp=hp, r0=r0, j=j, h=h: e.matmul(
                        out=p_upd[:], lhsT=s_kft[:, j, hp, r0:r0 + 64],
                        rhs=s_vaug[:, j, h, :], start=(j == 0), stop=(j == 3)),
                        wait=w, inc=TE if j == 3 else None, drain_inc=True)
                    if r is not None:
                        ut = r
                if own:
                    vv = P.op("vector", lambda e: e.tensor_copy(
                        out=s_dent[:], in_=p_ctx[HD:HD + 1, :]),
                        wait=[ctx_t], inc=V)
                    vv = P.op("vector", lambda e: e.tensor_scalar(
                        out=s_dent[:], in0=s_dent[:], scalar1=1e-6,
                        scalar2=None, op0=ALU.max), inc=V)
                    vv = P.op("vector", lambda e: e.reciprocal(
                        out=s_dent[:], in_=s_dent[:]), inc=V)
                    tb = P.op("tensor", lambda e: e.matmul(
                        out=p_bc[0:64, :], lhsT=s_ones64[:], rhs=s_dent[:],
                        start=True, stop=True), wait=[(V, vv), bc_free],
                        inc=TE, drain_inc=True)
                    vv = P.op("vector", lambda e: e.tensor_copy(
                        out=s_denb[:], in_=p_bc[0:64, :]), wait=[(TE, tb)], inc=V)
                    bc_free = (V, vv)
                    wat = [(V, vv)]
                    if h == 0:
                        wat.append(last_ao_read)
                    av = P.op("vector", lambda e, hp=hp, r0=r0: e.tensor_tensor(
                        out=s_attn[r0:r0 + 64, hp, :], in0=p_ctx[0:HD, :],
                        in1=s_denb[:], op=ALU.mult), wait=wat, inc=V)
                    ctx_free = (V, av)
                vv = P.op("vector", lambda e, h=h: e.tensor_tensor(
                    out=s_carry[:, h * 65:(h + 1) * 65],
                    in0=s_carry[:, h * 65:(h + 1) * 65], in1=p_upd[:],
                    op=ALU.add), wait=[(TE, ut)], inc=V)
                upd_free = (V, vv)
            attn_done = (V, P.count["V"])
            P.op("vector", lambda e: e.nop(), inc=CHU)

            if c == n_pre - 1 and stages >= 5:
                vv = P.op("vector", lambda e: e.tensor_scalar(
                    out=s_carry[:], in0=s_carry[:], scalar1=s_flag[0:64, 0:1],
                    scalar2=None, op0=ALU.mult), wait=[upd_free], inc=V)
                upd_free = (V, vv)

            if own and stages >= 6:
                last_ev = None
                for i in range(4):
                    for n in range(2):
                        b = mm_bank()
                        for k in range(KT):
                            w = [attn_done, mm_free[b]] if k == 0 else []
                            P.op("tensor", lambda e, i=i, n=n, k=k, b=b: e.matmul(
                                out=p_mm[b][:],
                                lhsT=s_attn[:, k, i * 128:(i + 1) * 128],
                                rhs=s_wout[:, k, n * CH:(n + 1) * CH],
                                start=(k == 0), stop=False), wait=w)
                        tt = P.op("tensor", lambda e, n=n, b=b: e.matmul(
                            out=p_mm[b][:], lhsT=s_ones_b[0:1, 0:128],
                            rhs=s_bo[0:1, n * CH:(n + 1) * CH],
                            start=False, stop=True), inc=TE)
                        w = [(TE, tt)]
                        last_ev = P.op("vector", lambda e, i=i, n=n, b=b:
                                       e.tensor_copy(out=s_ao[:, i, n * CH:(n + 1) * CH],
                                                     in_=p_mm[b][:]),
                                       wait=w, inc=V)
                        mm_free[b] = (V, last_ev)
                last_ao_read = (TE, P.count["TE"])
                dd = P.dma("sync", rearr_tok(ao_sc, o * CH, CH), s_ao[:],
                           wait=[(V, last_ev)], inc=DO)
                last_ao_dma = (DO, dd)
            P.op("tensor", lambda e: e.nop(), inc=CHT)
            P.op("vector", lambda e: e.nop(), inc=CHV)

        if standalone:
            P.op("sync", lambda e: e.nop(), wait=[(DO, P.count["DO"])])
            P.build(es)
        scr = dict(xn_sc=xn_sc, ao_sc=ao_sc, ident_b=ident_b, flag=flag,
                   tri=tri)
    return nc if standalone else scr


def stage_a_inputs(inputs, core, n_pre=7, n_own=9):
    import ml_dtypes
    x = np.asarray(inputs["L0_attn_x"])
    b, half = core // 2, core % 2
    seq = x[b]
    TO = n_own * CH
    T = (n_pre + n_own) * CH
    if half == 1:
        xw = seq[8192 - T:] if T < 8192 else seq
    else:
        xw = np.concatenate(
            [np.zeros((T - TO + CH, D), np.float32), seq[:TO - CH]], 0)
    wqkvT = np.ascontiguousarray(np.asarray(inputs["L0_attn_w_qkv"]).T)
    bq = np.ascontiguousarray(np.asarray(inputs["L0_attn_b_qkv"]).reshape(24, 128).T)
    bv = np.asarray(inputs["L0_attn_b_qkv"])[2 * D:].reshape(1, D)
    om = np.asarray(inputs["L0_attn_omega"])
    om_bd = np.zeros((8, 128, 128), np.float32)
    for hp in range(8):
        for t in range(2):
            h = 2 * hp + t
            blk = np.concatenate([om[0, h], om[1, h]], axis=1)
            om_bd[hp, 64 * t:64 * t + 64, 64 * t:64 * t + 64] = blk
    qv = np.asarray(inputs["L0_attn_quad_nodes"]).reshape(2, 1).astype(np.float32)
    qw = np.asarray(inputs["L0_attn_quad_weights"]).reshape(2, 1).astype(np.float32)
    selq = np.zeros((2, 128), np.float32)
    for p in range(128):
        selq[(p // 32) % 2, p] = 1.0
    sel2 = np.zeros((2, 128), np.float32)
    sel2[0, 0:64] = 1.0
    sel2[1, 64:128] = 1.0
    bd2 = np.zeros((128, 2), np.float32)
    bd2[0:64, 0] = 1.0
    bd2[64:128, 1] = 1.0
    flagv = np.full((128, 1), float(half), np.float32)
    tri = np.zeros((128, 4, CH), np.float32)
    ar = np.arange(CH)
    for j in range(4):
        for p in range(128):
            tri[p, j, :] = (j * 128 + p <= ar)
    woutT = np.ascontiguousarray(np.asarray(inputs["L0_attn_w_out"]).T)
    bo = np.asarray(inputs["L0_attn_b_out"]).reshape(1, D)
    return dict(
        xw=np.ascontiguousarray(xw, dtype=np.float32),
        wqkvT=wqkvT.astype(np.float32), bq_col=bq.astype(np.float32),
        bv_row=bv.astype(np.float32), omega_bd=om_bd, quadv=qv, quadw=qw,
        selq=selq, sel2=sel2, bd2=bd2, flag=flagv,
        flag_bf=flagv.astype(ml_dtypes.bfloat16), tri=tri,
        ident_b=np.eye(128, dtype=np.float32).astype(ml_dtypes.bfloat16),
        woutT=woutT.astype(np.float32), bo_row=bo.astype(np.float32))




def build_phase_b(n_own=9, chain="ve", ctx=None, scr=None):
    standalone = ctx is None
    if standalone:
        nc = bass.Bass("TRN2", target_bir_lowering=False, debug=False)
    else:
        nc, P = ctx
    TO = n_own * CH
    I = 2816
    IT = I // 128
    HW = 1152
    if standalone:
        xn_in = nc.dram_tensor("xn_in", [TO, D], F32, kind="ExternalInput").ap()
        ao_in = nc.dram_tensor("ao_in", [TO, D], F32, kind="ExternalInput").ap()
    else:
        xn_in, ao_in = scr["xn_sc"], scr["ao_sc"]
    x_in = nc.dram_tensor("x_in", [TO, D], F32, kind="ExternalInput").ap()
    wg_in = nc.dram_tensor("wg_in", [I, D], F32, kind="ExternalInput").ap()
    wu_in = nc.dram_tensor("wu_in", [I, D], F32, kind="ExternalInput").ap()
    wd_in = nc.dram_tensor("wd_in", [D, I], F32, kind="ExternalInput").ap()
    if standalone:
        ident_b = nc.dram_tensor("ident_b", [128, 128], BF16, kind="ExternalInput").ap()
    else:
        ident_b = scr["ident_b"]
    x1_sc = nc.dram_tensor(
        "x1_sc", [TO, D], F32,
        kind="ExternalOutput" if standalone else "Internal").ap()

    if standalone:
        P = Prog(nc, chain=chain)
    es = ExitStack()
    with es:
        sb = lambda n, s, d=F32: es.enter_context(nc.sbuf_tensor('pb_' + n, s, d))
        ps = lambda n, s, d=F32: es.enter_context(nc.psum_tensor('pb_' + n, s, d))

        s_wgq = sb("s_wgq", [128, KT, I], BF16)
        s_wuq = sb("s_wuq", [128, KT, I], BF16)
        s_wdq = sb("s_wdq", [128, IT, D], BF16)
        s_idb = sb("s_idb", [128, 128], BF16)
        s_h = sb("s_h", [128, 2, I])
        s_u = sb("s_u", [128, HW])
        s_au = sb("s_au", [128, HW])
        s_c = sb("s_c", [128, HW])
        s_d = sb("s_d", [128, HW])
        s_m = sb("s_m", [128, 72])
        s_s = sb("s_s", [128, 72])
        s_rcp = sb("s_rcp", [128, 72])
        s_sh = sb("s_sh", [128, 72])
        s_qh = sb("s_qh", [128, 2, I], BF16)
        s_xqT = sb("s_xqT", [128, KT, 128], BF16)
        s_hqT = sb("s_hqT", [128, IT, 128], BF16)
        s_xni = sb("s_xni", [128, D])
        s_aoi = sb("s_aoi", [128, D])
        s_xi = sb("s_xi", [128, D])
        s_sg = sb("s_sg", [128, CH])

        p_g = ps("p_g", [128, CH])
        p_u = ps("p_u", [128, CH])
        p_t = [ps("p_t0", [128, 128], BF16), ps("p_t1", [128, 128], BF16)]
        p_fo = [ps("p_fo0", [128, CH]), ps("p_fo1", [128, CH])]

        LD = P.sem("LD_B")
        V, S, TE, DO = P.sem("V"), P.sem("S"), P.sem("TE"), P.sem("DO")
        TRC = P.sem("TRC")
        DW = P.sem("DW_B")
        DOB = P.sem("DO_B")
        DXN = P.sem("DXN_B")
        DAO = P.sem("DAO_B")
        DXI = P.sem("DXI_B")

        P.dma("sync", s_idb[:], ident_b, inc=LD)
        ld0 = P.count["LD_B"]

        st = {"v": P.count["V"], "te": P.count["TE"]}
        tr_guard = {}

        def quant(src_ap, dst_ap, W, wait_src=(), gwait=(), dst_guard=()):
            nh = (W + HW - 1) // HW
            last = None
            for hh in range(nh):
                w0, w1 = hh * HW, min((hh + 1) * HW, W)
                ww = w1 - w0
                G = ww // 16
                src3 = src_ap[:, w0:w1].rearrange("p (g k) -> p g k", k=16)
                dst3 = dst_ap[:, w0:w1].rearrange("p (g k) -> p g k", k=16)
                wfirst = (list(wait_src) + list(gwait)) if hh == 0 else []
                P.op("vector", lambda e, src3=src3, G=G: e.tensor_reduce(
                    out=s_m[:, 0:G], in_=src3, axis=AX.X, op=ALU.max,
                    apply_absolute_value=True), wait=wfirst)
                P.op("vector", lambda e, G=G: e.tensor_scalar(
                    out=s_s[:, 0:G], in0=s_m[:, 0:G], scalar1=1.0 / 6.0,
                    scalar2=1e-12, op0=ALU.mult, op1=ALU.max))
                P.recip(s_rcp[:, 0:G], s_s[:, 0:G], None)
                P.op("vector", lambda e, G=G: e.tensor_scalar(
                    out=s_sh[:, 0:G], in0=s_s[:, 0:G], scalar1=0.5,
                    scalar2=None, op0=ALU.mult))
                vv = P.op("vector", lambda e, src3=src3, G=G, ww=ww:
                          e.tensor_tensor(
                    out=s_u[:, 0:ww].rearrange("p (g k) -> p g k", k=16),
                    in0=src3,
                    in1=s_rcp[:, 0:G].unsqueeze(-1).broadcast_to((128, G, 16)),
                    op=ALU.mult), inc=V)
                sa = P.op("scalar", lambda e, ww=ww: e.activation(
                    out=s_au[:, 0:ww], in_=s_u[:, 0:ww], func=AF.Abs),
                    wait=[(V, vv)], inc=S)
                P.op("vector", lambda e, ww=ww: e.tensor_scalar(
                    out=s_c[:, 0:ww], in0=s_au[:, 0:ww], scalar1=0.25,
                    scalar2=None, op0=ALU.is_gt), wait=[(S, sa)])
                for t in (0.75, 1.25, 1.75):
                    P.op("vector", lambda e, t=t, ww=ww: e.scalar_tensor_tensor(
                        out=s_c[:, 0:ww], in0=s_au[:, 0:ww], scalar=t,
                        in1=s_c[:, 0:ww], op0=ALU.is_gt, op1=ALU.add))
                P.op("vector", lambda e, ww=ww: e.tensor_scalar(
                    out=s_d[:, 0:ww], in0=s_au[:, 0:ww], scalar1=2.5,
                    scalar2=None, op0=ALU.is_gt))
                P.op("vector", lambda e, ww=ww: e.scalar_tensor_tensor(
                    out=s_d[:, 0:ww], in0=s_au[:, 0:ww], scalar=3.5,
                    in1=s_d[:, 0:ww], op0=ALU.is_gt, op1=ALU.add))
                P.op("vector", lambda e, ww=ww: e.scalar_tensor_tensor(
                    out=s_c[:, 0:ww], in0=s_d[:, 0:ww], scalar=2.0,
                    in1=s_c[:, 0:ww], op0=ALU.mult, op1=ALU.add))
                P.op("vector", lambda e, ww=ww: e.tensor_scalar(
                    out=s_d[:, 0:ww], in0=s_au[:, 0:ww], scalar1=5.0,
                    scalar2=4.0, op0=ALU.is_gt, op1=ALU.mult))
                vv = P.op("vector", lambda e, ww=ww: e.tensor_add(
                    out=s_c[:, 0:ww], in0=s_c[:, 0:ww], in1=s_d[:, 0:ww]),
                    inc=V)
                sg_ = P.op("scalar", lambda e, ww=ww: e.sign(
                    out=s_au[:, 0:ww], in_=s_u[:, 0:ww]), wait=[(V, vv)], inc=S)
                P.op("vector", lambda e, ww=ww: e.tensor_mul(
                    out=s_c[:, 0:ww], in0=s_c[:, 0:ww], in1=s_au[:, 0:ww]),
                    wait=[(S, sg_)] + (list(dst_guard) if hh == 0 else []))
                last = P.op("vector", lambda e, dst3=dst3, G=G, ww=ww:
                            e.tensor_tensor(
                    out=dst3,
                    in0=s_c[:, 0:ww].rearrange("p (g k) -> p g k", k=16),
                    in1=s_sh[:, 0:G].unsqueeze(-1).broadcast_to((128, G, 16)),
                    op=ALU.mult), inc=V)
            return last

        dwc = [0]
        wq_v = {}
        slabs = ([(wg_in, s_wgq, sl, D, True) for sl in range(IT)]
                 + [(wu_in, s_wuq, sl, D, True) for sl in range(IT)]
                 + [(wd_in, s_wdq, sl, I, False) for sl in range(KT)])
        for gi, (wsrc, wdst, sl, wwidth, d_is_g) in enumerate(slabs):
            slot = gi % 2
            wld = [(V, wq_v.get(gi - 2, 0))]
            if gi == 0:
                wld.append((LD, ld0))
            dwc[0] += 16
            P.dma("sync", s_h[:, slot, 0:wwidth],
                  wsrc[sl * 128:(sl + 1) * 128, :], wait=wld, inc=DW)
            lastq = quant(s_h[:, slot, :], s_qh[:, slot, :], wwidth,
                          wait_src=[(DW, dwc[0])],
                          dst_guard=[(TE, st["te"])])
            wq_v[gi] = lastq
            ntr = wwidth // 128
            for j in range(ntr):
                idx = j % 2
                w = [(V, lastq)] if j == 0 else []
                g = tr_guard.get(idx)
                if g is not None:
                    w.append((TRC, g))
                tt = P.op("tensor", lambda e, j=j, idx=idx, slot=slot: e.transpose(
                    out=p_t[idx][:], in_=s_qh[:, slot, j * 128:(j + 1) * 128],
                    identity=s_idb[:]), wait=w, inc=TE)
                cc = P.op("scalar", lambda e, j=j, idx=idx, sl=sl, wdst=wdst:
                          e.activation(
                    out=wdst[:, j, sl * 128:(sl + 1) * 128],
                    in_=p_t[idx][:], func=AF.Identity),
                    wait=[(TE, tt)], inc=TRC)
                tr_guard[idx] = cc
            st["te"] = P.count["TE"]
        wq_done = (TRC, P.count["TRC"])

        tiles = n_own * 4
        mm_free_g = (S, 0)
        mm_free_u = (V, 0)
        fo_free = [(V, 0), (V, 0)]
        last_x1_dma = {}
        dxn_c, dao_c, dxi_c = [0], [0], [0]
        xnq_done = {}
        xqT_done = {}
        xq_tr_te = {}
        xq_used = {}
        te_after_def = {}
        hq_done = {}
        ev_done = {}
        dao_t = {}
        dxi_t = {}

        def stage_a(t):
            w = [(V, xnq_done.get(t - 1, 0))]
            if t == 0:
                w.append(wq_done)
            dxn_c[0] += 16
            P.dma("sync", s_xni[:], xn_in[t * 128:(t + 1) * 128, :],
                  wait=w, inc=DXN)
            lastq = quant(s_xni, s_qh[:, t % 2, :], D,
                          wait_src=[(DXN, dxn_c[0])],
                          dst_guard=[(TE, te_after_def.get(t - 2, 0))])
            xnq_done[t] = lastq


        def stage_b(t):
            for j in range(KT):
                idx = j % 2
                w = [(V, xnq_done[t])] if j == 0 else []
                if j == 0:
                    w.append((TE, xq_used.get(t - 2, 0)))
                g = tr_guard.get(idx)
                if g is not None:
                    w.append((TRC, g))
                tt = P.op("tensor", lambda e, j=j, idx=idx, t=t: e.transpose(
                    out=p_t[idx][:], in_=s_qh[:, t % 2, j * 128:(j + 1) * 128],
                    identity=s_idb[:]), wait=w, inc=TE)
                cc = P.op("scalar", lambda e, j=j, idx=idx, t=t: e.activation(
                    out=s_xqT[:, j, :], in_=p_t[idx][:], func=AF.Identity),
                    wait=[(TE, tt)], inc=TRC)
                tr_guard[idx] = cc
            xqT_done[t] = P.count["TRC"]
            xq_tr_te[t] = P.count["TE"]

        def stage_c(t):
            for nt in range(6):
                n0 = nt * CH
                nn_ = min(n0 + CH, I) - n0
                tg = tu = None
                for k in range(KT):
                    w = [(TRC, xqT_done[t]), mm_free_g2[0]] if k == 0 else []
                    r = P.op("tensor", lambda e, k=k, n0=n0, nn_=nn_, t=t: e.matmul(
                        out=p_g[:, 0:nn_], lhsT=s_xqT[:, k, :],
                        rhs=s_wgq[:, k, n0:n0 + nn_],
                        start=(k == 0), stop=(k == KT - 1)),
                        wait=w, inc=TE if k == KT - 1 else None)
                    if r is not None:
                        tg = r
                for k in range(KT):
                    w = [mm_free_u2[0]] if k == 0 else []
                    r = P.op("tensor", lambda e, k=k, n0=n0, nn_=nn_, t=t: e.matmul(
                        out=p_u[:, 0:nn_], lhsT=s_xqT[:, k, :],
                        rhs=s_wuq[:, k, n0:n0 + nn_],
                        start=(k == 0), stop=(k == KT - 1)),
                        wait=w, inc=TE if k == KT - 1 else None)
                    if r is not None:
                        tu = r
                ssg = P.op("scalar", lambda e, nn_=nn_: e.activation(
                    out=s_sg[:, 0:nn_], in_=p_g[:, 0:nn_], func=AF.Silu),
                    wait=[(TE, tg), mm_free_u2[0]], inc=S)
                mm_free_g2[0] = (S, ssg)
                vh = P.op("vector", lambda e, n0=n0, nn_=nn_, t=t: e.tensor_tensor(
                    out=s_h[:, t % 2, n0:n0 + nn_], in0=p_u[:, 0:nn_],
                    in1=s_sg[:, 0:nn_], op=ALU.mult),
                    wait=[(S, ssg), (TE, tu)], inc=V)
                mm_free_u2[0] = (V, vh)
            ev_done[t] = P.count["V"]
            xq_used[t] = P.count["TE"]

        def stage_d(t):
            dxi_c[0] += 16
            P.dma("gpsimd", s_xi[:], x_in[t * 128:(t + 1) * 128, :],
                  wait=[(V, last_lastv[0])], inc=DXI)
            dxi_t[t] = dxi_c[0]
            dao_c[0] += 16
            P.dma("gpsimd", s_aoi[:], ao_in[t * 128:(t + 1) * 128, :],
                  wait=[(DOB, last_x1_dma.get(t - 1, 0))], inc=DAO)
            dao_t[t] = dao_c[0]
            lastq = quant(s_h[:, t % 2, :], s_qh[:, t % 2, :], I,
                          wait_src=[(V, ev_done[t])],
                          dst_guard=[(TE, xq_tr_te[t])])
            hq_done[t] = lastq

        def stage_ef(t):
            lastq = hq_done[t]
            for j in range(IT):
                idx = j % 2
                w = [(V, lastq)] if j == 0 else []
                g = tr_guard.get(idx)
                if g is not None:
                    w.append((TRC, g))
                tt = P.op("tensor", lambda e, j=j, idx=idx, t=t: e.transpose(
                    out=p_t[idx][:], in_=s_qh[:, t % 2, j * 128:(j + 1) * 128],
                    identity=s_idb[:]), wait=w, inc=TE)
                cc = P.op("scalar", lambda e, j=j, idx=idx: e.activation(
                    out=s_hqT[:, j, :], in_=p_t[idx][:], func=AF.Identity),
                    wait=[(TE, tt)], inc=TRC)
                tr_guard[idx] = cc
            hqT_t = P.count["TRC"]
            lastv = None
            for n2 in range(2):
                tf = None
                for k in range(IT):
                    w = [(TRC, hqT_t), fo_free[n2]] if k == 0 else []
                    r = P.op("tensor", lambda e, k=k, n2=n2: e.matmul(
                        out=p_fo[n2][:], lhsT=s_hqT[:, k, :],
                        rhs=s_wdq[:, k, n2 * CH:(n2 + 1) * CH],
                        start=(k == 0), stop=(k == IT - 1)),
                        wait=w, inc=TE if k == IT - 1 else None)
                    if r is not None:
                        tf = r
                vv = P.op("vector", lambda e, n2=n2, t=t: e.tensor_tensor(
                    out=s_aoi[:, n2 * CH:(n2 + 1) * CH],
                    in0=p_fo[n2][:],
                    in1=s_aoi[:, n2 * CH:(n2 + 1) * CH],
                    op=ALU.add), wait=[(TE, tf), (DAO, dao_t[t])], inc=V)
                fo_free[n2] = (V, vv)
                lastv = P.op("vector", lambda e, n2=n2, t=t: e.tensor_add(
                    out=s_aoi[:, n2 * CH:(n2 + 1) * CH],
                    in0=s_aoi[:, n2 * CH:(n2 + 1) * CH],
                    in1=s_xi[:, n2 * CH:(n2 + 1) * CH]),
                    wait=[(DXI, dxi_t[t])] if n2 == 0 else [], inc=V)
                last_lastv[0] = lastv
            dd = P.dma("gpsimd", x1_sc[t * 128:(t + 1) * 128, :],
                       s_aoi[:], wait=[(V, lastv)], inc=DOB)
            last_x1_dma[t] = dd
            te_after_def[t] = P.count["TE"]

        mm_free_g2 = [mm_free_g]
        mm_free_u2 = [mm_free_u]
        last_lastv = [0]
        for t in range(tiles + 1):
            if t < tiles:
                stage_a(t)
                stage_b(t)
            if t >= 1:
                stage_d(t - 1)
            if t < tiles:
                stage_c(t)
            if t >= 1:
                stage_ef(t - 1)
        if standalone:
            P.op("sync", lambda e: e.nop(), wait=[(DOB, P.count["DO_B"])])
            P.build(es)
        scr2 = dict(scr or {})
        scr2["x1_sc"] = x1_sc
    return nc if standalone else scr2


def stage_b_inputs(inputs, core, n_own=9):
    import ml_dtypes
    x = np.asarray(inputs["L0_attn_x"])
    b, half = core // 2, core % 2
    seq = x[b]
    TO = n_own * CH
    if half == 1:
        x_own = seq[8192 - TO:]
    else:
        x_own = np.concatenate([np.zeros((CH, D), np.float32),
                                seq[:TO - CH]], 0)
    return dict(
        x_in=np.ascontiguousarray(x_own, dtype=np.float32),
        wg_in=np.asarray(inputs["L0_ffn_w_gate"], np.float32),
        wu_in=np.asarray(inputs["L0_ffn_w_up"], np.float32),
        wd_in=np.asarray(inputs["L0_ffn_w_down"], np.float32),
        ident_b=np.eye(128, dtype=np.float32).astype(ml_dtypes.bfloat16))




def build_phase_c(n_own=9, chain="ve", ctx=None, scr=None):
    standalone = ctx is None
    if standalone:
        nc = bass.Bass("TRN2", target_bir_lowering=False, debug=False)
    else:
        nc, P = ctx
    TO = n_own * CH
    NQ = n_own - 1
    if standalone:
        x1_in = nc.dram_tensor("x1_in", [TO, D], F32, kind="ExternalInput").ap()
    else:
        x1_in = scr["x1_sc"]
    w1_in = nc.dram_tensor("w1_in", [D, 3 * D], F32, kind="ExternalInput").ap()
    wo1_in = nc.dram_tensor("wo1_in", [D, D], F32, kind="ExternalInput").ap()
    gwT_in = nc.dram_tensor("gwT_in", [D, 16], F32, kind="ExternalInput").ap()
    sgwT_in = nc.dram_tensor("sgwT_in", [128, KT], F32, kind="ExternalInput").ap()
    if standalone:
        flag = nc.dram_tensor("flag", [128, 1], F32, kind="ExternalInput").ap()
        tri = nc.dram_tensor("tri", [128, 4, CH], F32, kind="ExternalInput").ap()
        ident_b = nc.dram_tensor("ident_b", [128, 128], BF16, kind="ExternalInput").ap()
    else:
        flag, tri, ident_b = scr["flag"], scr["tri"], scr["ident_b"]
    ident_f = nc.dram_tensor("ident_f", [128, 128], F32, kind="ExternalInput").ap()

    kind = "ExternalOutput" if standalone else "Internal"
    ao1_sc = nc.dram_tensor("ao1_sc", [NQ * CH, D], F32, kind=kind).ap()
    xn1T_sc = nc.dram_tensor("xn1T_sc", [NQ, 128, KT, CH], BF16, kind=kind).ap()
    cmbT_sc = nc.dram_tensor("cmbT_sc", [NQ, 4, 16, 128], F32, kind=kind).ap()
    gate_sc = nc.dram_tensor("gate_sc", [NQ, 128, 4], F32, kind=kind).ap()

    if standalone:
        P = Prog(nc, chain=chain)
    es = ExitStack()
    with es:
        sb = lambda n, s, d=F32: es.enter_context(nc.sbuf_tensor('pc_' + n, s, d))
        ps = lambda n, s, d=F32: es.enter_context(nc.psum_tensor('pc_' + n, s, d))

        s_w1 = sb("s_w1", [128, KT, 3 * D], BF16)
        s_wo1 = sb("s_wo1", [128, KT, D], BF16)
        s_gwT = sb("s_gwT", [128, KT, 16])
        s_sgwT = sb("s_sgwT", [128, KT])
        s_flag = sb("s_flag", [128, 1])
        s_tri = sb("s_tri", [128, 4, CH])
        s_idb = sb("s_idb", [128, 128], BF16)
        s_idf = sb("s_idf", [128, 128])
        s_x1 = sb("s_x1", [128, 4, D])
        s_xn1 = sb("s_xn1", [128, 4, D])
        s_xnb = sb("s_xnb", [128, 4, D], BF16)
        s_lncol = sb("s_lncol", [128, 8])
        s_sq2 = sb("s_sq2", [128, D])
        s_xn1T = sb("s_xn1T", [128, KT, CH], BF16)
        s_xTf = sb("s_xTf", [128, KT, 128])
        s_qf = sb("s_qf", [128, KT, CH], BF16)
        s_kr = [sb("s_kr0", [128, KT, CH], BF16), sb("s_kr1", [128, KT, CH], BF16)]
        s_vr = [sb("s_vr0", [128, 4, NH, HD + 1], BF16),
                sb("s_vr1", [128, 4, NH, HD + 1], BF16)]
        s_eSf = sb("s_eSf", [128, CH])
        s_eSb = sb("s_eSb", [128, 2, CH], BF16)
        s_dent = sb("s_dent", [1, CH])
        s_dscr = sb("s_dscr", [1, CH])
        s_drec = sb("s_drec", [1, CH])
        s_denb = sb("s_denb", [64, CH])
        s_attn = sb("s_attn", [128, KT, CH], BF16)
        s_pr = sb("s_pr", [128, 16])
        s_pc = sb("s_pc", [128, 10])
        s_cmb = sb("s_cmb", [128, 16])
        s_cmbT = sb("s_cmbT", [16, 128])
        s_gq = sb("s_gq", [128, 4])
        s_ones64 = sb("s_ones64c", [1, 64])

        _pt0 = ps("p_t0", [128, 128], BF16)
        p_t = [_pt0, _pt0]
        p_tf = ps("p_tf", [128, 128])
        p_S = [ps("p_S0", [128, CH]), ps("p_S1", [128, CH])]
        p_ctx = ps("p_ctx", [HD + 1, CH])
        p_mm = [ps("p_mm0", [128, CH]), ps("p_mm1", [128, CH])]
        p_bc = ps("p_bc", [128, CH])

        LD = P.sem("LD_C")
        LDC = P.sem("LDC_C")
        V, S, TE, DO = P.sem("V"), P.sem("S"), P.sem("TE"), P.sem("DO")
        TRC, SAV = P.sem("TRC"), P.sem("SAV")
        DX1 = P.sem("DX_C")
        nld = [0, 0]

        def load(out, in_, cast=False):
            if cast:
                P.dma("gpsimd", out, in_, inc=LDC)
                nld[1] += 1
            else:
                P.dma("sync", out, in_, inc=LD)
                nld[0] += 1

        load(s_w1[:], w1_in.rearrange("(k p) n -> p k n", p=128), cast=True)
        load(s_wo1[:], wo1_in.rearrange("(k p) n -> p k n", p=128), cast=True)
        load(s_gwT[:], gwT_in.rearrange("(k p) n -> p k n", p=128))
        load(s_sgwT[:], sgwT_in)
        load(s_flag[:], flag)
        load(s_tri[:], tri)
        load(s_idb[:], ident_b)
        load(s_idf[:], ident_f)
        LDA = [(LD, 16 * nld[0]), (LDC, 16 * nld[1])]

        P.op("vector", lambda e: e.memset(s_vr[0][:, :, :, HD:HD + 1], 1.0),
             wait=LDA)
        P.op("vector", lambda e: e.memset(s_vr[1][:, :, :, HD:HD + 1], 1.0))
        P.op("vector", lambda e: e.memset(s_ones64[:], 1.0))
        setup_v = (V, P.op("vector", lambda e: e.nop(), inc=V))

        tr_guard = {}
        dx_cnt = [0]
        state = {"te": 0, "v": 0}
        mm_free = [(V, 0), (V, 0)]
        S_free = [(S, 0), (S, 0)]
        eSb_guard = [0, 0]
        ctx_free = (V, 0)
        bc_free = (V, 0)
        tf_free = (V, 0)
        ring_guard = [(TE, 0), (TE, 0)]
        qf_guard = (TE, 0)
        attn_guard = (TE, 0)
        last_ao_dma = (DO, 0)
        last_xn1T_dma = (DO, 0)
        last_cmb_dma = (DO, 0)
        prev_S_ln = (S, 0)
        mm_turn = [0]

        def mm_bank():
            b = mm_turn[0] % 2
            mm_turn[0] += 1
            return b

        for c in range(n_own):
            slot = c % 2
            cq = c - 1
            wl = [(V, state["v"])] if c >= 1 else []
            if c >= 1:
                wl.append(last_ao_dma)
            dx_cnt[0] += 16
            P.dma("sync", s_x1[:], rearr_tok(x1_in, c * CH, CH),
                  wait=wl, inc=DX1)
            ldx = (DX1, dx_cnt[0])

            for i in range(4):
                w0 = [ldx] if i == 0 else []
                if i == 0 and c == 0:
                    w0.append(setup_v)
                vr0 = P.op("vector", lambda e, i=i: e.tensor_reduce(
                    out=s_lncol[:, 2 * i:2 * i + 1], in_=s_x1[:, i, :],
                    axis=AX.X, op=ALU.add), wait=w0, inc=V)
                vm = P.op("vector", lambda e, i=i: e.tensor_scalar(
                    out=s_lncol[:, 2 * i:2 * i + 1],
                    in0=s_lncol[:, 2 * i:2 * i + 1],
                    scalar1=1.0 / D, scalar2=None, op0=ALU.mult),
                    wait=[(V, vr0)], inc=V)
                wxc = [(V, vm)]
                if i == 0:
                    wxc.append(prev_S_ln)
                vv = P.op("vector", lambda e, i=i: e.tensor_scalar(
                    out=s_xn1[:, i, :], in0=s_x1[:, i, :],
                    scalar1=s_lncol[:, 2 * i:2 * i + 1], scalar2=None,
                    op0=ALU.subtract), wait=wxc, inc=V)
                ssq = P.op("scalar", lambda e, i=i: e.activation(
                    out=s_sq2[:], in_=s_xn1[:, i, :], func=AF.Square,
                    accum_out=s_lncol[:, 2 * i + 1:2 * i + 2]),
                    wait=[(V, vv)], inc=S)
                vv = P.op("vector", lambda e, i=i: e.tensor_scalar(
                    out=s_lncol[:, 2 * i + 1:2 * i + 2],
                    in0=s_lncol[:, 2 * i + 1:2 * i + 2],
                    scalar1=1.0 / D, scalar2=1e-5, op0=ALU.mult, op1=ALU.add),
                    wait=[(S, ssq)], inc=S if False else V)
                ss = P.op("scalar", lambda e, i=i: e.activation(
                    out=s_lncol[:, 2 * i + 1:2 * i + 2],
                    in_=s_lncol[:, 2 * i + 1:2 * i + 2], func=AF.Ln),
                    wait=[(V, vv)], inc=S)
                ss = P.op("scalar", lambda e, i=i: e.activation(
                    out=s_lncol[:, 2 * i + 1:2 * i + 2],
                    in_=s_lncol[:, 2 * i + 1:2 * i + 2], func=AF.Exp,
                    scale=-0.5), wait=[(S, ss)], inc=S)
                vv = P.op("vector", lambda e, i=i: e.tensor_scalar(
                    out=s_xn1[:, i, :], in0=s_xn1[:, i, :],
                    scalar1=s_lncol[:, 2 * i + 1:2 * i + 2], scalar2=None,
                    op0=ALU.mult), wait=[(S, ss)], inc=V)
                P.op("scalar", lambda e, i=i: e.activation(
                    out=s_xnb[:, i, :], in_=s_xn1[:, i, :], func=AF.Copy),
                    wait=[(V, vv)], inc=S)
            prev_S_ln = (S, P.count["S"])
            ln_s = (S, P.count["S"])

            tloc = []
            for i in range(4):
                for j in range(KT):
                    idx = i * KT + j
                    w = []
                    if idx == 0:
                        w.append(ln_s)
                        w.append(qf_guard)
                        if c >= 2:
                            w.append(last_xn1T_dma)
                    g = tr_guard.get(0)
                    if g is not None:
                        w.append((TRC, g))
                    tt = P.op("tensor", lambda e, i=i, j=j, idx=idx: e.transpose(
                        out=p_t[0][:],
                        in_=s_xnb[:, i, j * 128:(j + 1) * 128],
                        identity=s_idb[:]), wait=w, inc=TE)
                    cc = P.op("vector", lambda e, i=i, j=j, idx=idx: e.tensor_copy(
                        out=s_xn1T[:, j, i * 128:(i + 1) * 128],
                        in_=p_t[0][:]), wait=[(TE, tt)], inc=TRC)
                    tr_guard[0] = cc
                    tloc.append(cc)
            xn1T_done = (TRC, P.count["TRC"])
            if c >= 1:
                dd = P.dma("sync", xn1T_sc[cq], s_xn1T[:],
                           wait=[xn1T_done], inc=DO)
                last_xn1T_dma = (DO, dd)

            for part, coff in (("q", 0), ("k", D)):
                for hp in range(8):
                    b = mm_bank()
                    tt = None
                    for k in range(KT):
                        w = [xn1T_done, mm_free[b]] if k == 0 else []
                        if k == 0 and part == "k":
                            w.append(ring_guard[slot])
                        r = P.op("tensor", lambda e, k=k, hp=hp, b=b, coff=coff:
                                 e.matmul(out=p_mm[b][:],
                                          lhsT=s_w1[:, k, coff + hp * 128:coff + (hp + 1) * 128],
                                          rhs=s_xn1T[:, k, :],
                                          start=(k == 0), stop=(k == KT - 1)),
                                 wait=w, inc=TE if k == KT - 1 else None)
                        if r is not None:
                            tt = r
                    dst = s_qf if part == "q" else s_kr[slot]
                    vv = P.op("vector", lambda e, b=b, hp=hp, dst=dst: e.tensor_copy(
                        out=dst[:, hp, :], in_=p_mm[b][:]),
                        wait=[(TE, tt)] + ([qf_guard] if (part == "q" and hp == 0) else []),
                        inc=V)
                    mm_free[b] = (V, vv)
            for i in range(4):
                for n in range(2):
                    b = mm_bank()
                    tt = None
                    for k in range(KT):
                        w = [xn1T_done, mm_free[b]] if k == 0 else []
                        r = P.op("tensor", lambda e, i=i, n=n, k=k, b=b: e.matmul(
                            out=p_mm[b][:],
                            lhsT=s_xn1T[:, k, i * 128:(i + 1) * 128],
                            rhs=s_w1[:, k, 2 * D + n * CH:2 * D + (n + 1) * CH],
                            start=(k == 0), stop=(k == KT - 1)),
                            wait=w, inc=TE if k == KT - 1 else None)
                        if r is not None:
                            tt = r
                    w = [(TE, tt)]
                    if i == 0 and n == 0:
                        w.append(ring_guard[slot])
                    vv = P.op("vector", lambda e, i=i, n=n, b=b, slot=slot:
                              e.tensor_copy(
                        out=s_vr[slot][:, i, 8 * n:8 * (n + 1), 0:HD],
                        in_=p_mm[b][:].rearrange("p (h d) -> p h d", d=HD)),
                        wait=w, inc=V)
                    mm_free[b] = (V, vv)
            kv_done = (V, P.count["V"])

            if c >= 1:
                for h in range(NH):
                    hp, half = h // 2, h % 2
                    r0 = 64 * half
                    ctx_started = False
                    for js in range(8):
                        ring = (c - 1 + js // 4) % 2
                        j = js % 4
                        eslot = js % 2
                        tt = P.op("tensor", lambda e, hp=hp, r0=r0, j=j,
                                  ring=ring, eslot=eslot: e.matmul(
                            out=p_S[eslot][:],
                            lhsT=s_kr[ring][r0:r0 + 64, hp, j * 128:(j + 1) * 128],
                            rhs=s_qf[r0:r0 + 64, hp, :],
                            start=True, stop=True),
                            wait=[S_free[eslot], kv_done] if js == 0 else [S_free[eslot]],
                            inc=TE)
                        wv = [(TE, tt)]
                        if eSb_guard[eslot] > 0:
                            wv.append((SAV, eSb_guard[eslot]))
                        masked = (js < 4 and c == 1) or js >= 4
                        if js < 4 and c == 1:
                            se = P.op("scalar", lambda e, eslot=eslot: e.activation(
                                out=s_eSf[:], in_=p_S[eslot][:], func=AF.Exp,
                                scale=0.125), wait=wv, inc=S)
                            S_free[eslot] = (S, se)
                            vv = P.op("vector", lambda e, eslot=eslot: e.tensor_scalar(
                                out=s_eSb[:, eslot, :], in0=s_eSf[:],
                                scalar1=s_flag[:, 0:1], scalar2=None,
                                op0=ALU.mult), wait=[(S, se)], inc=V)
                        elif js >= 4:
                            se = P.op("scalar", lambda e, eslot=eslot: e.activation(
                                out=s_eSf[:], in_=p_S[eslot][:], func=AF.Exp,
                                scale=0.125), wait=wv, inc=S)
                            S_free[eslot] = (S, se)
                            vv = P.op("vector", lambda e, eslot=eslot, j=j:
                                      e.tensor_tensor(
                                out=s_eSb[:, eslot, :], in0=s_eSf[:],
                                in1=s_tri[:, j, :], op=ALU.mult),
                                wait=[(S, se)], inc=V)
                        else:
                            se = P.op("scalar", lambda e, eslot=eslot: e.activation(
                                out=s_eSb[:, eslot, :], in_=p_S[eslot][:], func=AF.Exp,
                                scale=0.125), wait=wv, inc=S)
                            S_free[eslot] = (S, se)
                            vv = P.op("vector", lambda e: e.nop(),
                                      wait=[(S, se)], inc=V)
                        av = P.op("tensor", lambda e, h=h, j=j, ring=ring,
                                  eslot=eslot, st=(not ctx_started),
                                  sp=(js == 7): e.matmul(
                            out=p_ctx[:], lhsT=s_vr[ring][:, j, h, :],
                            rhs=s_eSb[:, eslot, :], start=st, stop=sp),
                            wait=[(V, vv)] + ([ctx_free] if not ctx_started else []),
                            inc=SAV)
                        eSb_guard[eslot] = av
                        ctx_started = True
                    ctx_t = (SAV, P.count["SAV"])
                    vv = P.op("vector", lambda e: e.tensor_copy(
                        out=s_dent[:], in_=p_ctx[HD:HD + 1, :]),
                        wait=[ctx_t], inc=V)
                    sd_ = P.op("scalar", lambda e: e.activation(
                        out=s_dscr[:], in_=s_dent[:], func=AF.Ln),
                        wait=[(V, vv)], inc=S)
                    sd_ = P.op("scalar", lambda e: e.activation(
                        out=s_drec[:], in_=s_dscr[:], func=AF.Exp, scale=-1.0),
                        wait=[(S, sd_)], inc=S)
                    tb = P.op("tensor", lambda e: e.matmul(
                        out=p_bc[0:64, :], lhsT=s_ones64[:], rhs=s_drec[:],
                        start=True, stop=True), wait=[(S, sd_), bc_free],
                        inc=TE, drain_inc=True)
                    vv = P.op("vector", lambda e: e.tensor_copy(
                        out=s_denb[:], in_=p_bc[0:64, :]), wait=[(TE, tb)], inc=V)
                    bc_free = (V, vv)
                    wat = [(V, vv)]
                    if h == 0:
                        wat.append(attn_guard)
                    av2 = P.op("vector", lambda e, hp=hp, r0=r0: e.tensor_tensor(
                        out=s_attn[r0:r0 + 64, hp, :], in0=p_ctx[0:HD, :],
                        in1=s_denb[:], op=ALU.mult), wait=wat, inc=V)
                    ctx_free = (V, av2)
                attn_done = (V, P.count["V"])
                last_ev = None
                for i in range(4):
                    for n in range(2):
                        b = mm_bank()
                        tt = None
                        for k in range(KT):
                            w = [attn_done, mm_free[b]] if k == 0 else []
                            r = P.op("tensor", lambda e, i=i, n=n, k=k, b=b: e.matmul(
                                out=p_mm[b][:],
                                lhsT=s_attn[:, k, i * 128:(i + 1) * 128],
                                rhs=s_wo1[:, k, n * CH:(n + 1) * CH],
                                start=(k == 0), stop=(k == KT - 1)),
                                wait=w, inc=TE if k == KT - 1 else None)
                            if r is not None:
                                tt = r
                        last_ev = P.op("vector", lambda e, i=i, n=n, b=b:
                                       e.tensor_copy(
                            out=s_x1[:, i, n * CH:(n + 1) * CH],
                            in_=p_mm[b][:]), wait=[(TE, tt)], inc=V)
                        mm_free[b] = (V, last_ev)
                attn_guard = (TE, P.count["TE"])
                dd = P.dma("sync", rearr_tok(ao1_sc, cq * CH, CH), s_x1[:],
                           wait=[(V, last_ev)], inc=DO)
                last_ao_dma = (DO, dd)

                for i in range(4):
                    lt = None
                    for j in range(KT):
                        w = [ln_s] if j == 0 else []
                        if j == 0:
                            w.append(tf_free)
                        if j >= 1:
                            w.append((TRC, lt))
                        ttf = P.op("tensor", lambda e, i=i, j=j: e.transpose(
                            out=p_tf[:], in_=s_xn1[:, i, j * 128:(j + 1) * 128],
                            identity=s_idf[:]), wait=w, inc=TE, drain_inc=True)
                        lt = P.op("vector", lambda e, j=j: e.tensor_copy(
                            out=s_xTf[:, j, :], in_=p_tf[:]),
                            wait=[(TE, ttf)], inc=TRC)
                    tf_free = (TRC, lt)
                    tl = None
                    for k in range(KT):
                        w = [(TRC, lt), bc_free] if k == 0 else []
                        r = P.op("tensor", lambda e, k=k: e.matmul(
                            out=p_bc[:, 0:16], lhsT=s_xTf[:, k, :],
                            rhs=s_gwT[:, k, :], start=(k == 0),
                            stop=(k == KT - 1)),
                            wait=w, inc=TE if k == KT - 1 else None)
                        if r is not None:
                            tl = r
                    tg2 = None
                    for k in range(KT):
                        r = P.op("tensor", lambda e, k=k: e.matmul(
                            out=p_bc[:, 16:17], lhsT=s_xTf[:, k, :],
                            rhs=s_sgwT[:, k:k + 1], start=(k == 0),
                            stop=(k == KT - 1)),
                            inc=TE if k == KT - 1 else None, drain_inc=True)
                        if r is not None:
                            tg2 = r
                    vv = P.op("vector", lambda e: e.tensor_copy(
                        out=s_pr[:], in_=p_bc[:, 0:16]), wait=[(TE, tl)], inc=V)
                    vv = P.op("vector", lambda e: e.tensor_reduce(
                        out=s_pc[:, 0:1], in_=s_pr[:], axis=AX.X, op=ALU.max),
                        wait=[(V, vv)], inc=V)
                    vv = P.op("vector", lambda e: e.tensor_scalar(
                        out=s_pc[:, 8:9], in0=s_pc[:, 0:1], scalar1=-1.0,
                        scalar2=None, op0=ALU.mult), wait=[(V, vv)], inc=V)
                    sv = P.op("scalar", lambda e: e.activation(
                        out=s_pr[:], in_=s_pr[:], func=AF.Exp,
                        bias=s_pc[:, 8:9]), wait=[(V, vv)], inc=S)
                    vv = P.op("vector", lambda e: e.tensor_reduce(
                        out=s_pc[:, 1:2], in_=s_pr[:], axis=AX.X, op=ALU.add),
                        wait=[(S, sv)], inc=V)
                    vv = P.recip(s_pc[:, 2:3], s_pc[:, 1:2], s_pc[:, 9:10],
                                 wait=[(V, vv)], inc=V)
                    vv = P.op("vector", lambda e: e.tensor_scalar(
                        out=s_pr[:], in0=s_pr[:], scalar1=s_pc[:, 2:3],
                        scalar2=None, op0=ALU.mult), wait=[(V, vv)], inc=V)
                    vv = P.op("vector", lambda e: e.tensor_reduce(
                        out=s_pc[:, 3:4], in_=s_pr[:], axis=AX.X, op=ALU.max),
                        wait=[(V, vv)], inc=V)
                    vv = P.op("vector", lambda e: e.tensor_scalar(
                        out=s_cmb[:], in0=s_pr[:], scalar1=s_pc[:, 3:4],
                        scalar2=None, op0=ALU.is_equal), wait=[(V, vv)], inc=V)
                    vv = P.op("vector", lambda e: e.scalar_tensor_tensor(
                        out=s_pr[:], in0=s_cmb[:], scalar=-2.0, in1=s_pr[:],
                        op0=ALU.mult, op1=ALU.add), wait=[(V, vv)], inc=V)
                    vv = P.op("vector", lambda e: e.tensor_reduce(
                        out=s_pc[:, 4:5], in_=s_pr[:], axis=AX.X, op=ALU.max),
                        wait=[(V, vv)], inc=V)
                    vv = P.op("vector", lambda e: e.tensor_add(
                        out=s_pc[:, 5:6], in0=s_pc[:, 3:4], in1=s_pc[:, 4:5]),
                        wait=[(V, vv)], inc=V)
                    vv = P.recip(s_pc[:, 6:7], s_pc[:, 5:6], s_pc[:, 9:10],
                                 wait=[(V, vv)], inc=V)
                    vv = P.op("vector", lambda e: e.tensor_mul(
                        out=s_pc[:, 7:8], in0=s_pc[:, 4:5], in1=s_pc[:, 6:7]),
                        wait=[(V, vv)], inc=V)
                    vv = P.op("vector", lambda e: e.tensor_mul(
                        out=s_pc[:, 6:7], in0=s_pc[:, 3:4], in1=s_pc[:, 6:7]),
                        wait=[(V, vv)], inc=V)
                    vv = P.op("vector", lambda e: e.tensor_scalar(
                        out=s_cmb[:], in0=s_cmb[:], scalar1=s_pc[:, 6:7],
                        scalar2=None, op0=ALU.mult), wait=[(V, vv)], inc=V)
                    vv = P.op("vector", lambda e: e.tensor_scalar(
                        out=s_pr[:], in0=s_pr[:], scalar1=s_pc[:, 4:5],
                        scalar2=None, op0=ALU.is_equal), wait=[(V, vv)], inc=V)
                    vv = P.op("vector", lambda e: e.scalar_tensor_tensor(
                        out=s_cmb[:], in0=s_pr[:], scalar=s_pc[:, 7:8],
                        in1=s_cmb[:], op0=ALU.mult, op1=ALU.add),
                        wait=[(V, vv)], inc=V)
                    ttf = P.op("tensor", lambda e: e.transpose(
                        out=p_tf[0:16, :], in_=s_cmb[:], identity=s_idf[:]),
                        wait=[(V, vv), tf_free], inc=TE, drain_inc=True)
                    vv = P.op("vector", lambda e: e.tensor_copy(
                        out=s_cmbT[:], in_=p_tf[0:16, :]),
                        wait=[(TE, ttf), last_cmb_dma], inc=V)
                    tf_free = (V, vv)
                    dd1 = P.dma("sync", cmbT_sc[cq, i], s_cmbT[:],
                                wait=[(V, vv)], inc=DO)
                    last_cmb_dma = (DO, dd1)
                    sv = P.op("scalar", lambda e, i=i: e.activation(
                        out=s_gq[:, i:i + 1], in_=p_bc[:, 16:17], func=AF.Exp,
                        scale=-1.0), wait=[(TE, tg2)], inc=S)
                    vv = P.op("vector", lambda e, i=i: e.tensor_scalar(
                        out=s_gq[:, i:i + 1], in0=s_gq[:, i:i + 1],
                        scalar1=1.0, scalar2=None, op0=ALU.add),
                        wait=[(S, sv)], inc=V)
                    vv = P.op("vector", lambda e, i=i: e.reciprocal(
                        out=s_gq[:, i:i + 1], in_=s_gq[:, i:i + 1]),
                        wait=[(V, vv)], inc=V)
                    bc_free = (V, vv)
                dd2 = P.dma("sync", gate_sc[cq], s_gq[:],
                            wait=[(V, P.count["V"])], inc=DO)
            else:
                attn_done = (V, P.count["V"])
            ring_guard[(c - 1) % 2] = (TE, P.count["TE"])
            qf_guard = (TE, P.count["TE"])
            state["v"] = P.count["V"]
        if standalone:
            P.op("sync", lambda e: e.nop(), wait=[(DO, P.count["DO"])])
            P.build(es)
        scr2 = dict(scr or {})
        scr2.update(ao1_sc=ao1_sc, xn1T_sc=xn1T_sc, cmbT_sc=cmbT_sc,
                    gate_sc=gate_sc)
    return nc if standalone else scr2


def stage_c_inputs(inputs, core, n_own=9):
    import ml_dtypes
    KTC = 8
    flagv = np.full((128, 1), float(core % 2), np.float32)
    tri = np.zeros((128, 4, CH), np.float32)
    ar = np.arange(CH)
    for j in range(4):
        for p in range(128):
            tri[p, j, :] = (j * 128 + p <= ar)
    return dict(
        w1_in=np.asarray(inputs["L1_attn_w_qkv"], np.float32),
        wo1_in=np.asarray(inputs["L1_attn_w_out"], np.float32),
        gwT_in=np.ascontiguousarray(np.asarray(inputs["L1_ffn_gate_w"], np.float32).T),
        sgwT_in=np.ascontiguousarray(np.asarray(inputs["L1_ffn_shared_gate_w"], np.float32).reshape(KTC, 128).T),
        flag=flagv, tri=tri,
        ident_b=np.eye(128, dtype=np.float32).astype(ml_dtypes.bfloat16),
        ident_f=np.eye(128, dtype=np.float32))




def build_phase_d1(n_own=9, chain="ve", ctx=None, scr=None):
    standalone = ctx is None
    if standalone:
        nc = bass.Bass("TRN2", target_bir_lowering=False, debug=False)
    else:
        nc, P = ctx
    NQ = n_own - 1
    TQ = NQ * CH
    I = 2816
    IT = I // 128
    if standalone:
        xn1T_in = nc.dram_tensor("xn1T_in", [NQ, 128, KT, CH], BF16, kind="ExternalInput").ap()
        x1_in = nc.dram_tensor("x1_in", [n_own * CH, D], F32, kind="ExternalInput").ap()
        ao1_in = nc.dram_tensor("ao1_in", [TQ, D], F32, kind="ExternalInput").ap()
        gate_in = nc.dram_tensor("gate_in", [NQ, 128, 4], F32, kind="ExternalInput").ap()
    else:
        xn1T_in, x1_in = scr["xn1T_sc"], scr["x1_sc"]
        ao1_in, gate_in = scr["ao1_sc"], scr["gate_sc"]
    swgT_in = nc.dram_tensor("swgT_in", [D, I], BF16, kind="ExternalInput").ap()
    swuT_in = nc.dram_tensor("swuT_in", [D, I], BF16, kind="ExternalInput").ap()
    swdT_in = nc.dram_tensor("swdT_in", [I, D], BF16, kind="ExternalInput").ap()
    po_sc = nc.dram_tensor(
        "po_sc", [TQ, D], F32,
        kind="ExternalOutput" if standalone else "Internal").ap()

    if standalone:
        P = Prog(nc, chain=chain)
    es = ExitStack()
    with es:
        sb = lambda n, s, d=F32: es.enter_context(nc.sbuf_tensor('pd_' + n, s, d))
        ps = lambda n, s, d=F32: es.enter_context(nc.psum_tensor('pd_' + n, s, d))
        s_swg = sb("s_swg", [128, KT, I], BF16)
        s_swu = sb("s_swu", [128, KT, I], BF16)
        s_swd = sb("s_swd", [128, IT, D], BF16)
        s_xT = sb("s_xT", [128, KT, CH], BF16)
        s_sh = sb("s_sh", [128, IT, CH], BF16)
        s_sg = sb("s_sg", [128, CH])
        s_x1 = sb("s_x1", [128, 4, D])
        s_ao = sb("s_ao", [128, 4, D])
        s_gt = sb("s_gt", [128, 4])
        p_g = ps("p_g", [128, CH])
        p_u = ps("p_u", [128, CH])
        p_fo = ps("p_fo", [128, CH])

        LDC = P.sem("LDC_D1")
        V, S, TE, DO = P.sem("V"), P.sem("S"), P.sem("TE"), P.sem("DO")
        DX = P.sem("DX_D1")
        dxc = [0]

        P.dma("gpsimd", s_swg[:], swgT_in.rearrange("(k p) n -> p k n", p=128), inc=LDC)
        P.dma("scalar", s_swu[:], swuT_in.rearrange("(k p) n -> p k n", p=128), inc=LDC)
        P.dma("gpsimd", s_swd[:], swdT_in.rearrange("(k p) n -> p k n", p=128), inc=LDC)
        LDA = [(LDC, 48)]

        mm_free_g = (S, 0)
        mm_free_u = (V, 0)
        fo_free = (V, 0)
        last_po_dma = (DO, 0)
        state_v = 0
        state_te = 0
        for cq in range(NQ):
            wl = [(TE, state_te), (V, state_v)] if cq else []
            dxc[0] += 16
            P.dma("sync", s_xT[:], xn1T_in[cq], wait=wl, inc=DX)
            dxc[0] += 16
            P.dma("sync", s_x1[:], rearr_tok(x1_in, (cq + 1) * CH, CH), inc=DX)
            dxc[0] += 16
            P.dma("sync", s_ao[:], rearr_tok(ao1_in, cq * CH, CH), inc=DX)
            dxc[0] += 16
            P.dma("sync", s_gt[:], gate_in[cq], inc=DX)
            ldx = (DX, dxc[0])
            for m in range(IT):
                tg = tu = None
                for k in range(KT):
                    w = ([ldx] + LDA if (cq == 0 and m == 0) else [ldx]) if k == 0 else []
                    if k == 0:
                        w = w + [mm_free_g]
                    r = P.op("tensor", lambda e, k=k, m=m: e.matmul(
                        out=p_g[:], lhsT=s_swg[:, k, m * 128:(m + 1) * 128],
                        rhs=s_xT[:, k, :], start=(k == 0), stop=(k == KT - 1)),
                        wait=w, inc=TE if k == KT - 1 else None)
                    if r is not None:
                        tg = r
                for k in range(KT):
                    w = [mm_free_u] if k == 0 else []
                    r = P.op("tensor", lambda e, k=k, m=m: e.matmul(
                        out=p_u[:], lhsT=s_swu[:, k, m * 128:(m + 1) * 128],
                        rhs=s_xT[:, k, :], start=(k == 0), stop=(k == KT - 1)),
                        wait=w, inc=TE if k == KT - 1 else None)
                    if r is not None:
                        tu = r
                ssg = P.op("scalar", lambda e: e.activation(
                    out=s_sg[:], in_=p_g[:], func=AF.Silu),
                    wait=[(TE, tg), mm_free_u], inc=S)
                mm_free_g = (S, ssg)
                vh = P.op("vector", lambda e, m=m: e.tensor_tensor(
                    out=s_sh[:, m, :], in0=p_u[:], in1=s_sg[:],
                    op=ALU.mult), wait=[(S, ssg), (TE, tu)], inc=V)
                mm_free_u = (V, vh)
            sh_done = (V, P.count["V"])
            lastv = None
            for i in range(4):
                for n2 in range(2):
                    tf = None
                    for k in range(IT):
                        w = [sh_done, fo_free] if k == 0 else []
                        r = P.op("tensor", lambda e, k=k, i=i, n2=n2: e.matmul(
                            out=p_fo[:],
                            lhsT=s_sh[:, k, i * 128:(i + 1) * 128],
                            rhs=s_swd[:, k, n2 * CH:(n2 + 1) * CH],
                            start=(k == 0), stop=(k == IT - 1)),
                            wait=w, inc=TE if k == IT - 1 else None)
                        if r is not None:
                            tf = r
                    vv = P.op("vector", lambda e, i=i, n2=n2: e.tensor_scalar(
                        out=s_sg[:], in0=p_fo[:], scalar1=s_gt[:, i:i + 1],
                        scalar2=None, op0=ALU.mult),
                        wait=[(TE, tf)], inc=V)
                    fo_free = (V, vv)
                    vv = P.op("vector", lambda e, i=i, n2=n2: e.tensor_tensor(
                        out=s_ao[:, i, n2 * CH:(n2 + 1) * CH], in0=s_sg[:],
                        in1=s_ao[:, i, n2 * CH:(n2 + 1) * CH], op=ALU.add),
                        wait=[last_po_dma] if (i == 0 and n2 == 0) else [], inc=V)
                    lastv = P.op("vector", lambda e, i=i, n2=n2: e.tensor_add(
                        out=s_ao[:, i, n2 * CH:(n2 + 1) * CH],
                        in0=s_ao[:, i, n2 * CH:(n2 + 1) * CH],
                        in1=s_x1[:, i, n2 * CH:(n2 + 1) * CH]), inc=V)
            dd = P.dma("sync", rearr_tok(po_sc, cq * CH, CH), s_ao[:],
                       wait=[(V, lastv)], inc=DO)
            last_po_dma = (DO, dd)
            state_v = P.count["V"]
            state_te = P.count["TE"]
        if standalone:
            P.op("sync", lambda e: e.nop(), wait=[(DO, P.count["DO"])])
            P.build(es)
        scr2 = dict(scr or {})
        scr2["po_sc"] = po_sc
    return nc if standalone else scr2


def build_phase_d2(n_own=9, chain="ve", ctx=None, scr=None):
    standalone = ctx is None
    if standalone:
        nc = bass.Bass("TRN2", target_bir_lowering=False, debug=False)
    else:
        nc, P = ctx
    NQ = n_own - 1
    TQ = NQ * CH
    NHALF = NQ // 2
    if standalone:
        xn1T_in = nc.dram_tensor("xn1T_in", [NQ, 128, KT, CH], BF16, kind="ExternalInput").ap()
        cmbT_in = nc.dram_tensor("cmbT_in", [NQ, 4, 16, 128], F32, kind="ExternalInput").ap()
        po_in = nc.dram_tensor("po_in", [TQ, D], F32, kind="ExternalInput").ap()
    else:
        xn1T_in, cmbT_in, po_in = scr["xn1T_sc"], scr["cmbT_sc"], scr["po_sc"]
    wgu_in = nc.dram_tensor("wgu16_in", [16, D, D], BF16, kind="ExternalInput").ap()
    wd_in = nc.dram_tensor("wd16_in", [16, CH, D], BF16, kind="ExternalInput").ap()
    out_sc = nc.dram_tensor("out_sc", [TQ, D], F32, kind="ExternalOutput").ap()

    if standalone:
        P = Prog(nc, chain=chain)
    es = ExitStack()
    with es:
        sb = lambda n, s, d=F32: es.enter_context(nc.sbuf_tensor('pe_' + n, s, d))
        ps = lambda n, s, d=F32: es.enter_context(nc.psum_tensor('pe_' + n, s, d))
        s_xh = sb("s_xh", [128, KT, NHALF * CH], BF16)
        s_acc = sb("s_acc", [128, NHALF * 4, D])
        s_wgu = [sb("s_wgu0", [128, KT, D], BF16), sb("s_wgu1", [128, KT, D], BF16)]
        s_wd = [sb("s_wd0", [128, 4, D], BF16), sb("s_wd1", [128, 4, D], BF16)]
        s_ce = sb("s_ce", [1, NHALF * CH])
        s_cb = sb("s_cb", [128, CH])
        s_sg = sb("s_sg", [128, 2, CH])
        s_hec = sb("s_hec", [128, 4, CH], BF16)
        s_ones = sb("s_ones1", [1, 128])
        s_po = sb("s_po", [128, 4, D])
        p_g = [ps("p_g0", [128, CH]), ps("p_g1", [128, CH])]
        p_u = [ps("p_u0", [128, CH]), ps("p_u1", [128, CH])]
        p_cb = ps("p_cb", [128, CH])
        p_o = [ps("p_o0", [128, CH]), ps("p_o1", [128, CH])]

        V, S, TE, DO = P.sem("V"), P.sem("S"), P.sem("TE"), P.sem("DO")
        DX = P.sem("DX_D2")
        DW = [P.sem("DW0_D2"), P.sem("DW1_D2")]
        DC = P.sem("DC_D2")
        dxc, dwc, dcc = [0], [0, 0], [0]
        state_v = 0
        state_te = 0
        mm_free_g = [(S, 0), (S, 0)]
        mm_free_u = [(V, 0), (V, 0)]
        cb_free = (V, 0)
        sg_free = [(V, 0), (V, 0)]
        o_free = [(V, 0), (V, 0)]
        hec_guard = (TE, 0)
        last_out_dma = (DO, 0)

        vo = P.op("vector", lambda e: e.memset(s_ones[:], 1.0), inc=V)
        ones_ready = (V, vo)

        for h2 in range(2):
            w = [(TE, state_te)] if h2 else []
            for q in range(NHALF):
                dxc[0] += 16
                P.dma("sync", s_xh[:, :, q * CH:(q + 1) * CH],
                      xn1T_in[h2 * NHALF + q], wait=w if q == 0 else [],
                      inc=DX)
            ldx = (DX, dxc[0])
            vz = P.op("vector", lambda e: e.memset(s_acc[:], 0.0),
                      wait=[(V, state_v)], inc=V)
            acc_ready = (V, vz)
            for e_ in range(16):
                buf = e_ % 2
                wld = [hec_guard] if (e_ >= 2 or h2) else []
                dwc[buf] += 16
                P.dma("gpsimd", s_wgu[buf][:],
                      wgu_in[e_].rearrange("(k p) n -> p k n", p=128),
                      wait=wld, inc=DW[buf])
                dwc[buf] += 16
                P.dma("scalar", s_wd[buf][:],
                      wd_in[e_].rearrange("(k p) n -> p k n", p=128),
                      wait=wld, inc=DW[buf])
                ldw = (DW[buf], dwc[buf])
                dcc[0] += 16
                P.dma("sync",
                      s_ce[:].rearrange("o (q i p) -> o q i p", q=NHALF, i=4),
                      cmbT_in[h2 * NHALF:(h2 + 1) * NHALF, :, e_, :].unsqueeze(0),
                      wait=[cb_free], inc=DC)
                ldc = (DC, dcc[0])
                for tg in range(NHALF):
                    tb = P.op("tensor", lambda e, tg=tg: e.matmul(
                        out=p_cb[:], lhsT=s_ones[:],
                        rhs=s_ce[0:1, tg * CH:(tg + 1) * CH],
                        start=True, stop=True),
                        wait=[ldc, cb_free, ones_ready], inc=TE,
                        drain_inc=True)
                    vcb = P.op("vector", lambda e: e.tensor_copy(
                        out=s_cb[:], in_=p_cb[:]), wait=[(TE, tb)], inc=V)
                    cb_free = (V, vcb)
                    for mg in range(4):
                        gb = mg % 2
                        tgm = tum = None
                        for k in range(KT):
                            w = [ldx, ldw, mm_free_g[gb]] if k == 0 else []
                            r = P.op("tensor", lambda e, k=k, mg=mg, tg=tg, buf=buf,
                                     gb=gb: e.matmul(
                                out=p_g[gb][:],
                                lhsT=s_wgu[buf][:, k, mg * 128:(mg + 1) * 128],
                                rhs=s_xh[:, k, tg * CH:(tg + 1) * CH],
                                start=(k == 0), stop=(k == KT - 1)),
                                wait=w, inc=TE if k == KT - 1 else None)
                            if r is not None:
                                tgm = r
                        for k in range(KT):
                            w = [mm_free_u[gb]] if k == 0 else []
                            r = P.op("tensor", lambda e, k=k, mg=mg, tg=tg, buf=buf,
                                     gb=gb: e.matmul(
                                out=p_u[gb][:],
                                lhsT=s_wgu[buf][:, k, CH + mg * 128:CH + (mg + 1) * 128],
                                rhs=s_xh[:, k, tg * CH:(tg + 1) * CH],
                                start=(k == 0), stop=(k == KT - 1)),
                                wait=w, inc=TE if k == KT - 1 else None)
                            if r is not None:
                                tum = r
                        ssg = P.op("scalar", lambda e, gb=gb, mg=mg: e.activation(
                            out=s_sg[:, mg % 2, :], in_=p_g[gb][:], func=AF.Silu),
                            wait=[(TE, tgm), sg_free[mg % 2]], inc=S)
                        mm_free_g[gb] = (S, ssg)
                        vh = P.op("vector", lambda e, gb=gb, mg=mg: e.tensor_tensor(
                            out=s_sg[:, mg % 2, :], in0=p_u[gb][:],
                            in1=s_sg[:, mg % 2, :],
                            op=ALU.mult), wait=[(S, ssg), (TE, tum)], inc=V)
                        mm_free_u[gb] = (V, vh)
                        vh = P.op("vector", lambda e, mg=mg: e.tensor_tensor(
                            out=s_hec[:, mg, :], in0=s_sg[:, mg % 2, :],
                            in1=s_cb[:], op=ALU.mult),
                            wait=[(V, vcb)] + ([hec_guard] if mg == 0 else []),
                            inc=V)
                        sg_free[mg % 2] = (V, vh)
                    hec_done = (V, P.count["V"])
                    for tt in range(4):
                        for n2 in range(2):
                            ob = (tt * 2 + n2) % 2
                            tf = None
                            for k in range(4):
                                w = [hec_done, o_free[ob]] if k == 0 else []
                                r = P.op("tensor", lambda e, k=k, tt=tt, n2=n2,
                                         ob=ob, buf=buf: e.matmul(
                                    out=p_o[ob][:],
                                    lhsT=s_hec[:, k, tt * 128:(tt + 1) * 128],
                                    rhs=s_wd[buf][:, k, n2 * CH:(n2 + 1) * CH],
                                    start=(k == 0), stop=(k == 3)),
                                    wait=w, inc=TE if k == 3 else None)
                                if r is not None:
                                    tf = r
                            vv = P.op("vector", lambda e, tg=tg, tt=tt, n2=n2,
                                      ob=ob: e.tensor_tensor(
                                out=s_acc[:, tg * 4 + tt, n2 * CH:(n2 + 1) * CH],
                                in0=p_o[ob][:],
                                in1=s_acc[:, tg * 4 + tt, n2 * CH:(n2 + 1) * CH],
                                op=ALU.add),
                                wait=[(TE, tf)] + ([acc_ready] if (e_ == 0 and tg == 0 and tt == 0 and n2 == 0) else []),
                                inc=V)
                            o_free[ob] = (V, vv)
                    hec_guard = (TE, P.count["TE"])
            acc_done = (V, P.count["V"])
            for q in range(NHALF):
                dxc[0] += 16
                dpo = P.dma("sync", s_po[:],
                            rearr_tok(po_in, (h2 * NHALF + q) * CH, CH),
                            wait=[(V, state_v)] if False else [], inc=DX)
                lastv = None
                for i in range(4):
                    lastv = P.op("vector", lambda e, q=q, i=i: e.tensor_tensor(
                        out=s_po[:, i, :], in0=s_po[:, i, :],
                        in1=s_acc[:, q * 4 + i, :], op=ALU.add),
                        wait=[(DX, dxc[0]), acc_done] if i == 0 else [],
                        inc=V)
                dd = P.dma("sync", rearr_tok(out_sc, (h2 * NHALF + q) * CH, CH),
                           s_po[:], wait=[(V, lastv)], inc=DO)
                last_out_dma = (DO, dd)
            state_v = P.count["V"]
            state_te = P.count["TE"]
        if standalone:
            P.op("sync", lambda e: e.nop(), wait=[(DO, P.count["DO"])])
            P.build(es)
    return nc


def _assemble_fused(n_pre, n_own, chain, plan=None):
    nc = bass.Bass("TRN2", target_bir_lowering=False, debug=False)
    P = Prog(nc, chain=chain)
    P.plan = plan
    ctx = (nc, P)
    scr = build_phase_a(n_pre=n_pre, n_own=n_own, ctx=ctx)
    P.barrier()
    scr = build_phase_b(n_own=n_own, ctx=ctx, scr=scr)
    P.barrier()
    scr = build_phase_c(n_own=n_own, ctx=ctx, scr=scr)
    P.barrier()
    scr = build_phase_d1(n_own=n_own, ctx=ctx, scr=scr)
    P.barrier()
    build_phase_d2(n_own=n_own, ctx=ctx, scr=scr)
    P.op("sync", lambda e: e.nop(),
         wait=[(P.sem("DO"), P.count["DO"]),
               (P.sem("DO_B"), P.count["DO_B"])])
    es = ExitStack()
    with es:
        P.build(es)
    return nc, P


def build_fused(n_pre=7, n_own=9, chain="sel"):
    if chain == "sel":
        _, P1 = _assemble_fused(n_pre, n_own, "analyze")
        plan = P1.compute_plan()
        nc, P = _assemble_fused(n_pre, n_own, "sel", plan=plan)
        P.plan_stats = P1.plan_stats
        return nc
    nc, _ = _assemble_fused(n_pre, n_own, chain)
    return nc


def _stage_d1_inputs(inputs):
    import ml_dtypes
    bf = ml_dtypes.bfloat16
    return dict(
        swgT_in=np.ascontiguousarray(np.asarray(inputs["L1_ffn_shared_w_gate"], np.float32).T).astype(bf),
        swuT_in=np.ascontiguousarray(np.asarray(inputs["L1_ffn_shared_w_up"], np.float32).T).astype(bf),
        swdT_in=np.ascontiguousarray(np.asarray(inputs["L1_ffn_shared_w_down"], np.float32).T).astype(bf))


def _stage_fused(inputs, core):
    m = {}
    m.update(stage_a_inputs(inputs, core))
    m.update(stage_b_inputs(inputs, core))
    mc = stage_c_inputs(inputs, core)
    for k in ("w1_in", "wo1_in", "gwT_in", "sgwT_in", "ident_f"):
        m[k] = mc[k]
    m.update(_stage_d1_inputs(inputs))
    import ml_dtypes
    m["wgu16_in"] = np.asarray(inputs["L1_ffn_w_gate_up"], np.float32).astype(ml_dtypes.bfloat16)
    m["wd16_in"] = np.asarray(inputs["L1_ffn_w_down"], np.float32).astype(ml_dtypes.bfloat16)
    return m


def kernel(**inputs):
    cores = list(range(NCORES))
    nc = build_fused()
    im = [_stage_fused(inputs, c) for c in cores]
    res = run_bass_kernel_spmd(nc, im, core_ids=cores).results
    out = np.zeros((4, 8192, D), np.float32)
    for c in cores:
        b, half = c // 2, c % 2
        out[b, half * 4096:(half + 1) * 4096] = res[c]["out_sc"]
    return out
